# version 02
# baseline (speedup 1.0000x reference)
import contextlib

import numpy as np

import concourse.bass as bass
import concourse.mybir as mybir
from concourse.bass_utils import run_bass_kernel_spmd

F32 = mybir.dt.float32
BF16 = mybir.dt.bfloat16
F16 = mybir.dt.float16
OP = mybir.AluOpType
AF = mybir.ActivationFunctionType
AX = mybir.AxisListType

B, S, D = 8, 2048, 1024
H, DK = 16, 64
NT = S // 128
NCH = D // 128
DI = 2048
DSTATE = 16
DTR = 64
NEXP, EDIM = 16, 512
NSH = 2
EPS = 1e-5

ENGINES = ("sync", "tensor", "vector", "scalar", "gpsimd")


class Prog:
    def __init__(self, nc, sems):
        self.nc = nc
        self.sems = sems
        self.ops = {e: [] for e in ENGINES}
        self.cnt = {name: 0 for name in sems}

    def op(self, eng, fn, inc=None, n=1):
        if inc is None:
            self.ops[eng].append(fn)
        else:
            sem = self.sems[inc]

            def g(e, fn=fn, sem=sem, n=n):
                fn(e).then_inc(sem, n)

            self.ops[eng].append(g)
            self.cnt[inc] += n
        return self.cnt[inc] if inc else None

    def wait(self, eng, sem, val=None):
        v = self.cnt[sem] if val is None else val
        s = self.sems[sem]
        self.ops[eng].append(lambda e, s=s, v=v: e.wait_ge(s, v))

    def dma(self, q, out, in_, inc=None):
        inc = inc or {"sync": "WD", "gpsimd": "GD", "scalar": "SD"}[q]
        self.op(q, lambda e, out=out, in_=in_: e.dma_start(out=out, in_=in_),
                inc=inc, n=16)
        return self.cnt[inc]

    def mark(self, eng, sem):
        if eng in ("vector", "gpsimd"):
            self.op(eng, lambda e: e.engine_nop(), inc=sem)
        else:
            self.op(eng, lambda e: e.nop(), inc=sem)
        return self.cnt[sem]


class Banks:

    def __init__(self, P):
        self.P = P
        self.state = [None] * 8

    def acquire(self, i):
        st = self.state[i]
        if st is not None:
            self.P.wait("tensor", st[0], st[1])

    def release(self, i, sem):
        self.state[i] = (sem, self.P.cnt[sem])



def build_program(debug_taps=(), stage=None):
    nc = bass.Bass("TRN2", target_bir_lowering=False, detect_race_conditions=False)
    dram = {}

    def din(name, shape, dt=F32):
        dram[name] = nc.dram_tensor(name, list(shape), dt, kind="ExternalInput").ap()

    def dout(name, shape, dt=F32):
        dram[name] = nc.dram_tensor(name, list(shape), dt, kind="ExternalOutput").ap()

    din("x_own", [S, D])
    din("x_all", [B, S, D])
    din("wq", [D, D]); din("wk", [D, D]); din("wv", [D, D]); din("wo", [D, D])
    din("bq_c", [128, NCH]); din("bk_c", [128, NCH])
    din("rows_c", [48, 2048])
    din("w_ffn0_t", [D, D])
    din("w_in_t", [D, 2 * DI])
    din("conv_w_c", [128, (DI // 128) * 4])
    din("conv_b_c", [128, DI // 128])
    din("w_x_t", [DI, 96])
    din("w_dt_t", [DTR, DI])
    din("b_dt_c", [128, DI // 128])
    din("a_log_c", [128, (DI // 128) * DSTATE])
    din("d_par_c", [128, DI // 128])
    din("w_out_t", [DI, D])
    din("ln_wb", [128, 2 * D])
    din("shup_t", [D, NSH * EDIM]); din("shdn_t", [NSH * EDIM, D])
    din("roup_t", [D, NEXP * EDIM]); din("rodn_t", [NEXP * EDIM, D])
    din("wrt_t", [D, NEXP])
    din("consts", [128, 1024])
    dout("y", [S, D])

    dbg_names = []
    es = contextlib.ExitStack()
    with es:
        sb = {}
        sb["x_res"] = es.enter_context(nc.sbuf_tensor("x_res", [128, NT * 1024], F32))
        sb["fmA"] = es.enter_context(nc.sbuf_tensor("fmA", [128, NT * 1024], F32))
        sb["s1"] = es.enter_context(nc.sbuf_tensor("s1", [128, 6912], F32))
        sb["s2"] = es.enter_context(nc.sbuf_tensor("s2", [128, 6144], F32))
        sb["wb"] = es.enter_context(nc.sbuf_tensor("wb", [128, 3584], F32))
        sb["cst"] = es.enter_context(nc.sbuf_tensor("cst", [128, 1024], F32))
        sb["rowa"] = es.enter_context(nc.sbuf_tensor("rowa", [128, 2048], F32))
        sb["stat"] = es.enter_context(nc.sbuf_tensor("stat", [128, 128], F32))
        psum = [es.enter_context(nc.psum_tensor(f"pb{i}", [128, 512], F32)) for i in range(8)]
        sb["psum"] = psum

        def _alias(name, shape, dtype, arena, byte_off):
            return nc.alloc_sbuf_tensor_at(
                name, shape, dtype, offset=nc.lookup_mloc(sb[arena]).addr + byte_off)
        sb["lnj16"] = _alias("lnj16", [128, 16384], F16, "fmA", 0)
        sb["att16"] = _alias("att16", [128, 16384], F16, "fmA", 32768)
        sb["kq16"] = _alias("kq16", [128, 4096], F16, "s1", 0)
        sb["v16"] = _alias("v16", [128, 2080], F16, "s1", 8192)
        sb["er16"] = _alias("er16", [128, 4096], F16, "s1", 12544)
        sb["w16"] = _alias("w16", [128, 4096], F16, "wb", 4096)
        sb["mask16"] = _alias("mask16", [128, 512], F16, "cst", 2560)
        sb["wqB"] = _alias("wqB", [128, 8192], BF16, "s2", 0)
        sb["xqB"] = _alias("xqB", [128, 4096], BF16, "s1", 0)

        sems = {}
        for name in ("WD", "GD", "SD", "T", "V", "Sc", "G", "St"):
            sems[name] = es.enter_context(nc.semaphore(name))
        P = Prog(nc, sems)
        banks = Banks(P)

        def dbg_tap(name, ap):
            if name in debug_taps:
                dout("dbg_" + name, [ap.shape[0], ap.shape[-1]])
                dbg_names.append(name)
                for e_ in ("V", "T", "Sc"):
                    P.wait("gpsimd", e_)
                P.dma("gpsimd", dram["dbg_" + name][:, :], ap, inc="St")

        emit_all(P, banks, dram, sb, dbg_tap, stage)

        block = es.enter_context(nc.Block())
        if True:
            @block.sync
            def _(eng):
                for f in P.ops["sync"]:
                    f(eng)

            @block.tensor
            def _(eng):
                for f in P.ops["tensor"]:
                    f(eng)

            @block.vector
            def _(eng):
                for f in P.ops["vector"]:
                    f(eng)

            @block.scalar
            def _(eng):
                for f in P.ops["scalar"]:
                    f(eng)

            @block.gpsimd
            def _(eng):
                for f in P.ops["gpsimd"]:
                    f(eng)

    return nc, dbg_names



def vfence(P):
    P.mark("vector", "V")
    P.wait("vector", "V")


def ln_tile(P, sb, src_ap, dst_ap, scratch_ap, gamma=None, beta=None):
    stat = sb["stat"]
    mean = stat[:, 0:1]
    vsum = stat[:, 1:2]
    rs = stat[:, 2:3]
    W = src_ap.shape[-1]
    P.op("vector", lambda e: e.tensor_reduce(mean, src_ap, axis=AX.X, op=OP.add), inc="V")
    P.wait("vector", "V")
    P.op("vector", lambda e: e.tensor_scalar(mean, mean, 1.0 / W, None, OP.mult), inc="V")
    P.wait("vector", "V")
    P.op("vector", lambda e: e.tensor_scalar(dst_ap, src_ap, mean, None, OP.subtract))
    P.op("vector", lambda e: e.scalar_tensor_tensor(
        scratch_ap, dst_ap, 1.0, dst_ap, OP.mult, OP.mult, accum_out=vsum), inc="V")
    P.wait("scalar", "V")
    P.op("scalar", lambda e: e.activation(out=rs, in_=vsum, func=AF.Sqrt,
                                          bias=stat[:, 3:4], scale=1.0 / W), inc="Sc")
    P.wait("vector", "Sc")
    P.op("vector", lambda e: e.reciprocal(rs, rs), inc="V")
    P.wait("vector", "V")
    last_ops = [lambda e: e.tensor_scalar(dst_ap, dst_ap, rs, None, OP.mult)]
    if gamma is not None:
        last_ops.append(lambda e: e.tensor_scalar(dst_ap, dst_ap, gamma, None, OP.mult))
    if beta is not None:
        last_ops.append(lambda e: e.tensor_scalar(dst_ap, dst_ap, beta, None, OP.add))
    for i, f in enumerate(last_ops):
        P.op("vector", f, inc="V" if i == len(last_ops) - 1 else None)
    return P.cnt["V"]


def ln_tile_c(P, sb, src_ap, dst_ap, scratch_ap, mean, vsum, rs):
    W = 1024
    P.op("vector", lambda e: e.tensor_reduce(mean, src_ap, axis=AX.X, op=OP.add), inc="V")
    P.wait("vector", "V")
    P.op("vector", lambda e: e.tensor_scalar(mean, mean, 1.0 / W, None, OP.mult), inc="V")
    P.wait("vector", "V")
    P.op("vector", lambda e: e.tensor_scalar(dst_ap, src_ap, mean, None, OP.subtract))
    P.op("vector", lambda e: e.scalar_tensor_tensor(
        scratch_ap, dst_ap, 1.0, dst_ap, OP.mult, OP.mult, accum_out=vsum), inc="V")
    P.wait("scalar", "V")
    P.op("scalar", lambda e: e.activation(out=rs, in_=vsum, func=AF.Sqrt,
                                          bias=sb["stat"][:, 3:4], scale=1.0 / W), inc="Sc")
    P.wait("vector", "Sc")
    P.op("vector", lambda e: e.reciprocal(rs, rs), inc="V")
    P.wait("vector", "V")
    P.op("vector", lambda e: e.tensor_scalar(dst_ap, dst_ap, rs, None, OP.mult), inc="V")
    return P.cnt["V"]


def transpose_tile(P, banks, sb, src_ap_1024, dst_fn, bank_pair=(6, 7),
                   evac_eng="scalar", out_scale=None, ident=None):
    psum = sb["psum"]
    if ident is None:
        ident = sb["cst"][:, 0:128]
    is_16 = (ident.dtype in (BF16, F16))

    def pslot(bank, j):
        if is_16:
            return psum[bank][:, j * 64:(j + 1) * 64].bitcast(ident.dtype)
        return psum[bank][:, j * 128:(j + 1) * 128]

    sem = {"scalar": "Sc", "vector": "V"}[evac_eng]
    for half, bank in ((0, bank_pair[0]), (1, bank_pair[1])):
        banks.acquire(bank)
        for j in range(4):
            kc = half * 4 + j
            P.op("tensor",
                 lambda e, kc=kc, j=j, bank=bank: e.matmul(
                     pslot(bank, j),
                     src_ap_1024[:, kc * 128:(kc + 1) * 128], ident,
                     is_transpose=True, start=True, stop=True),
                 inc="T" if j == 3 else None)
        wr = P.cnt["T"]
        P.wait(evac_eng, "T", wr)
        for j in range(4):
            kc = half * 4 + j
            if evac_eng == "scalar":
                if out_scale is None:
                    P.op("scalar", lambda e, kc=kc, j=j, bank=bank: e.copy(
                        out=dst_fn(kc), in_=pslot(bank, j)),
                        inc=sem if j == 3 else None)
                else:
                    P.op("scalar", lambda e, kc=kc, j=j, bank=bank: e.activation(
                        out=dst_fn(kc), in_=pslot(bank, j),
                        func=AF.Copy, scale=out_scale), inc=sem if j == 3 else None)
            else:
                P.op("vector", lambda e, kc=kc, j=j, bank=bank: e.tensor_copy(
                    dst_fn(kc), pslot(bank, j)),
                    inc=sem if j == 3 else None)
        banks.release(bank, sem)
    return P.cnt[sem]




def scalar_finalize(P, banks, sb, col_ap, out_row_ap, op=OP.max):
    psum = sb["psum"]
    ident = sb["cst"][:, 0:128]
    banks.acquire(6)
    P.wait("tensor", "V")
    P.op("tensor", lambda e: e.matmul(psum[6][0:1, 0:128], col_ap, ident,
                                      is_transpose=True, start=True, stop=True), inc="T")
    wr = P.cnt["T"]
    P.wait("vector", "T", wr)
    row = sb["rowa"][0:1, 1536:1664]
    P.op("vector", lambda e: e.tensor_copy(row, psum[6][0:1, 0:128]), inc="V")
    banks.release(6, "V")
    P.wait("vector", "V")
    P.op("vector", lambda e: e.tensor_reduce(out_row_ap, row, axis=AX.X, op=op), inc="V")
    P.wait("vector", "V")


def bcast_p0(P, banks, sb, src_row_ap, dst_ap, n):
    psum = sb["psum"]
    ones_row = sb["rowa"][0:1, 0:128]
    banks.acquire(6)
    P.wait("tensor", "V")
    P.op("tensor", lambda e: e.matmul(psum[6][:, 0:n], ones_row, src_row_ap,
                                      start=True, stop=True), inc="T")
    wr = P.cnt["T"]
    P.wait("vector", "T", wr)
    P.op("vector", lambda e: e.tensor_copy(dst_ap, psum[6][:, 0:n]), inc="V")
    banks.release(6, "V")
    P.wait("vector", "V")


def quant_grid_tile(P, sb, y_ap, out_ap, scr, c4col, mcol):
    c, d_, f1, msk = scr
    P.op("vector", lambda e: e.tensor_scalar(c, y_ap, c4col, None, OP.mult))
    P.op("vector", lambda e: e.tensor_tensor(d_, c, y_ap, OP.subtract))
    P.op("vector", lambda e: e.tensor_tensor(c, c, d_, OP.subtract))
    P.op("vector", lambda e: e.tensor_scalar(f1, y_ap, mcol, None, OP.add))
    P.op("vector", lambda e: e.tensor_scalar(f1, f1, mcol, None, OP.subtract))
    P.op("vector", lambda e: e.tensor_tensor(msk, y_ap, y_ap, OP.mult))
    P.op("vector", lambda e: e.tensor_scalar(msk, msk, 2.0 ** -12, None, OP.is_ge))
    P.op("vector", lambda e: e.tensor_tensor(d_, c, f1, OP.subtract))
    P.op("vector", lambda e: e.tensor_tensor(d_, d_, msk, OP.mult))
    P.op("vector", lambda e: e.tensor_tensor(out_ap, d_, f1, OP.add), inc="V")



def emit_all(P, banks, dram, sb, dbg_tap, stage=None):
    psum = sb["psum"]
    x_res = sb["x_res"]
    fmA = sb["fmA"]
    s1 = sb["s1"]
    s2 = sb["s2"]
    wb = sb["wb"]
    cst = sb["cst"]
    rowa = sb["rowa"]

    def xres_t(tt):
        return x_res[:, tt * 1024:(tt + 1) * 1024]

    def fm_ap(kc, tt):
        return fmA[:, kc * 2048 + tt * 128: kc * 2048 + (tt + 1) * 128]

    def fm_span(kc, q0, w):
        return fmA[:, kc * 2048 + q0: kc * 2048 + q0 + w]

    ones_row = rowa[0:1, 0:128]
    bias_row = rowa[0:1, 128:1152]
    bv_row = rowa[0:1, 1408:1536]
    recip_row = rowa[0:1, 1536:2048]
    maskrep = cst[:, 128:640]

    xo = dram["x_own"].rearrange("(t p) d -> t p d", p=128)
    for g in range(4):
        P.dma("sync", x_res[:, g * 4096:(g + 1) * 4096].rearrange("p (t d) -> p t d", t=4),
              xo[g * 4:(g + 1) * 4, :, :].rearrange("t p d -> p t d"))
    P.dma("sync", cst[:, :], dram["consts"][:, :])
    P.dma("sync", rowa[0:1, 0:1152], dram["rows_c"][0:1, 0:1152])
    P.dma("sync", rowa[32:48, :], dram["rows_c"][32:48, :])
    P.dma("sync", sb["stat"][:, 8:8 + NCH], dram["bq_c"][:, :])
    P.dma("sync", sb["stat"][:, 16:16 + NCH], dram["bk_c"][:, :])
    ldx = P.cnt["WD"]
    if stage == "x0":
        store_y(P, dram, x_res)
        return
    P.op("vector", lambda e: e.memset(sb["stat"][:, :], 0.0))
    P.op("vector", lambda e: e.memset(sb["stat"][:, 3:4], EPS))
    P.wait("vector", "WD", ldx)

    ln_tmp = s2[:, 0:1024]
    ln_sq = s2[:, 1024:2048]
    P.wait("vector", "WD", ldx)
    P.wait("tensor", "WD", ldx)
    for tt in range(NT):
        if tt > 0:
            P.wait("vector", "Sc")
        ln_tile(P, sb, xres_t(tt), ln_tmp, ln_sq)
        if stage == "ln0_notr":
            P.op("vector", lambda e, tt=tt: e.tensor_copy(
                fmA[:, tt * 1024:(tt + 1) * 1024], ln_tmp), inc="V")
            P.mark("scalar", "Sc")
            continue
        P.wait("tensor", "V")
        transpose_tile(P, banks, sb, ln_tmp, lambda kc, tt=tt: fm_ap(kc, tt))
    fm_done = P.cnt["Sc"]
    dbg_tap("ln0_fm", fmA[:, :])
    if stage in ("ln0", "ln0_notr"):
        store_y(P, dram, x_res)
        return

    k_pair = s1[:, 0:2048]
    q_pair = s1[:, 2048:4096]
    v_pair = s1[:, 4096:4096 + 16 * 130]
    rb = s1[0:64, 6176:6688]
    e_ring = s2[:, 0:4096]
    att_fm = s2[:, 4096:6144]

    wqs = wb[:, 0:1024]
    wks = wb[:, 1024:2048]
    wvs = wb[:, 2048:3072]

    def wslice_dma(dst, wname, hg):
        src = dram[wname][:, hg * 128:(hg + 1) * 128].rearrange("(c p) o -> p c o", p=128)
        return P.dma("sync", dst.rearrange("p (c o) -> p c o", c=NCH), src)

    last_hg_T = None
    for hg in range(8):
        if last_hg_T is not None:
            P.wait("sync", "T", last_hg_T)
        wslice_dma(wqs, "wq", hg)
        wslice_dma(wks, "wk", hg)
        P.dma("sync", bv_row, dram["rows_c"][3:4, hg * 128:(hg + 1) * 128])
        w_ready_kq = P.cnt["WD"]
        wslice_dma(wvs, "wv", hg)
        w_ready_v = P.cnt["WD"]

        P.wait("tensor", "WD", w_ready_kq)
        P.wait("tensor", "Sc", fm_done)

        def proj_fm(wsl, bcol, dst, scale, bank):
            for sp in range(4):
                banks.acquire(bank)
                for kc in range(NCH):
                    P.op("tensor", lambda e, kc=kc, sp=sp, wsl=wsl, bank=bank: e.matmul(
                        psum[bank][:, :], wsl[:, kc * 128:(kc + 1) * 128],
                        fm_span(kc, sp * 512, 512), start=(kc == 0),
                        stop=(kc == NCH - 1)), inc="T" if kc == NCH - 1 else None)
                wr = P.cnt["T"]
                P.wait("vector", "T", wr)
                if scale == 1.0:
                    P.op("vector", lambda e, sp=sp, dst=dst, bank=bank, bcol=bcol: e.tensor_scalar(
                        dst[:, sp * 512:(sp + 1) * 512], psum[bank][:, :],
                        bcol, None, OP.add), inc="V")
                else:
                    P.op("vector", lambda e, sp=sp, dst=dst, scale=scale, bank=bank, bcol=bcol: e.tensor_scalar(
                        dst[:, sp * 512:(sp + 1) * 512], psum[bank][:, :],
                        bcol, scale, OP.add, OP.mult), inc="V")
                banks.release(bank, "V")

        proj_fm(wks, sb["stat"][:, 16 + hg:17 + hg], k_pair, 1.0, 0)
        proj_fm(wqs, sb["stat"][:, 8 + hg:9 + hg], q_pair, 0.125, 1)

        P.wait("tensor", "WD", w_ready_v)
        for tt in range(NT):
            banks.acquire(2)
            for kc in range(NCH):
                P.op("tensor", lambda e, kc=kc, tt=tt: e.matmul(
                    psum[2][:, 0:128], fm_ap(kc, tt), wvs[:, kc * 128:(kc + 1) * 128],
                    start=(kc == 0), stop=False))
            P.op("tensor", lambda e: e.matmul(
                psum[2][:, 0:128], ones_row, bv_row,
                start=False, stop=True), inc="T")
            wr = P.cnt["T"]
            P.wait("vector", "T", wr)
            P.op("vector", lambda e, tt=tt: e.tensor_copy(
                v_pair[:, tt * 130:(tt + 1) * 130].rearrange("p (h c) -> p h c", h=2)[:, :, 0:64],
                psum[2][:, 0:128].rearrange("p (h c) -> p h c", h=2)), inc="V")
            banks.release(2, "V")
        P.op("vector", lambda e: e.memset(
            v_pair.rearrange("p (t h c) -> p t h c", t=16, h=2)[:, :, :, 64:65], 1.0),
            inc="V")
        v_done = P.cnt["V"]

        P.wait("tensor", "V")
        slot_guard = [0] * 8
        for h2 in range(2):
            hr0 = 64 * h2
            pv_bank = 3 + h2
            sc_bank = 0 + h2
            for sp in range(4):
                banks.acquire(pv_bank)
                nkc = 4 * (sp + 1)
                for kc in range(nkc):
                    is_diag = kc >= 4 * sp
                    j = kc - 4 * sp
                    q0 = 128 * j if is_diag else 0
                    qw = 512 - q0
                    slot = (4 + j) if is_diag else (kc % 2)
                    banks.acquire(sc_bank)
                    P.op("tensor", lambda e, kc=kc, sp=sp, hr0=hr0, q0=q0, qw=qw, b=sc_bank: e.matmul(
                        psum[b][:, 0:qw],
                        k_pair[hr0:hr0 + 64, kc * 128:(kc + 1) * 128],
                        q_pair[hr0:hr0 + 64, sp * 512 + q0: sp * 512 + q0 + qw],
                        start=True, stop=True), inc="T")
                    wr = P.cnt["T"]
                    P.wait("scalar", "T", max(wr, slot_guard[slot]))
                    P.op("scalar", lambda e, slot=slot, qw=qw, b=sc_bank: e.activation(
                        out=e_ring[:, slot * 512: slot * 512 + qw],
                        in_=psum[b][:, 0:qw], func=AF.Exp), inc="Sc")
                    banks.release(sc_bank, "Sc")
                    if not is_diag:
                        P.wait("tensor", "Sc")
                        if kc == 0:
                            P.wait("tensor", "V", v_done)
                        P.op("tensor", lambda e, kc=kc, h2=h2, pb=pv_bank, slot=slot: e.matmul(
                            psum[pb][0:65, 0:512],
                            v_pair[:, kc * 130 + 65 * h2: kc * 130 + 65 * h2 + 65],
                            e_ring[:, slot * 512: slot * 512 + 512],
                            start=(kc == 0), stop=False), inc="T")
                        slot_guard[slot] = P.cnt["T"]
                P.wait("vector", "Sc")
                P.op("vector", lambda e: e.tensor_tensor(
                    e_ring[:, 4 * 512:8 * 512].rearrange("p (s c) -> p s c", s=4)[:, :, 0:128],
                    e_ring[:, 4 * 512:8 * 512].rearrange("p (s c) -> p s c", s=4)[:, :, 0:128],
                    maskrep.rearrange("p (s c) -> p s c", s=4), OP.mult), inc="V")
                mk = P.cnt["V"]
                P.wait("tensor", "V", mk)
                if sp == 0:
                    P.wait("tensor", "V", v_done)
                for j in range(4):
                    kc = 4 * sp + j
                    q0 = 128 * j
                    P.op("tensor", lambda e, j=j, kc=kc, q0=q0, h2=h2, pb=pv_bank, sp=sp: e.matmul(
                        psum[pb][0:65, q0:512],
                        v_pair[:, kc * 130 + 65 * h2: kc * 130 + 65 * h2 + 65],
                        e_ring[:, (4 + j) * 512: (4 + j) * 512 + (512 - q0)],
                        start=(sp == 0 and j == 0), stop=(j == 3)), inc="T")
                    slot_guard[4 + j] = P.cnt["T"]
                wr = P.cnt["T"]
                P.wait("vector", "T", wr)
                P.op("vector", lambda e, pb=pv_bank: e.reciprocal(
                    recip_row, psum[pb][64:65, 0:512]), inc="V")
                rp = P.cnt["V"]
                banks.acquire(5)
                P.wait("tensor", "V", rp)
                P.op("tensor", lambda e: e.matmul(
                    psum[5][0:64, 0:512], ones_row[:, 0:64], recip_row,
                    start=True, stop=True), inc="T")
                wr2 = P.cnt["T"]
                P.wait("vector", "T", wr2)
                P.op("vector", lambda e: e.tensor_copy(rb, psum[5][0:64, 0:512]), inc="V")
                banks.release(5, "V")
                P.op("vector", lambda e, hr0=hr0, sp=sp, pb=pv_bank: e.tensor_tensor(
                    att_fm[hr0:hr0 + 64, sp * 512:(sp + 1) * 512],
                    psum[pb][0:64, 0:512], rb, OP.mult), inc="V")
                banks.release(pv_bank, "V")
        att_done = P.cnt["V"]

        P.wait("sync", "V", att_done)
        wos = wb[:, 0:1024]
        wo_rdy = P.dma("sync", wos, dram["wo"][hg * 128:(hg + 1) * 128, :])
        P.wait("tensor", "WD", wo_rdy)
        for tt in range(NT):
            for osp in range(2):
                banks.acquire(5)
                P.op("tensor", lambda e, tt=tt, osp=osp, hg=hg: e.matmul(
                    psum[5][:, :], att_fm[:, tt * 128:(tt + 1) * 128],
                    wos[:, osp * 512:(osp + 1) * 512],
                    start=True, stop=(hg != 0)), inc=None if hg == 0 else "T")
                if hg == 0:
                    P.op("tensor", lambda e, osp=osp: e.matmul(
                        psum[5][:, :], ones_row,
                        bias_row[:, osp * 512:(osp + 1) * 512],
                        start=False, stop=True), inc="T")
                wr = P.cnt["T"]
                P.wait("vector", "T", wr)
                P.op("vector", lambda e, tt=tt, osp=osp: e.tensor_tensor(
                    xres_t(tt)[:, osp * 512:(osp + 1) * 512],
                    xres_t(tt)[:, osp * 512:(osp + 1) * 512],
                    psum[5][:, :], OP.add), inc="V")
                banks.release(5, "V")
        last_hg_T = P.cnt["T"]



    stat = sb["stat"]
    amax_run = stat[:, 32:33]
    tmax = stat[:, 33:34]
    mean_c = stat[:, 35:36]
    vsum_c = stat[:, 36:37]
    rs_c = stat[:, 37:38]
    P.op("vector", lambda e: e.memset(amax_run, 0.0))
    lnj_bf = sb["lnj16"][:, :]
    att_bf = sb["att16"][:, :]
    kq_bf = sb["kq16"][:, :]
    v_bf = sb["v16"][:, :]
    er_bf = sb["er16"][:, :]
    w_bf = sb["w16"][:, :]
    xt32 = s2[:, 0:1024]
    lntm32 = s2[:, 2048:3072]
    x1t = s2[:, 1024:2048]
    mask_bf = sb["mask16"][:, :]
    rb_c = s1[0:64, 6176:6688]
    P.op("vector", lambda e: e.tensor_copy(mask_bf, cst[:, 128:640]), inc="V")

    xall = dram["x_all"].rearrange("b (t p) d -> b t p d", p=128)
    for j in range(B):
        for tt in range(NT):
            P.wait("gpsimd", "V")
            xrd = P.dma("gpsimd", xt32, xall[j, tt, :, :])
            P.wait("vector", "GD", xrd)
            ln_tile_c(P, sb, xt32, lntm32, s2[:, 1024:2048], mean_c, vsum_c, rs_c)
            P.wait("tensor", "V")
            transpose_tile(P, banks, sb, lntm32,
                           lambda kc, tt=tt: lnj_bf[:, kc * 2048 + tt * 128: kc * 2048 + (tt + 1) * 128],
                           bank_pair=(6, 7), evac_eng="vector")
        lnj_done = P.cnt["V"]

        last_T = None
        for hg in range(8):
            if last_T is not None:
                P.wait("gpsimd", "T", last_T)
            for wi, wname in enumerate(("wq", "wk", "wv")):
                P.dma("gpsimd", w_bf[:, wi * 1024:(wi + 1) * 1024].rearrange("p (c o) -> p c o", c=NCH),
                      dram[wname][:, hg * 128:(hg + 1) * 128].rearrange("(c p) o -> p c o", p=128))
            P.dma("gpsimd", bv_row, dram["rows_c"][3:4, hg * 128:(hg + 1) * 128])
            w_rdy = P.cnt["GD"]
            P.wait("tensor", "GD", w_rdy)
            P.wait("tensor", "V", lnj_done)
            for which, wofs, bcol, dst_ofs, sc_ in (
                    ("k", 1024, sb["stat"][:, 16 + hg:17 + hg], 0, 1.0),
                    ("q", 0, sb["stat"][:, 8 + hg:9 + hg], 2048, 0.125)):
                for sp in range(4):
                    banks.acquire(0)
                    for kc in range(NCH):
                        P.op("tensor", lambda e, kc=kc, sp=sp, wofs=wofs: e.matmul(
                            psum[0][:, :], w_bf[:, wofs + kc * 128:(wofs + kc * 128) + 128],
                            lnj_bf[:, kc * 2048 + sp * 512: kc * 2048 + (sp + 1) * 512],
                            start=(kc == 0), stop=(kc == NCH - 1)),
                            inc="T" if kc == NCH - 1 else None)
                    wr = P.cnt["T"]
                    P.wait("vector", "T", wr)
                    P.op("vector", lambda e, sp=sp, dst_ofs=dst_ofs, bcol=bcol, sc_=sc_: e.tensor_scalar(
                        kq_bf[:, dst_ofs + sp * 512: dst_ofs + (sp + 1) * 512],
                        psum[0][:, :], bcol, sc_, OP.add, OP.mult), inc="V")
                    banks.release(0, "V")
            for tt in range(NT):
                banks.acquire(2)
                for kc in range(NCH):
                    P.op("tensor", lambda e, kc=kc, tt=tt: e.matmul(
                        psum[2][:, 0:128],
                        lnj_bf[:, kc * 2048 + tt * 128: kc * 2048 + (tt + 1) * 128],
                        w_bf[:, 2048 + kc * 128: 2048 + (kc + 1) * 128],
                        start=(kc == 0), stop=False))
                P.op("tensor", lambda e: e.matmul(
                    psum[2][:, 0:128], ones_row, bv_row, start=False, stop=True), inc="T")
                wr = P.cnt["T"]
                P.wait("vector", "T", wr)
                P.op("vector", lambda e, tt=tt: e.tensor_copy(
                    v_bf[:, tt * 130:(tt + 1) * 130].rearrange("p (h c) -> p h c", h=2)[:, :, 0:64],
                    psum[2][:, 0:128].rearrange("p (h c) -> p h c", h=2)), inc="V")
                banks.release(2, "V")
            P.op("vector", lambda e: e.memset(
                v_bf.rearrange("p (t h c) -> p t h c", t=16, h=2)[:, :, :, 64:65], 1.0), inc="V")
            v_done = P.cnt["V"]
            P.wait("tensor", "V", v_done)
            slot_guard = [0] * 8
            for h2 in range(2):
                hr0 = 64 * h2
                pv_bank = 3 + h2
                sc_bank = 0 + h2
                for sp in range(4):
                    banks.acquire(pv_bank)
                    nkc = 4 * (sp + 1)
                    for kc in range(nkc):
                        is_diag = kc >= 4 * sp
                        jj = kc - 4 * sp
                        q0 = 128 * jj if is_diag else 0
                        qw = 512 - q0
                        slot = (4 + jj) if is_diag else (kc % 2)
                        banks.acquire(sc_bank)
                        P.op("tensor", lambda e, kc=kc, sp=sp, hr0=hr0, q0=q0, qw=qw, b=sc_bank: e.matmul(
                            psum[b][:, 0:qw],
                            kq_bf[hr0:hr0 + 64, kc * 128:(kc + 1) * 128],
                            kq_bf[hr0:hr0 + 64, 2048 + sp * 512 + q0: 2048 + sp * 512 + q0 + qw],
                            start=True, stop=True), inc="T")
                        wr = P.cnt["T"]
                        P.wait("scalar", "T", max(wr, slot_guard[slot]))
                        P.op("scalar", lambda e, slot=slot, qw=qw, b=sc_bank: e.activation(
                            out=er_bf[:, slot * 512: slot * 512 + qw],
                            in_=psum[b][:, 0:qw], func=AF.Exp), inc="Sc")
                        banks.release(sc_bank, "Sc")
                        if not is_diag:
                            P.wait("tensor", "Sc")
                            P.op("tensor", lambda e, kc=kc, h2=h2, pb=pv_bank, slot=slot: e.matmul(
                                psum[pb][0:65, 0:512],
                                v_bf[:, kc * 130 + 65 * h2: kc * 130 + 65 * h2 + 65],
                                er_bf[:, slot * 512: slot * 512 + 512],
                                start=(kc == 0), stop=False), inc="T")
                            slot_guard[slot] = P.cnt["T"]
                    P.wait("vector", "Sc")
                    P.op("vector", lambda e: e.tensor_tensor(
                        er_bf[:, 4 * 512:8 * 512].rearrange("p (s c) -> p s c", s=4)[:, :, 0:128],
                        er_bf[:, 4 * 512:8 * 512].rearrange("p (s c) -> p s c", s=4)[:, :, 0:128],
                        mask_bf.rearrange("p (s c) -> p s c", s=4), OP.mult), inc="V")
                    mk = P.cnt["V"]
                    P.wait("tensor", "V", mk)
                    for jj in range(4):
                        kc = 4 * sp + jj
                        q0 = 128 * jj
                        P.op("tensor", lambda e, jj=jj, kc=kc, q0=q0, h2=h2, pb=pv_bank, sp=sp: e.matmul(
                            psum[pb][0:65, q0:512],
                            v_bf[:, kc * 130 + 65 * h2: kc * 130 + 65 * h2 + 65],
                            er_bf[:, (4 + jj) * 512: (4 + jj) * 512 + (512 - q0)],
                            start=(sp == 0 and jj == 0), stop=(jj == 3)), inc="T")
                        slot_guard[4 + jj] = P.cnt["T"]
                    wr = P.cnt["T"]
                    P.wait("vector", "T", wr)
                    P.op("vector", lambda e, pb=pv_bank: e.reciprocal(
                        recip_row, psum[pb][64:65, 0:512]), inc="V")
                    rp = P.cnt["V"]
                    banks.acquire(5)
                    P.wait("tensor", "V", rp)
                    P.op("tensor", lambda e: e.matmul(
                        psum[5][0:64, 0:512], ones_row[:, 0:64], recip_row,
                        start=True, stop=True), inc="T")
                    wr2 = P.cnt["T"]
                    P.wait("vector", "T", wr2)
                    P.op("vector", lambda e: e.tensor_copy(rb_c, psum[5][0:64, 0:512]), inc="V")
                    banks.release(5, "V")
                    P.op("vector", lambda e, hg=hg, hr0=hr0, sp=sp, pb=pv_bank: e.tensor_tensor(
                        att_bf[hr0:hr0 + 64, hg * 2048 + sp * 512: hg * 2048 + (sp + 1) * 512],
                        psum[pb][0:64, 0:512], rb_c, OP.mult), inc="V")
                    banks.release(pv_bank, "V")
            last_T = P.cnt["T"]
        att_done_c = P.cnt["V"]

        wo_bf0 = sb["w16"][:, :]
        wo_bf1 = sb["er16"][:, :]
        P.wait("gpsimd", "T", last_T)
        P.wait("gpsimd", "Sc")
        P.dma("gpsimd", wo_bf0.rearrange("p (c o) -> p c o", c=NCH),
              dram["wo"][:, 0:512].rearrange("(c p) o -> p c o", p=128))
        P.dma("gpsimd", wo_bf1.rearrange("p (c o) -> p c o", c=NCH),
              dram["wo"][:, 512:1024].rearrange("(c p) o -> p c o", p=128))
        wo_rdy = P.cnt["GD"]
        P.wait("tensor", "GD", wo_rdy)
        P.wait("tensor", "V", att_done_c)
        for tt in range(NT):
            P.wait("gpsimd", "V")
            xrd = P.dma("gpsimd", xt32, xall[j, tt, :, :])
            for osp in range(2):
                banks.acquire(5)
                wo_h = wo_bf0 if osp == 0 else wo_bf1
                for kc in range(NCH):
                    P.op("tensor", lambda e, kc=kc, tt=tt, wo_h=wo_h: e.matmul(
                        psum[5][:, :], att_bf[:, kc * 2048 + tt * 128:(kc * 2048 + tt * 128) + 128],
                        wo_h[:, kc * 512:(kc + 1) * 512],
                        start=(kc == 0), stop=False))
                P.op("tensor", lambda e, osp=osp: e.matmul(
                    psum[5][:, :], ones_row, bias_row[:, osp * 512:(osp + 1) * 512],
                    start=False, stop=True), inc="T")
                wr = P.cnt["T"]
                if osp == 0:
                    P.wait("vector", "GD", xrd)
                P.wait("vector", "T", wr)
                P.op("vector", lambda e, osp=osp: e.tensor_tensor(
                    x1t[:, osp * 512:(osp + 1) * 512], psum[5][:, :],
                    xt32[:, osp * 512:(osp + 1) * 512], OP.add), inc="V")
                banks.release(5, "V")
            P.op("vector", lambda e: e.tensor_reduce(mean_c, x1t, axis=AX.X, op=OP.add), inc="V")
            P.wait("vector", "V")
            P.op("vector", lambda e: e.tensor_scalar(mean_c, mean_c, 1.0 / D, None, OP.mult), inc="V")
            P.wait("vector", "V")
            P.op("vector", lambda e: e.tensor_scalar(
                s2[:, 0:1024], x1t, mean_c, None, OP.subtract))
            P.op("vector", lambda e: e.scalar_tensor_tensor(
                s2[:, 4096:5120], s2[:, 0:1024], 1.0, s2[:, 0:1024], OP.mult, OP.mult,
                accum_out=vsum_c), inc="V")
            P.wait("scalar", "V")
            P.op("scalar", lambda e: e.activation(out=rs_c, in_=vsum_c, func=AF.Sqrt,
                                                  bias=stat[:, 3:4], scale=1.0 / D), inc="Sc")
            P.wait("vector", "Sc")
            P.op("vector", lambda e: e.reciprocal(rs_c, rs_c), inc="V")
            P.wait("vector", "V")
            P.op("vector", lambda e: e.tensor_reduce(
                tmax, s2[:, 0:1024], axis=AX.X, op=OP.max, apply_absolute_value=True), inc="V")
            P.wait("vector", "V")
            P.op("vector", lambda e: e.tensor_tensor(tmax, tmax, rs_c, OP.mult), inc="V")
            P.wait("vector", "V")
            P.op("vector", lambda e: e.tensor_tensor(amax_run, amax_run, tmax, OP.max), inc="V")

    if stage == "amax":
        store_y(P, dram, x_res)
        return

    stat = sb["stat"]
    ind = stat[:, 24:25]
    c4col = stat[:, 25:26]
    mcol = stat[:, 26:27]
    sc_x = stat[:, 27:28]
    inv_x = stat[:, 28:29]
    sc_w = stat[:, 29:30]
    inv_w = stat[:, 30:31]
    inv2 = stat[:, 31:32]
    amax_run = stat[:, 32:33]
    tmax = stat[:, 33:34]

    def vf():
        P.mark("vector", "V")
        P.wait("vector", "V")

    P.op("vector", lambda e: e.memset(ind, 2.0 ** -30))
    vf()
    P.op("vector", lambda e: e.tensor_scalar(ind, ind, 1.0, None, OP.add))
    vf()
    P.op("vector", lambda e: e.tensor_scalar(ind, ind, 1.0, None, OP.subtract))
    vf()
    P.op("vector", lambda e: e.tensor_scalar(ind, ind, 2.0 ** 30, None, OP.mult))
    vf()
    P.op("vector", lambda e: e.tensor_scalar(c4col, ind, 2.0 ** 49, None, OP.mult))
    P.op("vector", lambda e: e.tensor_scalar(c4col, c4col, float(2 ** 20 + 1), None, OP.add))
    vf()
    P.op("vector", lambda e: e.tensor_scalar(tmax, ind, 2.0 ** 20, None, OP.mult))
    vf()
    P.op("vector", lambda e: e.tensor_tensor(c4col, c4col, tmax, OP.subtract))
    vf()
    P.op("vector", lambda e: e.tensor_scalar(mcol, ind, 1.5 * 2.0 ** 43, None, OP.mult))
    P.op("vector", lambda e: e.tensor_scalar(mcol, mcol, 1.5 * 2.0 ** 14, None, OP.add))
    vf()
    P.op("vector", lambda e: e.tensor_scalar(tmax, ind, 1.5 * 2.0 ** 14, None, OP.mult))
    vf()
    P.op("vector", lambda e: e.tensor_tensor(mcol, mcol, tmax, OP.subtract))
    vf()

    for tt in range(NT):
        if tt > 0:
            P.wait("vector", "Sc")
        ln_tile(P, sb, xres_t(tt), ln_tmp, ln_sq)
        P.op("vector", lambda e: e.tensor_reduce(
            tmax, ln_tmp, axis=AX.X, op=OP.max, apply_absolute_value=True), inc="V")
        P.wait("vector", "V")
        P.op("vector", lambda e: e.tensor_tensor(amax_run, amax_run, tmax, OP.max))
        P.wait("tensor", "V")
        transpose_tile(P, banks, sb, ln_tmp, lambda kc, tt=tt: fm_ap(kc, tt))
    ln1_done = P.cnt["Sc"]
    dbg_tap("ln1_fm", fmA[:, :])

    vf()
    arow = rowa[0:1, 1664:1665]
    scrow = rowa[0:1, 1665:1667]
    scalar_finalize(P, banks, sb, amax_run, arow, op=OP.max)
    P.op("vector", lambda e: e.tensor_scalar(arow, arow, 1e-12, None, OP.max))
    vf()
    P.op("vector", lambda e: e.reciprocal(scrow[:, 0:1], arow))
    vf()
    P.op("vector", lambda e: e.tensor_scalar(scrow[:, 0:1], scrow[:, 0:1], 448.0, None, OP.mult))
    P.op("vector", lambda e: e.tensor_scalar(scrow[:, 1:2], arow, 1.0 / 448.0, None, OP.mult))
    vf()
    bcast_p0(P, banks, sb, scrow, stat[:, 27:29], 2)

    wstage = s1[:, 2048:6144]
    P.op("vector", lambda e: e.memset(amax_run, 0.0))
    wf_t = dram["w_ffn0_t"].rearrange("(c p) o -> p c o", p=128)
    P.wait("sync", "GD")
    P.wait("sync", "V")
    P.wait("sync", "T")
    for half in range(2):
        if half > 0:
            P.wait("sync", "V")
        wrd = P.dma("sync", wstage.rearrange("p (c o) -> p c o", c=4),
                    wf_t[:, half * 4:(half + 1) * 4, :])
        P.wait("vector", "WD", wrd)
        P.op("vector", lambda e: e.tensor_reduce(
            tmax, wstage, axis=AX.X, op=OP.max, apply_absolute_value=True), inc="V")
        P.wait("vector", "V")
        P.op("vector", lambda e: e.tensor_tensor(amax_run, amax_run, tmax, OP.max), inc="V")
    vf()
    scalar_finalize(P, banks, sb, amax_run, arow, op=OP.max)
    P.op("vector", lambda e: e.tensor_scalar(arow, arow, 1e-12, None, OP.max))
    vf()
    P.op("vector", lambda e: e.reciprocal(scrow[:, 0:1], arow))
    vf()
    P.op("vector", lambda e: e.tensor_scalar(scrow[:, 0:1], scrow[:, 0:1], 448.0, None, OP.mult))
    P.op("vector", lambda e: e.tensor_scalar(scrow[:, 1:2], arow, 1.0 / 448.0, None, OP.mult))
    vf()
    bcast_p0(P, banks, sb, scrow, stat[:, 29:31], 2)
    P.op("vector", lambda e: e.tensor_tensor(inv2, inv_x, inv_w, OP.mult), inc="V")
    P.wait("vector", "V")

    wq_bf = sb["wqB"][:, :]
    qscr = [s2[:, 4096 + i * 512:4096 + (i + 1) * 512] for i in range(4)]
    yq = s1[:, 0:2048]
    for half in range(2):
        P.wait("sync", "V")
        wrd = P.dma("sync", wstage.rearrange("p (c o) -> p c o", c=4),
                    wf_t[:, half * 4:(half + 1) * 4, :])
        P.wait("vector", "WD", wrd)
        for sub in range(8):
            kc = half * 4 + sub // 2
            off = (sub % 2) * 512
            ysl_w = yq[:, 0:512]
            P.op("vector", lambda e, sub=sub, ysl_w=ysl_w: e.tensor_scalar(
                ysl_w, wstage[:, sub * 512:(sub + 1) * 512], sc_w, None, OP.mult))
            quant_grid_tile(P, sb, ysl_w, wq_bf[:, kc * 1024 + off: kc * 1024 + off + 512],
                            qscr, c4col, mcol)
    wq_done = P.cnt["V"]
    dbg_tap("wq_s2", s2[:, 0:4096])

    brd = P.dma("sync", bias_row, dram["rows_c"][4:5, 0:1024])
    P.wait("vector", "WD", brd)
    P.op("vector", lambda e: e.tensor_tensor(tmax, sc_x, sc_w, OP.mult), inc="V")
    P.wait("vector", "V")
    P.op("vector", lambda e: e.tensor_scalar(bias_row, bias_row, tmax[0:1, 0:1], None, OP.mult), inc="V")
    bias_done = P.cnt["V"]

    xq_bf = sb["xqB"][:, :]
    for sp in range(4):
        for kc in range(NCH):
            ysl_x = s1[:, 6400:6912]
            P.op("vector", lambda e, kc=kc, sp=sp, ysl_x=ysl_x: e.tensor_scalar(
                ysl_x, fm_span(kc, sp * 512, 512), sc_x, None, OP.mult))
            quant_grid_tile(P, sb, ysl_x, xq_bf[:, kc * 512:(kc + 1) * 512],
                            qscr, c4col, mcol)
        xq_done = P.cnt["V"]
        P.wait("tensor", "V", xq_done)
        P.wait("tensor", "Sc", ln1_done)
        for t4 in range(4):
            tt = sp * 4 + t4
            for osp in range(2):
                banks.acquire(5)
                for kc in range(NCH):
                    P.op("tensor", lambda e, kc=kc, t4=t4, osp=osp: e.matmul(
                        psum[5][:, :],
                        xq_bf[:, kc * 512 + t4 * 128: kc * 512 + (t4 + 1) * 128],
                        wq_bf[:, kc * 1024 + osp * 512: kc * 1024 + (osp + 1) * 512],
                        start=(kc == 0), stop=False))
                P.op("tensor", lambda e, osp=osp: e.matmul(
                    psum[5][:, :], ones_row, bias_row[:, osp * 512:(osp + 1) * 512],
                    start=False, stop=True), inc="T")
                wr = P.cnt["T"]
                P.wait("vector", "T", wr)
                P.op("vector", lambda e, tt=tt, osp=osp: e.scalar_tensor_tensor(
                    xres_t(tt)[:, osp * 512:(osp + 1) * 512],
                    psum[5][:, :], inv2,
                    xres_t(tt)[:, osp * 512:(osp + 1) * 512],
                    OP.mult, OP.add), inc="V")
                banks.release(5, "V")

    dbg_tap("statd", sb["stat"][:, :])
    if stage == "x2":
        store_y(P, dram, x_res)
        return

    TC = 256
    NCHK = S // TC
    NPT = DI // 128
    xs_a = lambda pt: fmA[:, pt * TC:(pt + 1) * TC]
    u_a = lambda pt: fmA[:, 4096 + pt * TC: 4096 + (pt + 1) * TC]
    dl_a = lambda pt: fmA[:, 8192 + pt * TC: 8192 + (pt + 1) * TC]
    y_a = lambda pt: fmA[:, 12288 + pt * TC: 12288 + (pt + 1) * TC]
    wstg1 = fmA[:, 4096:12288]
    ln2fm = lambda kc: s1[:, kc * TC:(kc + 1) * TC]
    halo = s1[:, 4096:4144]
    negA = s1[:, 5424:5680]
    convw = s1[:, 5680:5744]
    convb = s1[:, 5744:5760]
    bdt_c = s1[:, 5760:5776]
    dpar_c = s1[:, 5776:5792]
    carry = s1[:, 6144:6400]
    a_t = s2[:, 0:256]
    b_t = s2[:, 256:512]
    h_t = s2[:, 512:768]
    xdbl = s2[:, 768:1024]
    e_scr = s2[:, 1024:2048]
    ln2tm = s2[:, 2048:4096]
    t_tm = s2[:, 4096:5120]
    ln_scr = s2[:, 5120:6144]
    bc_b = s1[32:48, 6400:6656]
    bc_c = s1[32:48, 6656:6912]
    onehot16 = rowa[32:48, :]

    P.wait("sync", "V")
    P.wait("sync", "T")
    P.dma("sync", s1[:, 5424:5680], dram["a_log_c"][:, :])
    P.dma("sync", convw, dram["conv_w_c"][:, :])
    P.dma("sync", convb, dram["conv_b_c"][:, :])
    P.dma("sync", bdt_c, dram["b_dt_c"][:, :])
    P.dma("sync", dpar_c, dram["d_par_c"][:, :])
    P.dma("sync", wb[:, 0:1536].rearrange("p (c r) -> p c r", c=16),
          dram["w_x_t"].rearrange("(c p) r -> p c r", p=128))
    P.dma("sync", wb[0:DTR, 1536:3584], dram["w_dt_t"][:, :])
    ld_e = P.cnt["WD"]
    P.wait("scalar", "WD", ld_e)
    P.op("scalar", lambda e: e.activation(out=negA, in_=negA, func=AF.Exp, scale=1.0), inc="Sc")
    P.wait("vector", "Sc")
    P.op("vector", lambda e: e.tensor_scalar(negA, negA, -1.0, None, OP.mult), inc="V")
    P.op("vector", lambda e: e.memset(halo, 0.0))
    P.op("vector", lambda e: e.memset(carry, 0.0), inc="V")

    win_t = dram["w_in_t"].rearrange("(c p) e -> p c e", p=128)
    wout_t = dram["w_out_t"].rearrange("(c p) o -> p c o", p=128)

    for ch in range(NCHK):
        P.wait("vector", "T")
        for t2 in range(2):
            tile = 2 * ch + t2
            ln_tile(P, sb, xres_t(tile), ln2tm[:, t2 * 1024:(t2 + 1) * 1024], ln_scr)
            P.wait("tensor", "V")
            transpose_tile(P, banks, sb, ln2tm[:, t2 * 1024:(t2 + 1) * 1024],
                           lambda kc, t2=t2: ln2fm(kc)[:, t2 * 128:(t2 + 1) * 128],
                           bank_pair=(6, 7), evac_eng="vector")
        ln2_done = P.cnt["V"]

        for half in range(2):
            P.wait("sync", "V")
            wrd = P.dma("sync", wstg1, win_t[:, :, half * 1024:(half + 1) * 1024])
            P.wait("tensor", "WD", wrd)
            P.wait("tensor", "V", ln2_done)
            for es8 in range(8):
                es = half * 8 + es8
                banks.acquire(4)
                for kc in range(NCH):
                    P.op("tensor", lambda e, kc=kc, es8=es8: e.matmul(
                        psum[4][:, 0:TC], wstg1[:, kc * 1024 + es8 * 128: kc * 1024 + (es8 + 1) * 128],
                        ln2fm(kc), start=(kc == 0), stop=(kc == NCH - 1)),
                        inc="T" if kc == NCH - 1 else None)
                wr = P.cnt["T"]
                P.wait("vector", "T", wr)
                P.op("vector", lambda e, es=es: e.tensor_copy(xs_a(es), psum[4][:, 0:TC]), inc="V")
                banks.release(4, "V")

        for pt in range(NPT):
            if pt > 0:
                P.wait("vector", "Sc")
            acc = s2[:, 1024:1280]
            acc2 = s2[:, 1280:1536]
            for k in range(4):
                def shifted(e_unused, pt=pt, k=k):
                    pass
                if k < 3:
                    nh = 3 - k
                    srcs = [(halo[:, pt * 3 + (3 - nh): pt * 3 + 3], 0, nh),
                            (xs_a(pt)[:, 0:TC - nh], nh, TC - nh)]
                else:
                    srcs = [(xs_a(pt), 0, TC)]
                wcol = convw[:, pt * 4 + k: pt * 4 + k + 1]
                for (src_ap, off, w_) in srcs:
                    if k == 0 and off == 0:
                        P.op("vector", lambda e, src_ap=src_ap, off=off, w_=w_, wcol=wcol, acc=acc: e.tensor_scalar(
                            acc[:, off:off + w_], src_ap, wcol, None, OP.mult))
                    elif k == 0:
                        P.op("vector", lambda e, src_ap=src_ap, off=off, w_=w_, wcol=wcol, acc=acc: e.tensor_scalar(
                            acc[:, off:off + w_], src_ap, wcol, None, OP.mult))
                    else:
                        P.op("vector", lambda e, src_ap=src_ap, off=off, w_=w_, wcol=wcol, acc=acc, acc2=acc2: e.scalar_tensor_tensor(
                            acc2[:, off:off + w_], src_ap, wcol, acc[:, off:off + w_],
                            OP.mult, OP.add))
                if k > 0:
                    acc, acc2 = acc2, acc
            P.op("vector", lambda e, acc=acc: e.engine_nop(), inc="V")
            P.wait("scalar", "V")
            P.op("scalar", lambda e, pt=pt, acc=acc: e.activation(
                out=u_a(pt), in_=acc, func=AF.Silu,
                bias=convb[:, pt:pt + 1], scale=1.0), inc="Sc")
        P.op("vector", lambda e: e.tensor_copy(
            halo.rearrange("p (t c) -> p t c", t=16),
            fmA[:, 0:4096].rearrange("p (t c) -> p t c", t=16)[:, :, TC - 3:TC]), inc="V")
        u_done = P.cnt["Sc"]

        P.wait("tensor", "Sc", u_done)
        P.wait("tensor", "WD", ld_e)
        banks.acquire(5)
        for ec in range(NPT):
            P.op("tensor", lambda e, ec=ec: e.matmul(
                psum[5][0:96, 0:TC], wb[:, ec * 96:(ec + 1) * 96], u_a(ec),
                start=(ec == 0), stop=(ec == NPT - 1)), inc="T" if ec == NPT - 1 else None)
        wr = P.cnt["T"]
        P.wait("vector", "T", wr)
        P.op("vector", lambda e: e.tensor_copy(xdbl[0:96, :], psum[5][0:96, 0:TC]), inc="V")
        banks.release(5, "V")
        P.wait("gpsimd", "V")
        P.dma("gpsimd", bc_b, xdbl[64:80, :], inc="GD")
        P.dma("gpsimd", bc_c, xdbl[80:96, :], inc="GD")
        bc_done = P.cnt["GD"]

        for pt in range(NPT):
            banks.acquire(5)
            wsl = wb[0:DTR, 1536 + pt * 128: 1536 + (pt + 1) * 128]
            P.op("tensor", lambda e, wsl=wsl: e.matmul(
                psum[5][:, 0:TC], wsl, xdbl[0:64, :], start=True, stop=True), inc="T")
            wr = P.cnt["T"]
            P.wait("scalar", "T", wr)
            P.op("scalar", lambda e, pt=pt: e.activation(
                out=e_scr[:, 0:TC], in_=psum[5][:, 0:TC], func=AF.Exp,
                bias=bdt_c[:, pt:pt + 1], scale=1.0), inc="Sc")
            banks.release(5, "Sc")
            P.op("scalar", lambda e, pt=pt: e.activation(
                out=dl_a(pt), in_=e_scr[:, 0:TC], func=AF.Ln, bias=1.0, scale=1.0), inc="Sc")
        dl_done = P.cnt["Sc"]

        P.wait("vector", "Sc", dl_done)
        for pt in range(NPT):
            P.op("vector", lambda e, pt=pt: e.tensor_tensor(xs_a(pt), dl_a(pt), u_a(pt), OP.mult))
        P.op("vector", lambda e: e.engine_nop(), inc="V")
        du_done = P.cnt["V"]

        prev_scan = [P.cnt["V"]]
        P.wait("tensor", "GD", bc_done)
        for n in range(DSTATE):
            banks.acquire(2)
            P.op("tensor", lambda e, n=n: e.matmul(
                psum[2][:, 0:TC], onehot16[:, n * 128:(n + 1) * 128], bc_b,
                start=True, stop=True), inc="T")
            banks.acquire(3)
            P.op("tensor", lambda e, n=n: e.matmul(
                psum[3][:, 0:TC], onehot16[:, n * 128:(n + 1) * 128], bc_c,
                start=True, stop=True), inc="T")
            wr = P.cnt["T"]
            P.wait("vector", "T", wr)
            P.wait("scalar", "T", wr)
            if n == 0:
                P.wait("vector", "V", du_done)
                P.wait("scalar", "V", du_done)
            for pt in range(NPT):
                nacol = negA[:, pt * 16 + n: pt * 16 + n + 1]
                ccol = carry[:, pt * 16 + n: pt * 16 + n + 1]
                P.wait("scalar", "V", prev_scan[0])
                P.op("scalar", lambda e, pt=pt, nacol=nacol: e.activation(
                    out=a_t, in_=dl_a(pt), func=AF.Exp, scale=nacol), inc="Sc")
                sc_a = P.cnt["Sc"]
                P.op("vector", lambda e, pt=pt: e.tensor_tensor(
                    b_t, xs_a(pt), psum[2][:, 0:TC], OP.mult), inc="V")
                P.wait("vector", "Sc", sc_a)
                P.op("vector", lambda e, ccol=ccol: e.tensor_tensor_scan(
                    h_t, a_t, b_t, ccol, OP.mult, OP.add), inc="V")
                prev_scan[0] = P.cnt["V"]
                P.wait("vector", "V")
                P.op("vector", lambda e, ccol=ccol: e.tensor_copy(ccol, h_t[:, TC - 1:TC]))
                P.op("vector", lambda e: e.tensor_tensor(b_t, h_t, psum[3][:, 0:TC], OP.mult))
                if n == 0:
                    P.op("vector", lambda e, pt=pt: e.tensor_copy(y_a(pt), b_t), inc="V")
                else:
                    P.op("vector", lambda e, pt=pt: e.tensor_tensor(
                        y_a(pt), y_a(pt), b_t, OP.add), inc="V")
            banks.release(2, "V")
            banks.release(3, "V")
        scan_done = P.cnt["V"]

        for pt in range(NPT):
            P.op("vector", lambda e, pt=pt: e.scalar_tensor_tensor(
                y_a(pt), u_a(pt), dpar_c[:, pt:pt + 1], y_a(pt), OP.mult, OP.add), inc="V")
        for q in range(4):
            stg = fmA[:, 0:4096] if q % 2 == 0 else fmA[:, 8192:12288]
            P.wait("sync", "V")
            P.wait("sync", "T")
            wrd = P.dma("sync", stg.rearrange("p (c e) -> p c e", c=8),
                        win_t[:, :, 2048 + q * 512: 2048 + (q + 1) * 512])
            P.wait("tensor", "WD", wrd)
            for es4 in range(4):
                pt = q * 4 + es4
                banks.acquire(4)
                for kc in range(NCH):
                    P.op("tensor", lambda e, kc=kc, es4=es4, stg=stg: e.matmul(
                        psum[4][:, 0:TC], stg[:, kc * 512 + es4 * 128: kc * 512 + (es4 + 1) * 128],
                        ln2fm(kc), start=(kc == 0), stop=(kc == NCH - 1)),
                        inc="T" if kc == NCH - 1 else None)
                wr = P.cnt["T"]
                P.wait("scalar", "T", wr)
                P.op("scalar", lambda e: e.activation(
                    out=e_scr[:, 0:TC], in_=psum[4][:, 0:TC], func=AF.Silu), inc="Sc")
                banks.release(4, "Sc")
                P.wait("vector", "Sc")
                P.op("vector", lambda e, pt=pt: e.tensor_tensor(
                    y_a(pt), y_a(pt), e_scr[:, 0:TC], OP.mult), inc="V")
        y_done = P.cnt["V"]

        lnwb = s1[:, 0:2048]
        P.wait("sync", "V", y_done)
        P.wait("sync", "T")
        for wh in range(2):
            if wh == 1:
                P.wait("sync", "T")
            wrd = P.dma("sync", fmA[:, 0:8192].rearrange("p (c o) -> p c o", c=8),
                        wout_t[:, wh * 8:(wh + 1) * 8, :])
            P.wait("tensor", "WD", wrd)
            for t2 in range(2):
                for osp in range(2):
                    bank = 2 + t2 * 2 + osp
                    if wh == 0:
                        banks.acquire(bank)
                    for e8 in range(8):
                        ec = wh * 8 + e8
                        P.op("tensor", lambda e, ec=ec, e8=e8, t2=t2, osp=osp, bank=bank: e.matmul(
                            psum[bank][:, :], y_a(ec)[:, t2 * 128:(t2 + 1) * 128],
                            fmA[:, e8 * 1024 + osp * 512: e8 * 1024 + osp * 512 + 512],
                            start=(ec == 0), stop=(ec == NPT - 1)),
                            inc="T" if (e8 == 7) else None)
        wo_mm = P.cnt["T"]
        brd = P.dma("sync", lnwb, dram["ln_wb"][:, :])
        for t2 in range(2):
            P.wait("vector", "T", wo_mm)
            for osp in range(2):
                bank = 2 + t2 * 2 + osp
                P.op("vector", lambda e, t2=t2, osp=osp, bank=bank: e.tensor_tensor(
                    t_tm[:, osp * 512:(osp + 1) * 512], psum[bank][:, :],
                    ln2tm[:, t2 * 1024 + osp * 512: t2 * 1024 + (osp + 1) * 512],
                    OP.add), inc="V")
                banks.release(bank, "V")
            tile = 2 * ch + t2
            ln_tile(P, sb, t_tm, ln_scr[:, 0:1024], s2[:, 1024:2048])
            if t2 == 0:
                P.wait("vector", "WD", brd)
            P.op("vector", lambda e: e.tensor_tensor(
                ln_scr[:, 0:1024], ln_scr[:, 0:1024], lnwb[:, 0:1024], OP.mult))
            P.op("vector", lambda e: e.tensor_tensor(
                ln_scr[:, 0:1024], ln_scr[:, 0:1024], lnwb[:, 1024:2048], OP.add))
            P.op("vector", lambda e, tile=tile: e.tensor_tensor(
                xres_t(tile), xres_t(tile), ln_scr[:, 0:1024], OP.add), inc="V")

    if stage == "x3":
        store_y(P, dram, x_res)
        return

    ln3fm = lambda kc: s1[:, kc * TC:(kc + 1) * TC]
    act_sb = s1[:, 2048:2304]
    wrt_sb = s1[:, 6400:6528]
    probs = s1[:, 6528:6544]
    rank_a = s1[:, 6544:6560]
    rank_b = s1[:, 6560:6576]
    gate_tm = s1[:, 6576:6592]
    gfm = s1[32:48, 6592:6848]
    ln3tm = s2[:, 2048:4096]
    den = sb["stat"][:, 34:35]

    P.wait("sync", "V")
    P.wait("sync", "T")
    wrd = P.dma("sync", wrt_sb.rearrange("p (c r) -> p c r", c=NCH),
                dram["wrt_t"].rearrange("(c p) r -> p c r", p=128))
    ld_f = P.cnt["WD"]

    roup = dram["roup_t"].rearrange("(c p) e -> p c e", p=128)
    shup = dram["shup_t"].rearrange("(c p) e -> p c e", p=128)
    rodn = dram["rodn_t"].rearrange("(r p) o -> r p o", p=128)
    shdn = dram["shdn_t"].rearrange("(r p) o -> r p o", p=128)

    NEX = NEXP + NSH
    for ch in range(NCHK):
        P.wait("vector", "T")
        for t2 in range(2):
            tile = 2 * ch + t2
            ln_tile(P, sb, xres_t(tile), ln3tm[:, t2 * 1024:(t2 + 1) * 1024], s2[:, 1024:2048])
            P.wait("tensor", "V")
            transpose_tile(P, banks, sb, ln3tm[:, t2 * 1024:(t2 + 1) * 1024],
                           lambda kc, t2=t2: ln3fm(kc)[:, t2 * 128:(t2 + 1) * 128],
                           bank_pair=(6, 7), evac_eng="vector")
        ln3_done = P.cnt["V"]

        P.wait("tensor", "WD", ld_f)
        for t2 in range(2):
            banks.acquire(7)
            for kc in range(NCH):
                P.op("tensor", lambda e, kc=kc, t2=t2: e.matmul(
                    psum[7][:, 0:NEXP], ln3fm(kc)[:, t2 * 128:(t2 + 1) * 128],
                    wrt_sb[:, kc * 16:(kc + 1) * 16],
                    start=(kc == 0), stop=(kc == NCH - 1)), inc="T" if kc == NCH - 1 else None)
            wr = P.cnt["T"]
            P.wait("scalar", "T", wr)
            P.op("scalar", lambda e: e.activation(
                out=probs, in_=psum[7][:, 0:NEXP], func=AF.Exp, accum_out=den), inc="Sc")
            banks.release(7, "Sc")
            P.wait("vector", "Sc")
            P.op("vector", lambda e: e.reciprocal(den, den), inc="V")
            P.wait("vector", "V")
            P.op("vector", lambda e: e.tensor_scalar(probs, probs, den, None, OP.mult), inc="V")
            P.wait("vector", "V")
            P.op("vector", lambda e: e.tensor_scalar(
                rank_a, probs, probs[:, 0:1], None, OP.is_lt))
            cur, nxt = rank_a, rank_b
            for j in range(1, NEXP):
                P.op("vector", lambda e, j=j, cur=cur, nxt=nxt: e.scalar_tensor_tensor(
                    nxt, probs, probs[:, j:j + 1], cur, OP.is_lt, OP.add))
                cur, nxt = nxt, cur
            P.op("vector", lambda e: e.engine_nop(), inc="V")
            P.wait("vector", "V")
            mask_buf = s2[:, 1024:1040]
            P.op("vector", lambda e, cur=cur: e.tensor_scalar(
                mask_buf, cur, 3.5, None, OP.is_lt), inc="V")
            P.wait("vector", "V")
            P.op("vector", lambda e: e.tensor_tensor(
                gate_tm, probs, mask_buf, OP.mult), inc="V")
            gt = P.cnt["V"]
            banks.acquire(7)
            P.wait("tensor", "V", gt)
            P.op("tensor", lambda e: e.matmul(
                psum[7][0:16, 0:128], gate_tm, cst[:, 0:128],
                is_transpose=True, start=True, stop=True), inc="T")
            wr = P.cnt["T"]
            P.wait("vector", "T", wr)
            P.op("vector", lambda e: e.tensor_copy(s1[0:16, 6144:6272], psum[7][0:16, 0:128]), inc="V")
            banks.release(7, "V")
            gv = P.cnt["V"]
            P.wait("gpsimd", "V", gv)
            P.dma("gpsimd", gfm[:, t2 * 128:(t2 + 1) * 128], s1[0:16, 6144:6272], inc="GD")
        gate_done = P.cnt["GD"]

        for b_ in range(4):
            banks.acquire(b_)
        dn_start = True
        act_v = [0]
        act_guard = [0]
        for ex in range(NEX):
            par = ex % 2
            upst = fmA[:, par * 8192: par * 8192 + 4096]
            dnst = fmA[:, par * 8192 + 4096: par * 8192 + 8192]
            P.wait("sync", "T")
            if ex < NEXP:
                P.dma("sync", upst.rearrange("p (c e) -> p c e", c=NCH),
                      roup[:, :, ex * EDIM:(ex + 1) * EDIM])
                wrd = P.dma("sync", dnst.rearrange("p (c o) -> p c o", c=4),
                            rodn[ex * 4:(ex + 1) * 4, :, :].rearrange("r p o -> p r o"))
            else:
                sx = ex - NEXP
                P.dma("sync", upst.rearrange("p (c e) -> p c e", c=NCH),
                      shup[:, :, sx * EDIM:(sx + 1) * EDIM])
                wrd = P.dma("sync", dnst.rearrange("p (c o) -> p c o", c=4),
                            shdn[sx * 4:(sx + 1) * 4, :, :].rearrange("r p o -> p r o"))
            P.wait("tensor", "WD", wrd)
            if ex < NEXP:
                banks.acquire(6)
                P.wait("tensor", "GD", gate_done)
                P.op("tensor", lambda e, ex=ex: e.matmul(
                    psum[6][:, 0:TC], rowa[32:48, ex * 128:(ex + 1) * 128], gfm[:, 0:TC],
                    start=True, stop=True), inc="T")
            gb = P.cnt["T"]
            for et in range(4):
                ub = 4 + (et % 2)
                banks.acquire(ub)
                P.wait("tensor", "V", ln3_done)
                for kc in range(NCH):
                    P.op("tensor", lambda e, kc=kc, et=et, upst=upst, ub=ub: e.matmul(
                        psum[ub][:, 0:TC], upst[:, kc * 512 + et * 128: kc * 512 + (et + 1) * 128],
                        ln3fm(kc), start=(kc == 0), stop=(kc == NCH - 1)),
                        inc="T" if kc == NCH - 1 else None)
                wr = P.cnt["T"]
                P.wait("scalar", "T", max(wr, act_guard[0]))
                P.op("scalar", lambda e, ub=ub: e.activation(
                    out=act_sb, in_=psum[ub][:, 0:TC], func=AF.Gelu), inc="Sc")
                banks.release(ub, "Sc")
                sc = P.cnt["Sc"]
                if ex < NEXP:
                    P.wait("vector", "Sc", sc)
                    P.wait("vector", "T", gb)
                    P.op("vector", lambda e: e.tensor_tensor(
                        act_sb, act_sb, psum[6][:, 0:TC], OP.mult), inc="V")
                    act_v[0] = P.cnt["V"]
                    P.wait("tensor", "V", act_v[0])
                else:
                    P.wait("tensor", "Sc", sc)
                st_ = (ex == NEX - 1 and et == 3)
                for t2 in range(2):
                    for osp in range(2):
                        P.op("tensor", lambda e, t2=t2, osp=osp, et=et, dnst=dnst, ds=dn_start, st_=st_: e.matmul(
                            psum[t2 * 2 + osp][:, :], act_sb[:, t2 * 128:(t2 + 1) * 128],
                            dnst[:, et * 1024 + osp * 512: et * 1024 + osp * 512 + 512],
                            start=ds, stop=st_),
                            inc="T" if (t2 == 1 and osp == 1) else None)
                act_guard[0] = P.cnt["T"]
                dn_start = False
            if ex < NEXP:
                banks.release(6, "V")
        dn_done = P.cnt["T"]
        dn_start = True
        P.wait("vector", "T", dn_done)
        for t2 in range(2):
            tile = 2 * ch + t2
            for osp in range(2):
                P.op("vector", lambda e, tile=tile, t2=t2, osp=osp: e.tensor_tensor(
                    xres_t(tile)[:, osp * 512:(osp + 1) * 512],
                    xres_t(tile)[:, osp * 512:(osp + 1) * 512],
                    psum[t2 * 2 + osp][:, :], OP.add), inc="V")
                banks.release(t2 * 2 + osp, "V")



    store_y(P, dram, x_res)


def store_y(P, dram, x_res):
    for e_ in ("V", "T", "Sc", "WD"):
        P.wait("gpsimd", e_)
    yo = dram["y"].rearrange("(t p) d -> t p d", p=128)
    for g in range(4):
        P.dma("gpsimd", yo[g * 4:(g + 1) * 4, :, :].rearrange("t p d -> p t d"),
              x_res[:, g * 4096:(g + 1) * 4096].rearrange("p (t d) -> p t d", t=4), inc="St")
    P.wait("gpsimd", "St")



def _consts_array(ins):
    c = np.zeros((128, 1024), np.float32)
    c[:, 0:128] = np.eye(128, dtype=np.float32)
    tri = (np.arange(128)[None, :] >= np.arange(128)[:, None]).astype(np.float32)
    c[:, 128:640] = np.tile(tri, (1, 4))
    return c


def _rows_array(ins):
    r = np.zeros((48, 2048), np.float32)
    for k in range(16):
        r[32 + k, k * 128:(k + 1) * 128] = 1.0
    r[0, 0:128] = 1.0
    r[0, 128:1152] = np.asarray(ins["L0_attn_bo"], np.float32)
    r[1, 0:1024] = np.asarray(ins["L0_attn_bq"], np.float32)
    r[2, 0:1024] = np.asarray(ins["L0_attn_bk"], np.float32)
    r[3, 0:1024] = np.asarray(ins["L0_attn_bv"], np.float32)
    r[4, 0:1024] = np.asarray(ins["L0_ffn_bias"], np.float32)
    return r


def _make_in_maps(ins):
    x = np.ascontiguousarray(np.asarray(ins["L0_attn_x"], np.float32))

    def T(a):
        return np.ascontiguousarray(np.asarray(a, np.float32).T)

    def colt(v, nch):
        return np.ascontiguousarray(np.asarray(v, np.float32).reshape(nch, 128).T)

    shup = np.asarray(ins["L1_ffn_shared_up"], np.float32)
    shdn = np.asarray(ins["L1_ffn_shared_down"], np.float32)
    roup = np.asarray(ins["L1_ffn_routed_up"], np.float32)
    rodn = np.asarray(ins["L1_ffn_routed_down"], np.float32)

    common = {
        "x_all": x,
        "wq": np.ascontiguousarray(np.asarray(ins["L0_attn_wq"], np.float32)),
        "wk": np.ascontiguousarray(np.asarray(ins["L0_attn_wk"], np.float32)),
        "wv": np.ascontiguousarray(np.asarray(ins["L0_attn_wv"], np.float32)),
        "wo": np.ascontiguousarray(np.asarray(ins["L0_attn_wo"], np.float32)),
        "bq_c": colt(ins["L0_attn_bq"], NCH),
        "bk_c": colt(ins["L0_attn_bk"], NCH),
        "rows_c": _rows_array(ins),
        "w_ffn0_t": T(ins["L0_ffn_weight"]),
        "w_in_t": T(ins["L1_attn_W_in"]),
        "conv_w_c": np.ascontiguousarray(
            np.asarray(ins["L1_attn_conv_w"], np.float32)[:, 0, :]
            .reshape(DI // 128, 128, 4).transpose(1, 0, 2).reshape(128, -1)),
        "conv_b_c": colt(ins["L1_attn_conv_b"], DI // 128),
        "w_x_t": T(ins["L1_attn_W_x"]),
        "w_dt_t": T(ins["L1_attn_W_dt"]),
        "b_dt_c": colt(ins["L1_attn_b_dt"], DI // 128),
        "a_log_c": np.ascontiguousarray(
            np.asarray(ins["L1_attn_A_log"], np.float32)
            .reshape(DI // 128, 128, DSTATE).transpose(1, 0, 2).reshape(128, -1)),
        "d_par_c": colt(ins["L1_attn_D_param"], DI // 128),
        "w_out_t": T(ins["L1_attn_W_out"]),
        "ln_wb": np.ascontiguousarray(np.concatenate([
            np.tile(np.asarray(ins["L1_attn_ln_w"], np.float32)[None, :], (128, 1)),
            np.tile(np.asarray(ins["L1_attn_ln_b"], np.float32)[None, :], (128, 1))], axis=1)),
        "shup_t": np.ascontiguousarray(shup.reshape(NSH * EDIM, D).T),
        "shdn_t": np.ascontiguousarray(shdn.transpose(0, 2, 1).reshape(NSH * EDIM, D)),
        "roup_t": np.ascontiguousarray(roup.reshape(NEXP * EDIM, D).T),
        "rodn_t": np.ascontiguousarray(rodn.transpose(0, 2, 1).reshape(NEXP * EDIM, D)),
        "wrt_t": T(ins["L1_ffn_router_w"]),
        "consts": _consts_array(ins),
    }
    maps = []
    for c in range(8):
        m = dict(common)
        m["x_own"] = np.ascontiguousarray(x[c])
        maps.append(m)
    return maps


def kernel(**inputs):
    ins = {k: np.asarray(v) for k, v in inputs.items()}
    nc, _ = build_program()
    res = run_bass_kernel_spmd(nc, _make_in_maps(ins), core_ids=list(range(8)))
    return np.stack([np.asarray(res.results[i]["y"]) for i in range(8)], axis=0)


# version 03
# speedup vs baseline: 1.1168x; 1.1168x over previous
import contextlib

import numpy as np

import concourse.bass as bass
import concourse.mybir as mybir
from concourse.bass_utils import run_bass_kernel_spmd

F32 = mybir.dt.float32
BF16 = mybir.dt.bfloat16
F16 = mybir.dt.float16
OP = mybir.AluOpType
AF = mybir.ActivationFunctionType
AX = mybir.AxisListType

B, S, D = 8, 2048, 1024
H, DK = 16, 64
NT = S // 128
NCH = D // 128
DI = 2048
DSTATE = 16
DTR = 64
NEXP, EDIM = 16, 512
NSH = 2
EPS = 1e-5

ENGINES = ("sync", "tensor", "vector", "scalar", "gpsimd")


class Prog:
    def __init__(self, nc, sems):
        self.nc = nc
        self.sems = sems
        self.ops = {e: [] for e in ENGINES}
        self.cnt = {name: 0 for name in sems}

    def op(self, eng, fn, inc=None, n=1):
        if inc is None:
            self.ops[eng].append(fn)
        else:
            sem = self.sems[inc]

            def g(e, fn=fn, sem=sem, n=n):
                fn(e).then_inc(sem, n)

            self.ops[eng].append(g)
            self.cnt[inc] += n
        return self.cnt[inc] if inc else None

    def wait(self, eng, sem, val=None):
        v = self.cnt[sem] if val is None else val
        s = self.sems[sem]
        self.ops[eng].append(lambda e, s=s, v=v: e.wait_ge(s, v))

    def dma(self, q, out, in_, inc=None):
        inc = inc or {"sync": "WD", "gpsimd": "GD", "scalar": "SD"}[q]
        self.op(q, lambda e, out=out, in_=in_: e.dma_start(out=out, in_=in_),
                inc=inc, n=16)
        return self.cnt[inc]

    def mark(self, eng, sem):
        if eng in ("vector", "gpsimd"):
            self.op(eng, lambda e: e.engine_nop(), inc=sem)
        else:
            self.op(eng, lambda e: e.nop(), inc=sem)
        return self.cnt[sem]


class Banks:

    def __init__(self, P):
        self.P = P
        self.state = [None] * 8

    def acquire(self, i):
        st = self.state[i]
        if st is not None:
            self.P.wait("tensor", st[0], st[1])

    def release(self, i, sem):
        self.state[i] = (sem, self.P.cnt[sem])



def build_program(debug_taps=(), stage=None):
    nc = bass.Bass("TRN2", target_bir_lowering=False, detect_race_conditions=False)
    dram = {}

    def din(name, shape, dt=F32):
        dram[name] = nc.dram_tensor(name, list(shape), dt, kind="ExternalInput").ap()

    def dout(name, shape, dt=F32):
        dram[name] = nc.dram_tensor(name, list(shape), dt, kind="ExternalOutput").ap()

    din("x_own", [S, D])
    din("x_all", [B, S, D])
    din("wq", [D, D]); din("wk", [D, D]); din("wv", [D, D]); din("wo", [D, D])
    din("bq_c", [128, NCH]); din("bk_c", [128, NCH])
    din("rows_c", [48, 2048])
    din("w_ffn0_t", [D, D])
    din("w_in_t", [D, 2 * DI])
    din("conv_w_c", [128, (DI // 128) * 4])
    din("conv_b_c", [128, DI // 128])
    din("w_x_t", [DI, 96])
    din("w_dt_t", [DTR, DI])
    din("b_dt_c", [128, DI // 128])
    din("a_log_c", [128, (DI // 128) * DSTATE])
    din("d_par_c", [128, DI // 128])
    din("w_out_t", [DI, D])
    din("ln_wb", [128, 2 * D])
    din("shup_t", [D, NSH * EDIM]); din("shdn_t", [NSH * EDIM, D])
    din("roup_t", [D, NEXP * EDIM]); din("rodn_t", [NEXP * EDIM, D])
    din("wrt_t", [D, NEXP])
    din("consts", [128, 1024])
    dout("y", [S, D])

    dbg_names = []
    es = contextlib.ExitStack()
    with es:
        sb = {}
        sb["x_res"] = es.enter_context(nc.sbuf_tensor("x_res", [128, NT * 1024], F32))
        sb["fmA"] = es.enter_context(nc.sbuf_tensor("fmA", [128, NT * 1024], F32))
        sb["s1"] = es.enter_context(nc.sbuf_tensor("s1", [128, 6912], F32))
        sb["s2"] = es.enter_context(nc.sbuf_tensor("s2", [128, 6144], F32))
        sb["wb"] = es.enter_context(nc.sbuf_tensor("wb", [128, 3584], F32))
        sb["cst"] = es.enter_context(nc.sbuf_tensor("cst", [128, 1024], F32))
        sb["rowa"] = es.enter_context(nc.sbuf_tensor("rowa", [128, 2048], F32))
        sb["stat"] = es.enter_context(nc.sbuf_tensor("stat", [128, 128], F32))
        psum = [es.enter_context(nc.psum_tensor(f"pb{i}", [128, 512], F32)) for i in range(8)]
        sb["psum"] = psum

        def _alias(name, shape, dtype, arena, byte_off):
            return nc.alloc_sbuf_tensor_at(
                name, shape, dtype, offset=nc.lookup_mloc(sb[arena]).addr + byte_off)
        sb["lnj16"] = _alias("lnj16", [128, 16384], F16, "fmA", 0)
        sb["att16"] = _alias("att16", [128, 16384], F16, "fmA", 32768)
        sb["kq16"] = _alias("kq16", [128, 4096], F16, "s1", 0)
        sb["v16"] = _alias("v16", [128, 2080], F16, "s1", 8192)
        sb["er16"] = _alias("er16", [128, 4096], F16, "s1", 12544)
        sb["w16"] = _alias("w16", [128, 4096], F16, "wb", 4096)
        sb["mask16"] = _alias("mask16", [128, 512], F16, "cst", 2560)
        sb["wqB"] = _alias("wqB", [128, 8192], BF16, "s2", 0)
        sb["xqB"] = _alias("xqB", [128, 4096], BF16, "s1", 0)
        sb["lnH"] = _alias("lnH", [128, 16384], BF16, "fmA", 0)
        sb["lnL"] = _alias("lnL", [128, 16384], BF16, "fmA", 32768)
        sb["upH"] = _alias("upH", [128, 4096], BF16, "s2", 0)
        sb["upL"] = _alias("upL", [128, 4096], BF16, "s2", 8192)
        sb["dnH"] = _alias("dnH", [128, 4096], BF16, "s2", 16384)
        sb["dnL"] = _alias("dnL", [128, 4096], BF16, "wb", 0)
        sb["wrtH"] = _alias("wrtH", [128, 128], BF16, "wb", 8192)
        sb["wrtL"] = _alias("wrtL", [128, 128], BF16, "wb", 8448)
        sb["act16"] = _alias("act16", [128, 4096], BF16, "s1", 8192)

        sems = {}
        for name in ("WD", "GD", "SD", "T", "V", "Sc", "G", "St"):
            sems[name] = es.enter_context(nc.semaphore(name))
        P = Prog(nc, sems)
        banks = Banks(P)

        def dbg_tap(name, ap):
            if name in debug_taps:
                dout("dbg_" + name, [ap.shape[0], ap.shape[-1]])
                dbg_names.append(name)
                for e_ in ("V", "T", "Sc"):
                    P.wait("gpsimd", e_)
                P.dma("gpsimd", dram["dbg_" + name][:, :], ap, inc="St")
                P.wait("vector", "St")
                P.wait("scalar", "St")

        emit_all(P, banks, dram, sb, dbg_tap, stage)

        block = es.enter_context(nc.Block())
        if True:
            @block.sync
            def _(eng):
                for f in P.ops["sync"]:
                    f(eng)

            @block.tensor
            def _(eng):
                for f in P.ops["tensor"]:
                    f(eng)

            @block.vector
            def _(eng):
                for f in P.ops["vector"]:
                    f(eng)

            @block.scalar
            def _(eng):
                for f in P.ops["scalar"]:
                    f(eng)

            @block.gpsimd
            def _(eng):
                for f in P.ops["gpsimd"]:
                    f(eng)

    return nc, dbg_names



def vfence(P):
    P.mark("vector", "V")
    P.wait("vector", "V")


def ln_tile(P, sb, src_ap, dst_ap, scratch_ap, gamma=None, beta=None):
    stat = sb["stat"]
    mean = stat[:, 0:1]
    vsum = stat[:, 1:2]
    rs = stat[:, 2:3]
    W = src_ap.shape[-1]
    P.op("vector", lambda e: e.tensor_reduce(mean, src_ap, axis=AX.X, op=OP.add), inc="V")
    P.wait("vector", "V")
    P.op("vector", lambda e: e.tensor_scalar(mean, mean, 1.0 / W, None, OP.mult), inc="V")
    P.wait("vector", "V")
    P.op("vector", lambda e: e.tensor_scalar(dst_ap, src_ap, mean, None, OP.subtract))
    P.op("vector", lambda e: e.scalar_tensor_tensor(
        scratch_ap, dst_ap, 1.0, dst_ap, OP.mult, OP.mult, accum_out=vsum), inc="V")
    P.wait("scalar", "V")
    P.op("scalar", lambda e: e.activation(out=rs, in_=vsum, func=AF.Sqrt,
                                          bias=stat[:, 3:4], scale=1.0 / W), inc="Sc")
    P.wait("vector", "Sc")
    P.op("vector", lambda e: e.reciprocal(rs, rs), inc="V")
    P.wait("vector", "V")
    last_ops = [lambda e: e.tensor_scalar(dst_ap, dst_ap, rs, None, OP.mult)]
    if gamma is not None:
        last_ops.append(lambda e: e.tensor_scalar(dst_ap, dst_ap, gamma, None, OP.mult))
    if beta is not None:
        last_ops.append(lambda e: e.tensor_scalar(dst_ap, dst_ap, beta, None, OP.add))
    for i, f in enumerate(last_ops):
        P.op("vector", f, inc="V" if i == len(last_ops) - 1 else None)
    return P.cnt["V"]


def ln_tile_c(P, sb, src_ap, dst_ap, scratch_ap, mean, vsum, rs):
    W = 1024
    P.op("vector", lambda e: e.tensor_reduce(mean, src_ap, axis=AX.X, op=OP.add), inc="V")
    P.wait("vector", "V")
    P.op("vector", lambda e: e.tensor_scalar(mean, mean, 1.0 / W, None, OP.mult), inc="V")
    P.wait("vector", "V")
    P.op("vector", lambda e: e.tensor_scalar(dst_ap, src_ap, mean, None, OP.subtract))
    P.op("vector", lambda e: e.scalar_tensor_tensor(
        scratch_ap, dst_ap, 1.0, dst_ap, OP.mult, OP.mult, accum_out=vsum), inc="V")
    P.wait("scalar", "V")
    P.op("scalar", lambda e: e.activation(out=rs, in_=vsum, func=AF.Sqrt,
                                          bias=sb["stat"][:, 3:4], scale=1.0 / W), inc="Sc")
    P.wait("vector", "Sc")
    P.op("vector", lambda e: e.reciprocal(rs, rs), inc="V")
    P.wait("vector", "V")
    P.op("vector", lambda e: e.tensor_scalar(dst_ap, dst_ap, rs, None, OP.mult), inc="V")
    return P.cnt["V"]


def transpose_tile(P, banks, sb, src_ap_1024, dst_fn, bank_pair=(6, 7),
                   evac_eng="scalar", out_scale=None, ident=None, dst_lo_fn=None):
    psum = sb["psum"]
    if ident is None:
        ident = sb["cst"][:, 0:128]
    is_16 = (ident.dtype in (BF16, F16))

    def pslot(bank, j):
        if is_16:
            return psum[bank][:, j * 64:(j + 1) * 64].bitcast(ident.dtype)
        return psum[bank][:, j * 128:(j + 1) * 128]

    sem = {"scalar": "Sc", "vector": "V"}[evac_eng]
    for half, bank in ((0, bank_pair[0]), (1, bank_pair[1])):
        banks.acquire(bank)
        for j in range(4):
            kc = half * 4 + j
            P.op("tensor",
                 lambda e, kc=kc, j=j, bank=bank: e.matmul(
                     pslot(bank, j),
                     src_ap_1024[:, kc * 128:(kc + 1) * 128], ident,
                     is_transpose=True, start=True, stop=True),
                 inc="T" if j == 3 else None)
        wr = P.cnt["T"]
        P.wait(evac_eng, "T", wr)
        for j in range(4):
            kc = half * 4 + j
            if evac_eng == "scalar":
                if out_scale is None:
                    P.op("scalar", lambda e, kc=kc, j=j, bank=bank: e.copy(
                        out=dst_fn(kc), in_=pslot(bank, j)),
                        inc=sem if j == 3 else None)
                else:
                    P.op("scalar", lambda e, kc=kc, j=j, bank=bank: e.activation(
                        out=dst_fn(kc), in_=pslot(bank, j),
                        func=AF.Copy, scale=out_scale), inc=sem if j == 3 else None)
            else:
                P.op("vector", lambda e, kc=kc, j=j, bank=bank: e.tensor_copy(
                    dst_fn(kc), pslot(bank, j)),
                    inc=None if dst_lo_fn is not None else (sem if j == 3 else None))
                if dst_lo_fn is not None:
                    P.op("vector", lambda e, kc=kc, j=j, bank=bank: e.tensor_tensor(
                        dst_lo_fn(kc), pslot(bank, j), dst_fn(kc), OP.subtract),
                        inc=sem if j == 3 else None)
        banks.release(bank, sem)
    return P.cnt[sem]




def scalar_finalize(P, banks, sb, col_ap, out_row_ap, op=OP.max):
    psum = sb["psum"]
    ident = sb["cst"][:, 0:128]
    banks.acquire(6)
    P.wait("tensor", "V")
    P.op("tensor", lambda e: e.matmul(psum[6][0:1, 0:128], col_ap, ident,
                                      is_transpose=True, start=True, stop=True), inc="T")
    wr = P.cnt["T"]
    P.wait("vector", "T", wr)
    row = sb["rowa"][0:1, 1536:1664]
    P.op("vector", lambda e: e.tensor_copy(row, psum[6][0:1, 0:128]), inc="V")
    banks.release(6, "V")
    P.wait("vector", "V")
    P.op("vector", lambda e: e.tensor_reduce(out_row_ap, row, axis=AX.X, op=op), inc="V")
    P.wait("vector", "V")


def bcast_p0(P, banks, sb, src_row_ap, dst_ap, n):
    psum = sb["psum"]
    ones_row = sb["rowa"][0:1, 0:128]
    banks.acquire(6)
    P.wait("tensor", "V")
    P.op("tensor", lambda e: e.matmul(psum[6][:, 0:n], ones_row, src_row_ap,
                                      start=True, stop=True), inc="T")
    wr = P.cnt["T"]
    P.wait("vector", "T", wr)
    P.op("vector", lambda e: e.tensor_copy(dst_ap, psum[6][:, 0:n]), inc="V")
    banks.release(6, "V")
    P.wait("vector", "V")


def quant_grid_tile(P, sb, y_ap, out_ap, scr, c4col, mcol):
    c, d_, f1, msk = scr
    P.op("vector", lambda e: e.tensor_scalar(c, y_ap, c4col, None, OP.mult))
    P.op("vector", lambda e: e.tensor_tensor(d_, c, y_ap, OP.subtract))
    P.op("vector", lambda e: e.tensor_tensor(c, c, d_, OP.subtract))
    P.op("vector", lambda e: e.tensor_scalar(f1, y_ap, mcol, None, OP.add))
    P.op("vector", lambda e: e.tensor_scalar(f1, f1, mcol, None, OP.subtract))
    P.op("vector", lambda e: e.tensor_tensor(msk, y_ap, y_ap, OP.mult))
    P.op("vector", lambda e: e.tensor_scalar(msk, msk, 2.0 ** -12, None, OP.is_ge))
    P.op("vector", lambda e: e.tensor_tensor(d_, c, f1, OP.subtract))
    P.op("vector", lambda e: e.tensor_tensor(d_, d_, msk, OP.mult))
    P.op("vector", lambda e: e.tensor_tensor(out_ap, d_, f1, OP.add), inc="V")



def emit_all(P, banks, dram, sb, dbg_tap, stage=None):
    psum = sb["psum"]
    x_res = sb["x_res"]
    fmA = sb["fmA"]
    s1 = sb["s1"]
    s2 = sb["s2"]
    wb = sb["wb"]
    cst = sb["cst"]
    rowa = sb["rowa"]

    def xres_t(tt):
        return x_res[:, tt * 1024:(tt + 1) * 1024]

    def fm_ap(kc, tt):
        return fmA[:, kc * 2048 + tt * 128: kc * 2048 + (tt + 1) * 128]

    def fm_span(kc, q0, w):
        return fmA[:, kc * 2048 + q0: kc * 2048 + q0 + w]

    ones_row = rowa[0:1, 0:128]
    bias_row = rowa[0:1, 128:1152]
    bv_row = rowa[0:1, 1408:1536]
    recip_row = rowa[0:1, 1536:2048]
    maskrep = cst[:, 128:640]

    xo = dram["x_own"].rearrange("(t p) d -> t p d", p=128)
    for g in range(4):
        P.dma("sync", x_res[:, g * 4096:(g + 1) * 4096].rearrange("p (t d) -> p t d", t=4),
              xo[g * 4:(g + 1) * 4, :, :].rearrange("t p d -> p t d"))
    P.dma("sync", cst[:, :], dram["consts"][:, :])
    P.dma("sync", rowa[0:1, 0:1152], dram["rows_c"][0:1, 0:1152])
    P.dma("sync", rowa[32:48, :], dram["rows_c"][32:48, :])
    P.dma("sync", sb["stat"][:, 8:8 + NCH], dram["bq_c"][:, :])
    P.dma("sync", sb["stat"][:, 16:16 + NCH], dram["bk_c"][:, :])
    ldx = P.cnt["WD"]
    if stage == "x0":
        store_y(P, dram, x_res)
        return
    P.op("vector", lambda e: e.memset(sb["stat"][:, :], 0.0))
    P.op("vector", lambda e: e.memset(sb["stat"][:, 3:4], EPS))
    P.wait("vector", "WD", ldx)

    ln_tmp = s2[:, 0:1024]
    ln_sq = s2[:, 1024:2048]
    P.wait("vector", "WD", ldx)
    P.wait("tensor", "WD", ldx)
    for tt in range(NT):
        if tt > 0:
            P.wait("vector", "Sc")
        ln_tile(P, sb, xres_t(tt), ln_tmp, ln_sq)
        if stage == "ln0_notr":
            P.op("vector", lambda e, tt=tt: e.tensor_copy(
                fmA[:, tt * 1024:(tt + 1) * 1024], ln_tmp), inc="V")
            P.mark("scalar", "Sc")
            continue
        P.wait("tensor", "V")
        transpose_tile(P, banks, sb, ln_tmp, lambda kc, tt=tt: fm_ap(kc, tt))
    fm_done = P.cnt["Sc"]
    dbg_tap("ln0_fm", fmA[:, :])
    if stage in ("ln0", "ln0_notr"):
        store_y(P, dram, x_res)
        return

    k_pair = s1[:, 0:2048]
    q_pair = s1[:, 2048:4096]
    v_pair = s1[:, 4096:4096 + 16 * 130]
    rb = s1[0:64, 6176:6688]
    e_ring = s2[:, 0:4096]
    att_fm = s2[:, 4096:6144]

    wqs = wb[:, 0:1024]
    wks = wb[:, 1024:2048]
    wvs = wb[:, 2048:3072]

    def wslice_dma(dst, wname, hg):
        src = dram[wname][:, hg * 128:(hg + 1) * 128].rearrange("(c p) o -> p c o", p=128)
        return P.dma("sync", dst.rearrange("p (c o) -> p c o", c=NCH), src)

    last_hg_T = None
    for hg in range(8):
        if last_hg_T is not None:
            P.wait("sync", "T", last_hg_T)
        wslice_dma(wqs, "wq", hg)
        wslice_dma(wks, "wk", hg)
        P.dma("sync", bv_row, dram["rows_c"][3:4, hg * 128:(hg + 1) * 128])
        w_ready_kq = P.cnt["WD"]
        wslice_dma(wvs, "wv", hg)
        w_ready_v = P.cnt["WD"]

        P.wait("tensor", "WD", w_ready_kq)
        P.wait("tensor", "Sc", fm_done)

        def proj_fm(wsl, bcol, dst, scale, bank):
            for sp in range(4):
                banks.acquire(bank)
                for kc in range(NCH):
                    P.op("tensor", lambda e, kc=kc, sp=sp, wsl=wsl, bank=bank: e.matmul(
                        psum[bank][:, :], wsl[:, kc * 128:(kc + 1) * 128],
                        fm_span(kc, sp * 512, 512), start=(kc == 0),
                        stop=(kc == NCH - 1)), inc="T" if kc == NCH - 1 else None)
                wr = P.cnt["T"]
                P.wait("vector", "T", wr)
                if scale == 1.0:
                    P.op("vector", lambda e, sp=sp, dst=dst, bank=bank, bcol=bcol: e.tensor_scalar(
                        dst[:, sp * 512:(sp + 1) * 512], psum[bank][:, :],
                        bcol, None, OP.add), inc="V")
                else:
                    P.op("vector", lambda e, sp=sp, dst=dst, scale=scale, bank=bank, bcol=bcol: e.tensor_scalar(
                        dst[:, sp * 512:(sp + 1) * 512], psum[bank][:, :],
                        bcol, scale, OP.add, OP.mult), inc="V")
                banks.release(bank, "V")

        proj_fm(wks, sb["stat"][:, 16 + hg:17 + hg], k_pair, 1.0, 0)
        proj_fm(wqs, sb["stat"][:, 8 + hg:9 + hg], q_pair, 0.125, 1)

        P.wait("tensor", "WD", w_ready_v)
        for tt in range(NT):
            banks.acquire(2)
            for kc in range(NCH):
                P.op("tensor", lambda e, kc=kc, tt=tt: e.matmul(
                    psum[2][:, 0:128], fm_ap(kc, tt), wvs[:, kc * 128:(kc + 1) * 128],
                    start=(kc == 0), stop=False))
            P.op("tensor", lambda e: e.matmul(
                psum[2][:, 0:128], ones_row, bv_row,
                start=False, stop=True), inc="T")
            wr = P.cnt["T"]
            P.wait("vector", "T", wr)
            P.op("vector", lambda e, tt=tt: e.tensor_copy(
                v_pair[:, tt * 130:(tt + 1) * 130].rearrange("p (h c) -> p h c", h=2)[:, :, 0:64],
                psum[2][:, 0:128].rearrange("p (h c) -> p h c", h=2)), inc="V")
            banks.release(2, "V")
        P.op("vector", lambda e: e.memset(
            v_pair.rearrange("p (t h c) -> p t h c", t=16, h=2)[:, :, :, 64:65], 1.0),
            inc="V")
        v_done = P.cnt["V"]

        P.wait("tensor", "V")
        slot_guard = [0] * 8
        for h2 in range(2):
            hr0 = 64 * h2
            pv_bank = 3 + h2
            sc_bank = 0 + h2
            for sp in range(4):
                banks.acquire(pv_bank)
                nkc = 4 * (sp + 1)
                for kc in range(nkc):
                    is_diag = kc >= 4 * sp
                    j = kc - 4 * sp
                    q0 = 128 * j if is_diag else 0
                    qw = 512 - q0
                    slot = (4 + j) if is_diag else (kc % 2)
                    banks.acquire(sc_bank)
                    P.op("tensor", lambda e, kc=kc, sp=sp, hr0=hr0, q0=q0, qw=qw, b=sc_bank: e.matmul(
                        psum[b][:, 0:qw],
                        k_pair[hr0:hr0 + 64, kc * 128:(kc + 1) * 128],
                        q_pair[hr0:hr0 + 64, sp * 512 + q0: sp * 512 + q0 + qw],
                        start=True, stop=True), inc="T")
                    wr = P.cnt["T"]
                    P.wait("scalar", "T", max(wr, slot_guard[slot]))
                    P.op("scalar", lambda e, slot=slot, qw=qw, b=sc_bank: e.activation(
                        out=e_ring[:, slot * 512: slot * 512 + qw],
                        in_=psum[b][:, 0:qw], func=AF.Exp), inc="Sc")
                    banks.release(sc_bank, "Sc")
                    if not is_diag:
                        P.wait("tensor", "Sc")
                        if kc == 0:
                            P.wait("tensor", "V", v_done)
                        P.op("tensor", lambda e, kc=kc, h2=h2, pb=pv_bank, slot=slot: e.matmul(
                            psum[pb][0:65, 0:512],
                            v_pair[:, kc * 130 + 65 * h2: kc * 130 + 65 * h2 + 65],
                            e_ring[:, slot * 512: slot * 512 + 512],
                            start=(kc == 0), stop=False), inc="T")
                        slot_guard[slot] = P.cnt["T"]
                P.wait("vector", "Sc")
                P.op("vector", lambda e: e.tensor_tensor(
                    e_ring[:, 4 * 512:8 * 512].rearrange("p (s c) -> p s c", s=4)[:, :, 0:128],
                    e_ring[:, 4 * 512:8 * 512].rearrange("p (s c) -> p s c", s=4)[:, :, 0:128],
                    maskrep.rearrange("p (s c) -> p s c", s=4), OP.mult), inc="V")
                mk = P.cnt["V"]
                P.wait("tensor", "V", mk)
                if sp == 0:
                    P.wait("tensor", "V", v_done)
                for j in range(4):
                    kc = 4 * sp + j
                    q0 = 128 * j
                    P.op("tensor", lambda e, j=j, kc=kc, q0=q0, h2=h2, pb=pv_bank, sp=sp: e.matmul(
                        psum[pb][0:65, q0:512],
                        v_pair[:, kc * 130 + 65 * h2: kc * 130 + 65 * h2 + 65],
                        e_ring[:, (4 + j) * 512: (4 + j) * 512 + (512 - q0)],
                        start=(sp == 0 and j == 0), stop=(j == 3)), inc="T")
                    slot_guard[4 + j] = P.cnt["T"]
                wr = P.cnt["T"]
                P.wait("vector", "T", wr)
                P.op("vector", lambda e, pb=pv_bank: e.reciprocal(
                    recip_row, psum[pb][64:65, 0:512]), inc="V")
                rp = P.cnt["V"]
                banks.acquire(5)
                P.wait("tensor", "V", rp)
                P.op("tensor", lambda e: e.matmul(
                    psum[5][0:64, 0:512], ones_row[:, 0:64], recip_row,
                    start=True, stop=True), inc="T")
                wr2 = P.cnt["T"]
                P.wait("vector", "T", wr2)
                P.op("vector", lambda e: e.tensor_copy(rb, psum[5][0:64, 0:512]), inc="V")
                banks.release(5, "V")
                P.op("vector", lambda e, hr0=hr0, sp=sp, pb=pv_bank: e.tensor_tensor(
                    att_fm[hr0:hr0 + 64, sp * 512:(sp + 1) * 512],
                    psum[pb][0:64, 0:512], rb, OP.mult), inc="V")
                banks.release(pv_bank, "V")
        att_done = P.cnt["V"]

        P.wait("sync", "V", att_done)
        wos = wb[:, 0:1024]
        wo_rdy = P.dma("sync", wos, dram["wo"][hg * 128:(hg + 1) * 128, :])
        P.wait("tensor", "WD", wo_rdy)
        for tt in range(NT):
            for osp in range(2):
                banks.acquire(5)
                P.op("tensor", lambda e, tt=tt, osp=osp, hg=hg: e.matmul(
                    psum[5][:, :], att_fm[:, tt * 128:(tt + 1) * 128],
                    wos[:, osp * 512:(osp + 1) * 512],
                    start=True, stop=(hg != 0)), inc=None if hg == 0 else "T")
                if hg == 0:
                    P.op("tensor", lambda e, osp=osp: e.matmul(
                        psum[5][:, :], ones_row,
                        bias_row[:, osp * 512:(osp + 1) * 512],
                        start=False, stop=True), inc="T")
                wr = P.cnt["T"]
                P.wait("vector", "T", wr)
                P.op("vector", lambda e, tt=tt, osp=osp: e.tensor_tensor(
                    xres_t(tt)[:, osp * 512:(osp + 1) * 512],
                    xres_t(tt)[:, osp * 512:(osp + 1) * 512],
                    psum[5][:, :], OP.add), inc="V")
                banks.release(5, "V")
        last_hg_T = P.cnt["T"]



    stat = sb["stat"]
    amax_run = stat[:, 32:33]
    tmax = stat[:, 33:34]
    mean_c = stat[:, 35:36]
    vsum_c = stat[:, 36:37]
    rs_c = stat[:, 37:38]
    P.op("vector", lambda e: e.memset(amax_run, 0.0))
    lnj_bf = sb["lnj16"][:, :]
    att_bf = sb["att16"][:, :]
    kq_bf = sb["kq16"][:, :]
    v_bf = sb["v16"][:, :]
    er_bf = sb["er16"][:, :]
    w_bf = sb["w16"][:, :]
    xt32 = s2[:, 0:1024]
    lntm32 = s2[:, 2048:3072]
    x1t = s2[:, 1024:2048]
    mask_bf = sb["mask16"][:, :]
    rb_c = s1[0:64, 6176:6688]
    P.op("vector", lambda e: e.tensor_copy(mask_bf, cst[:, 128:640]), inc="V")

    xall = dram["x_all"].rearrange("b (t p) d -> b t p d", p=128)
    for j in range(B):
        for tt in range(NT):
            P.wait("gpsimd", "V")
            xrd = P.dma("gpsimd", xt32, xall[j, tt, :, :])
            P.wait("vector", "GD", xrd)
            ln_tile_c(P, sb, xt32, lntm32, s2[:, 1024:2048], mean_c, vsum_c, rs_c)
            P.wait("tensor", "V")
            transpose_tile(P, banks, sb, lntm32,
                           lambda kc, tt=tt: lnj_bf[:, kc * 2048 + tt * 128: kc * 2048 + (tt + 1) * 128],
                           bank_pair=(6, 7), evac_eng="vector")
        lnj_done = P.cnt["V"]

        last_T = None
        for hg in range(8):
            if last_T is not None:
                P.wait("gpsimd", "T", last_T)
            for wi, wname in enumerate(("wq", "wk", "wv")):
                P.dma("gpsimd", w_bf[:, wi * 1024:(wi + 1) * 1024].rearrange("p (c o) -> p c o", c=NCH),
                      dram[wname][:, hg * 128:(hg + 1) * 128].rearrange("(c p) o -> p c o", p=128))
            P.dma("gpsimd", bv_row, dram["rows_c"][3:4, hg * 128:(hg + 1) * 128])
            w_rdy = P.cnt["GD"]
            P.wait("tensor", "GD", w_rdy)
            P.wait("tensor", "V", lnj_done)
            for which, wofs, bcol, dst_ofs, sc_ in (
                    ("k", 1024, sb["stat"][:, 16 + hg:17 + hg], 0, 1.0),
                    ("q", 0, sb["stat"][:, 8 + hg:9 + hg], 2048, 0.125)):
                for sp in range(4):
                    banks.acquire(0)
                    for kc in range(NCH):
                        P.op("tensor", lambda e, kc=kc, sp=sp, wofs=wofs: e.matmul(
                            psum[0][:, :], w_bf[:, wofs + kc * 128:(wofs + kc * 128) + 128],
                            lnj_bf[:, kc * 2048 + sp * 512: kc * 2048 + (sp + 1) * 512],
                            start=(kc == 0), stop=(kc == NCH - 1)),
                            inc="T" if kc == NCH - 1 else None)
                    wr = P.cnt["T"]
                    P.wait("vector", "T", wr)
                    P.op("vector", lambda e, sp=sp, dst_ofs=dst_ofs, bcol=bcol, sc_=sc_: e.tensor_scalar(
                        kq_bf[:, dst_ofs + sp * 512: dst_ofs + (sp + 1) * 512],
                        psum[0][:, :], bcol, sc_, OP.add, OP.mult), inc="V")
                    banks.release(0, "V")
            for tt in range(NT):
                banks.acquire(2)
                for kc in range(NCH):
                    P.op("tensor", lambda e, kc=kc, tt=tt: e.matmul(
                        psum[2][:, 0:128],
                        lnj_bf[:, kc * 2048 + tt * 128: kc * 2048 + (tt + 1) * 128],
                        w_bf[:, 2048 + kc * 128: 2048 + (kc + 1) * 128],
                        start=(kc == 0), stop=False))
                P.op("tensor", lambda e: e.matmul(
                    psum[2][:, 0:128], ones_row, bv_row, start=False, stop=True), inc="T")
                wr = P.cnt["T"]
                P.wait("vector", "T", wr)
                P.op("vector", lambda e, tt=tt: e.tensor_copy(
                    v_bf[:, tt * 130:(tt + 1) * 130].rearrange("p (h c) -> p h c", h=2)[:, :, 0:64],
                    psum[2][:, 0:128].rearrange("p (h c) -> p h c", h=2)), inc="V")
                banks.release(2, "V")
            P.op("vector", lambda e: e.memset(
                v_bf.rearrange("p (t h c) -> p t h c", t=16, h=2)[:, :, :, 64:65], 1.0), inc="V")
            v_done = P.cnt["V"]
            P.wait("tensor", "V", v_done)
            slot_guard = [0] * 8
            for h2 in range(2):
                hr0 = 64 * h2
                pv_bank = 3 + h2
                sc_bank = 0 + h2
                for sp in range(4):
                    banks.acquire(pv_bank)
                    nkc = 4 * (sp + 1)
                    for kc in range(nkc):
                        is_diag = kc >= 4 * sp
                        jj = kc - 4 * sp
                        q0 = 128 * jj if is_diag else 0
                        qw = 512 - q0
                        slot = (4 + jj) if is_diag else (kc % 2)
                        banks.acquire(sc_bank)
                        P.op("tensor", lambda e, kc=kc, sp=sp, hr0=hr0, q0=q0, qw=qw, b=sc_bank: e.matmul(
                            psum[b][:, 0:qw],
                            kq_bf[hr0:hr0 + 64, kc * 128:(kc + 1) * 128],
                            kq_bf[hr0:hr0 + 64, 2048 + sp * 512 + q0: 2048 + sp * 512 + q0 + qw],
                            start=True, stop=True), inc="T")
                        wr = P.cnt["T"]
                        P.wait("scalar", "T", max(wr, slot_guard[slot]))
                        P.op("scalar", lambda e, slot=slot, qw=qw, b=sc_bank: e.activation(
                            out=er_bf[:, slot * 512: slot * 512 + qw],
                            in_=psum[b][:, 0:qw], func=AF.Exp), inc="Sc")
                        banks.release(sc_bank, "Sc")
                        if not is_diag:
                            P.wait("tensor", "Sc")
                            P.op("tensor", lambda e, kc=kc, h2=h2, pb=pv_bank, slot=slot: e.matmul(
                                psum[pb][0:65, 0:512],
                                v_bf[:, kc * 130 + 65 * h2: kc * 130 + 65 * h2 + 65],
                                er_bf[:, slot * 512: slot * 512 + 512],
                                start=(kc == 0), stop=False), inc="T")
                            slot_guard[slot] = P.cnt["T"]
                    P.wait("vector", "Sc")
                    P.op("vector", lambda e: e.tensor_tensor(
                        er_bf[:, 4 * 512:8 * 512].rearrange("p (s c) -> p s c", s=4)[:, :, 0:128],
                        er_bf[:, 4 * 512:8 * 512].rearrange("p (s c) -> p s c", s=4)[:, :, 0:128],
                        mask_bf.rearrange("p (s c) -> p s c", s=4), OP.mult), inc="V")
                    mk = P.cnt["V"]
                    P.wait("tensor", "V", mk)
                    for jj in range(4):
                        kc = 4 * sp + jj
                        q0 = 128 * jj
                        P.op("tensor", lambda e, jj=jj, kc=kc, q0=q0, h2=h2, pb=pv_bank, sp=sp: e.matmul(
                            psum[pb][0:65, q0:512],
                            v_bf[:, kc * 130 + 65 * h2: kc * 130 + 65 * h2 + 65],
                            er_bf[:, (4 + jj) * 512: (4 + jj) * 512 + (512 - q0)],
                            start=(sp == 0 and jj == 0), stop=(jj == 3)), inc="T")
                        slot_guard[4 + jj] = P.cnt["T"]
                    wr = P.cnt["T"]
                    P.wait("vector", "T", wr)
                    P.op("vector", lambda e, pb=pv_bank: e.reciprocal(
                        recip_row, psum[pb][64:65, 0:512]), inc="V")
                    rp = P.cnt["V"]
                    banks.acquire(5)
                    P.wait("tensor", "V", rp)
                    P.op("tensor", lambda e: e.matmul(
                        psum[5][0:64, 0:512], ones_row[:, 0:64], recip_row,
                        start=True, stop=True), inc="T")
                    wr2 = P.cnt["T"]
                    P.wait("vector", "T", wr2)
                    P.op("vector", lambda e: e.tensor_copy(rb_c, psum[5][0:64, 0:512]), inc="V")
                    banks.release(5, "V")
                    P.op("vector", lambda e, hg=hg, hr0=hr0, sp=sp, pb=pv_bank: e.tensor_tensor(
                        att_bf[hr0:hr0 + 64, hg * 2048 + sp * 512: hg * 2048 + (sp + 1) * 512],
                        psum[pb][0:64, 0:512], rb_c, OP.mult), inc="V")
                    banks.release(pv_bank, "V")
            last_T = P.cnt["T"]
        att_done_c = P.cnt["V"]

        wo_bf0 = sb["w16"][:, :]
        wo_bf1 = sb["er16"][:, :]
        P.wait("gpsimd", "T", last_T)
        P.wait("gpsimd", "Sc")
        P.dma("gpsimd", wo_bf0.rearrange("p (c o) -> p c o", c=NCH),
              dram["wo"][:, 0:512].rearrange("(c p) o -> p c o", p=128))
        P.dma("gpsimd", wo_bf1.rearrange("p (c o) -> p c o", c=NCH),
              dram["wo"][:, 512:1024].rearrange("(c p) o -> p c o", p=128))
        wo_rdy = P.cnt["GD"]
        P.wait("tensor", "GD", wo_rdy)
        P.wait("tensor", "V", att_done_c)
        for tt in range(NT):
            P.wait("gpsimd", "V")
            xrd = P.dma("gpsimd", xt32, xall[j, tt, :, :])
            for osp in range(2):
                banks.acquire(5)
                wo_h = wo_bf0 if osp == 0 else wo_bf1
                for kc in range(NCH):
                    P.op("tensor", lambda e, kc=kc, tt=tt, wo_h=wo_h: e.matmul(
                        psum[5][:, :], att_bf[:, kc * 2048 + tt * 128:(kc * 2048 + tt * 128) + 128],
                        wo_h[:, kc * 512:(kc + 1) * 512],
                        start=(kc == 0), stop=False))
                P.op("tensor", lambda e, osp=osp: e.matmul(
                    psum[5][:, :], ones_row, bias_row[:, osp * 512:(osp + 1) * 512],
                    start=False, stop=True), inc="T")
                wr = P.cnt["T"]
                if osp == 0:
                    P.wait("vector", "GD", xrd)
                P.wait("vector", "T", wr)
                P.op("vector", lambda e, osp=osp: e.tensor_tensor(
                    x1t[:, osp * 512:(osp + 1) * 512], psum[5][:, :],
                    xt32[:, osp * 512:(osp + 1) * 512], OP.add), inc="V")
                banks.release(5, "V")
            P.op("vector", lambda e: e.tensor_reduce(mean_c, x1t, axis=AX.X, op=OP.add), inc="V")
            P.wait("vector", "V")
            P.op("vector", lambda e: e.tensor_scalar(mean_c, mean_c, 1.0 / D, None, OP.mult), inc="V")
            P.wait("vector", "V")
            P.op("vector", lambda e: e.tensor_scalar(
                s2[:, 0:1024], x1t, mean_c, None, OP.subtract))
            P.op("vector", lambda e: e.scalar_tensor_tensor(
                s2[:, 4096:5120], s2[:, 0:1024], 1.0, s2[:, 0:1024], OP.mult, OP.mult,
                accum_out=vsum_c), inc="V")
            P.wait("scalar", "V")
            P.op("scalar", lambda e: e.activation(out=rs_c, in_=vsum_c, func=AF.Sqrt,
                                                  bias=stat[:, 3:4], scale=1.0 / D), inc="Sc")
            P.wait("vector", "Sc")
            P.op("vector", lambda e: e.reciprocal(rs_c, rs_c), inc="V")
            P.wait("vector", "V")
            P.op("vector", lambda e: e.tensor_reduce(
                tmax, s2[:, 0:1024], axis=AX.X, op=OP.max, apply_absolute_value=True), inc="V")
            P.wait("vector", "V")
            P.op("vector", lambda e: e.tensor_tensor(tmax, tmax, rs_c, OP.mult), inc="V")
            P.wait("vector", "V")
            P.op("vector", lambda e: e.tensor_tensor(amax_run, amax_run, tmax, OP.max), inc="V")

    if stage == "amax":
        store_y(P, dram, x_res)
        return

    stat = sb["stat"]
    ind = stat[:, 24:25]
    c4col = stat[:, 25:26]
    mcol = stat[:, 26:27]
    sc_x = stat[:, 27:28]
    inv_x = stat[:, 28:29]
    sc_w = stat[:, 29:30]
    inv_w = stat[:, 30:31]
    inv2 = stat[:, 31:32]
    amax_run = stat[:, 32:33]
    tmax = stat[:, 33:34]

    def vf():
        P.mark("vector", "V")
        P.wait("vector", "V")

    P.op("vector", lambda e: e.memset(ind, 2.0 ** -30))
    vf()
    P.op("vector", lambda e: e.tensor_scalar(ind, ind, 1.0, None, OP.add))
    vf()
    P.op("vector", lambda e: e.tensor_scalar(ind, ind, 1.0, None, OP.subtract))
    vf()
    P.op("vector", lambda e: e.tensor_scalar(ind, ind, 2.0 ** 30, None, OP.mult))
    vf()
    P.op("vector", lambda e: e.tensor_scalar(c4col, ind, 2.0 ** 49, None, OP.mult))
    P.op("vector", lambda e: e.tensor_scalar(c4col, c4col, float(2 ** 20 + 1), None, OP.add))
    vf()
    P.op("vector", lambda e: e.tensor_scalar(tmax, ind, 2.0 ** 20, None, OP.mult))
    vf()
    P.op("vector", lambda e: e.tensor_tensor(c4col, c4col, tmax, OP.subtract))
    vf()
    P.op("vector", lambda e: e.tensor_scalar(mcol, ind, 1.5 * 2.0 ** 43, None, OP.mult))
    P.op("vector", lambda e: e.tensor_scalar(mcol, mcol, 1.5 * 2.0 ** 14, None, OP.add))
    vf()
    P.op("vector", lambda e: e.tensor_scalar(tmax, ind, 1.5 * 2.0 ** 14, None, OP.mult))
    vf()
    P.op("vector", lambda e: e.tensor_tensor(mcol, mcol, tmax, OP.subtract))
    vf()

    for tt in range(NT):
        if tt > 0:
            P.wait("vector", "Sc")
        ln_tile(P, sb, xres_t(tt), ln_tmp, ln_sq)
        P.op("vector", lambda e: e.tensor_reduce(
            tmax, ln_tmp, axis=AX.X, op=OP.max, apply_absolute_value=True), inc="V")
        P.wait("vector", "V")
        P.op("vector", lambda e: e.tensor_tensor(amax_run, amax_run, tmax, OP.max))
        P.wait("tensor", "V")
        transpose_tile(P, banks, sb, ln_tmp, lambda kc, tt=tt: fm_ap(kc, tt))
    ln1_done = P.cnt["Sc"]
    dbg_tap("ln1_fm", fmA[:, :])

    vf()
    arow = rowa[0:1, 1664:1665]
    scrow = rowa[0:1, 1665:1667]
    scalar_finalize(P, banks, sb, amax_run, arow, op=OP.max)
    P.op("vector", lambda e: e.tensor_scalar(arow, arow, 1e-12, None, OP.max))
    vf()
    P.op("vector", lambda e: e.reciprocal(scrow[:, 0:1], arow))
    vf()
    P.op("vector", lambda e: e.tensor_scalar(scrow[:, 0:1], scrow[:, 0:1], 448.0, None, OP.mult))
    P.op("vector", lambda e: e.tensor_scalar(scrow[:, 1:2], arow, 1.0 / 448.0, None, OP.mult))
    vf()
    bcast_p0(P, banks, sb, scrow, stat[:, 27:29], 2)

    wstage = s1[:, 2048:6144]
    P.op("vector", lambda e: e.memset(amax_run, 0.0))
    wf_t = dram["w_ffn0_t"].rearrange("(c p) o -> p c o", p=128)
    P.wait("sync", "GD")
    P.wait("sync", "V")
    P.wait("sync", "T")
    for half in range(2):
        if half > 0:
            P.wait("sync", "V")
        wrd = P.dma("sync", wstage.rearrange("p (c o) -> p c o", c=4),
                    wf_t[:, half * 4:(half + 1) * 4, :])
        P.wait("vector", "WD", wrd)
        P.op("vector", lambda e: e.tensor_reduce(
            tmax, wstage, axis=AX.X, op=OP.max, apply_absolute_value=True), inc="V")
        P.wait("vector", "V")
        P.op("vector", lambda e: e.tensor_tensor(amax_run, amax_run, tmax, OP.max), inc="V")
    vf()
    scalar_finalize(P, banks, sb, amax_run, arow, op=OP.max)
    P.op("vector", lambda e: e.tensor_scalar(arow, arow, 1e-12, None, OP.max))
    vf()
    P.op("vector", lambda e: e.reciprocal(scrow[:, 0:1], arow))
    vf()
    P.op("vector", lambda e: e.tensor_scalar(scrow[:, 0:1], scrow[:, 0:1], 448.0, None, OP.mult))
    P.op("vector", lambda e: e.tensor_scalar(scrow[:, 1:2], arow, 1.0 / 448.0, None, OP.mult))
    vf()
    bcast_p0(P, banks, sb, scrow, stat[:, 29:31], 2)
    P.op("vector", lambda e: e.tensor_tensor(inv2, inv_x, inv_w, OP.mult), inc="V")
    P.wait("vector", "V")

    wq_bf = sb["wqB"][:, :]
    qscr = [s2[:, 4096 + i * 512:4096 + (i + 1) * 512] for i in range(4)]
    yq = s1[:, 0:2048]
    for half in range(2):
        P.wait("sync", "V")
        wrd = P.dma("sync", wstage.rearrange("p (c o) -> p c o", c=4),
                    wf_t[:, half * 4:(half + 1) * 4, :])
        P.wait("vector", "WD", wrd)
        for sub in range(8):
            kc = half * 4 + sub // 2
            off = (sub % 2) * 512
            ysl_w = yq[:, 0:512]
            P.op("vector", lambda e, sub=sub, ysl_w=ysl_w: e.tensor_scalar(
                ysl_w, wstage[:, sub * 512:(sub + 1) * 512], sc_w, None, OP.mult))
            quant_grid_tile(P, sb, ysl_w, wq_bf[:, kc * 1024 + off: kc * 1024 + off + 512],
                            qscr, c4col, mcol)
    wq_done = P.cnt["V"]
    dbg_tap("wq_s2", s2[:, 0:4096])

    brd = P.dma("sync", bias_row, dram["rows_c"][4:5, 0:1024])
    P.wait("vector", "WD", brd)
    P.op("vector", lambda e: e.tensor_tensor(tmax, sc_x, sc_w, OP.mult), inc="V")
    P.wait("vector", "V")
    P.op("vector", lambda e: e.tensor_scalar(bias_row, bias_row, tmax[0:1, 0:1], None, OP.mult), inc="V")
    bias_done = P.cnt["V"]

    xq_bf = sb["xqB"][:, :]
    for sp in range(4):
        for kc in range(NCH):
            ysl_x = s1[:, 6400:6912]
            P.op("vector", lambda e, kc=kc, sp=sp, ysl_x=ysl_x: e.tensor_scalar(
                ysl_x, fm_span(kc, sp * 512, 512), sc_x, None, OP.mult))
            quant_grid_tile(P, sb, ysl_x, xq_bf[:, kc * 512:(kc + 1) * 512],
                            qscr, c4col, mcol)
        xq_done = P.cnt["V"]
        P.wait("tensor", "V", xq_done)
        P.wait("tensor", "Sc", ln1_done)
        for t4 in range(4):
            tt = sp * 4 + t4
            for osp in range(2):
                banks.acquire(5)
                for kc in range(NCH):
                    P.op("tensor", lambda e, kc=kc, t4=t4, osp=osp: e.matmul(
                        psum[5][:, :],
                        xq_bf[:, kc * 512 + t4 * 128: kc * 512 + (t4 + 1) * 128],
                        wq_bf[:, kc * 1024 + osp * 512: kc * 1024 + (osp + 1) * 512],
                        start=(kc == 0), stop=False))
                P.op("tensor", lambda e, osp=osp: e.matmul(
                    psum[5][:, :], ones_row, bias_row[:, osp * 512:(osp + 1) * 512],
                    start=False, stop=True), inc="T")
                wr = P.cnt["T"]
                P.wait("vector", "T", wr)
                P.op("vector", lambda e, tt=tt, osp=osp: e.scalar_tensor_tensor(
                    xres_t(tt)[:, osp * 512:(osp + 1) * 512],
                    psum[5][:, :], inv2,
                    xres_t(tt)[:, osp * 512:(osp + 1) * 512],
                    OP.mult, OP.add), inc="V")
                banks.release(5, "V")

    dbg_tap("statd", sb["stat"][:, :])
    if stage == "x2":
        store_y(P, dram, x_res)
        return

    TC = 256
    NCHK = S // TC
    NPT = DI // 128
    xs_a = lambda pt: fmA[:, pt * TC:(pt + 1) * TC]
    u_a = lambda pt: fmA[:, 4096 + pt * TC: 4096 + (pt + 1) * TC]
    dl_a = lambda pt: fmA[:, 8192 + pt * TC: 8192 + (pt + 1) * TC]
    y_a = lambda pt: fmA[:, 12288 + pt * TC: 12288 + (pt + 1) * TC]
    wstg1 = fmA[:, 4096:12288]
    ln2fm = lambda kc: s1[:, kc * TC:(kc + 1) * TC]
    halo = s1[:, 4096:4144]
    negA = s1[:, 5424:5680]
    convw = s1[:, 5680:5744]
    convb = s1[:, 5744:5760]
    bdt_c = s1[:, 5760:5776]
    dpar_c = s1[:, 5776:5792]
    carry = s1[:, 6144:6400]
    a_t = s2[:, 0:256]
    b_t = s2[:, 256:512]
    h_t = s2[:, 512:768]
    xdbl = s2[:, 768:1024]
    e_scr = s2[:, 1024:2048]
    ln2tm = s2[:, 2048:4096]
    t_tm = s2[:, 4096:5120]
    ln_scr = s2[:, 5120:6144]
    bc_b = s1[32:48, 6400:6656]
    bc_c = s1[32:48, 6656:6912]
    onehot16 = rowa[32:48, :]

    P.wait("sync", "V")
    P.wait("sync", "T")
    P.dma("sync", s1[:, 5424:5680], dram["a_log_c"][:, :])
    P.dma("sync", convw, dram["conv_w_c"][:, :])
    P.dma("sync", convb, dram["conv_b_c"][:, :])
    P.dma("sync", bdt_c, dram["b_dt_c"][:, :])
    P.dma("sync", dpar_c, dram["d_par_c"][:, :])
    P.dma("sync", wb[:, 0:1536].rearrange("p (c r) -> p c r", c=16),
          dram["w_x_t"].rearrange("(c p) r -> p c r", p=128))
    P.dma("sync", wb[0:DTR, 1536:3584], dram["w_dt_t"][:, :])
    ld_e = P.cnt["WD"]
    P.wait("scalar", "WD", ld_e)
    P.op("scalar", lambda e: e.activation(out=negA, in_=negA, func=AF.Exp, scale=1.0), inc="Sc")
    P.wait("vector", "Sc")
    P.op("vector", lambda e: e.tensor_scalar(negA, negA, -1.0, None, OP.mult), inc="V")
    P.op("vector", lambda e: e.memset(halo, 0.0))
    P.op("vector", lambda e: e.memset(carry, 0.0), inc="V")

    win_t = dram["w_in_t"].rearrange("(c p) e -> p c e", p=128)
    wout_t = dram["w_out_t"].rearrange("(c p) o -> p c o", p=128)

    for ch in range(NCHK):
        P.wait("vector", "T")
        for t2 in range(2):
            tile = 2 * ch + t2
            ln_tile(P, sb, xres_t(tile), ln2tm[:, t2 * 1024:(t2 + 1) * 1024], ln_scr)
            P.wait("tensor", "V")
            transpose_tile(P, banks, sb, ln2tm[:, t2 * 1024:(t2 + 1) * 1024],
                           lambda kc, t2=t2: ln2fm(kc)[:, t2 * 128:(t2 + 1) * 128],
                           bank_pair=(6, 7), evac_eng="vector")
        ln2_done = P.cnt["V"]

        for half in range(2):
            P.wait("sync", "V")
            wrd = P.dma("sync", wstg1, win_t[:, :, half * 1024:(half + 1) * 1024])
            P.wait("tensor", "WD", wrd)
            P.wait("tensor", "V", ln2_done)
            for es8 in range(8):
                es = half * 8 + es8
                banks.acquire(4)
                for kc in range(NCH):
                    P.op("tensor", lambda e, kc=kc, es8=es8: e.matmul(
                        psum[4][:, 0:TC], wstg1[:, kc * 1024 + es8 * 128: kc * 1024 + (es8 + 1) * 128],
                        ln2fm(kc), start=(kc == 0), stop=(kc == NCH - 1)),
                        inc="T" if kc == NCH - 1 else None)
                wr = P.cnt["T"]
                P.wait("vector", "T", wr)
                P.op("vector", lambda e, es=es: e.tensor_copy(xs_a(es), psum[4][:, 0:TC]), inc="V")
                banks.release(4, "V")

        for pt in range(NPT):
            if pt > 0:
                P.wait("vector", "Sc")
            acc = s2[:, 1024:1280]
            acc2 = s2[:, 1280:1536]
            for k in range(4):
                def shifted(e_unused, pt=pt, k=k):
                    pass
                if k < 3:
                    nh = 3 - k
                    srcs = [(halo[:, pt * 3 + (3 - nh): pt * 3 + 3], 0, nh),
                            (xs_a(pt)[:, 0:TC - nh], nh, TC - nh)]
                else:
                    srcs = [(xs_a(pt), 0, TC)]
                wcol = convw[:, pt * 4 + k: pt * 4 + k + 1]
                for (src_ap, off, w_) in srcs:
                    if k == 0 and off == 0:
                        P.op("vector", lambda e, src_ap=src_ap, off=off, w_=w_, wcol=wcol, acc=acc: e.tensor_scalar(
                            acc[:, off:off + w_], src_ap, wcol, None, OP.mult))
                    elif k == 0:
                        P.op("vector", lambda e, src_ap=src_ap, off=off, w_=w_, wcol=wcol, acc=acc: e.tensor_scalar(
                            acc[:, off:off + w_], src_ap, wcol, None, OP.mult))
                    else:
                        P.op("vector", lambda e, src_ap=src_ap, off=off, w_=w_, wcol=wcol, acc=acc, acc2=acc2: e.scalar_tensor_tensor(
                            acc2[:, off:off + w_], src_ap, wcol, acc[:, off:off + w_],
                            OP.mult, OP.add))
                if k > 0:
                    acc, acc2 = acc2, acc
            P.op("vector", lambda e, acc=acc: e.engine_nop(), inc="V")
            P.wait("scalar", "V")
            P.op("scalar", lambda e, pt=pt, acc=acc: e.activation(
                out=u_a(pt), in_=acc, func=AF.Silu,
                bias=convb[:, pt:pt + 1], scale=1.0), inc="Sc")
        P.op("vector", lambda e: e.tensor_copy(
            halo.rearrange("p (t c) -> p t c", t=16),
            fmA[:, 0:4096].rearrange("p (t c) -> p t c", t=16)[:, :, TC - 3:TC]), inc="V")
        u_done = P.cnt["Sc"]

        P.wait("tensor", "Sc", u_done)
        P.wait("tensor", "WD", ld_e)
        banks.acquire(5)
        for ec in range(NPT):
            P.op("tensor", lambda e, ec=ec: e.matmul(
                psum[5][0:96, 0:TC], wb[:, ec * 96:(ec + 1) * 96], u_a(ec),
                start=(ec == 0), stop=(ec == NPT - 1)), inc="T" if ec == NPT - 1 else None)
        wr = P.cnt["T"]
        P.wait("vector", "T", wr)
        P.op("vector", lambda e: e.tensor_copy(xdbl[0:96, :], psum[5][0:96, 0:TC]), inc="V")
        banks.release(5, "V")
        P.wait("gpsimd", "V")
        P.dma("gpsimd", bc_b, xdbl[64:80, :], inc="GD")
        P.dma("gpsimd", bc_c, xdbl[80:96, :], inc="GD")
        bc_done = P.cnt["GD"]

        for pt in range(NPT):
            banks.acquire(5)
            wsl = wb[0:DTR, 1536 + pt * 128: 1536 + (pt + 1) * 128]
            P.op("tensor", lambda e, wsl=wsl: e.matmul(
                psum[5][:, 0:TC], wsl, xdbl[0:64, :], start=True, stop=True), inc="T")
            wr = P.cnt["T"]
            P.wait("scalar", "T", wr)
            P.op("scalar", lambda e, pt=pt: e.activation(
                out=e_scr[:, 0:TC], in_=psum[5][:, 0:TC], func=AF.Exp,
                bias=bdt_c[:, pt:pt + 1], scale=1.0), inc="Sc")
            banks.release(5, "Sc")
            P.op("scalar", lambda e, pt=pt: e.activation(
                out=dl_a(pt), in_=e_scr[:, 0:TC], func=AF.Ln, bias=1.0, scale=1.0), inc="Sc")
        dl_done = P.cnt["Sc"]

        P.wait("vector", "Sc", dl_done)
        for pt in range(NPT):
            P.op("vector", lambda e, pt=pt: e.tensor_tensor(xs_a(pt), dl_a(pt), u_a(pt), OP.mult))
        P.op("vector", lambda e: e.engine_nop(), inc="V")
        du_done = P.cnt["V"]

        prev_scan = [P.cnt["V"]]
        P.wait("tensor", "GD", bc_done)
        for n in range(DSTATE):
            banks.acquire(2)
            P.op("tensor", lambda e, n=n: e.matmul(
                psum[2][:, 0:TC], onehot16[:, n * 128:(n + 1) * 128], bc_b,
                start=True, stop=True), inc="T")
            banks.acquire(3)
            P.op("tensor", lambda e, n=n: e.matmul(
                psum[3][:, 0:TC], onehot16[:, n * 128:(n + 1) * 128], bc_c,
                start=True, stop=True), inc="T")
            wr = P.cnt["T"]
            P.wait("vector", "T", wr)
            P.wait("scalar", "T", wr)
            if n == 0:
                P.wait("vector", "V", du_done)
                P.wait("scalar", "V", du_done)
            for pt in range(NPT):
                nacol = negA[:, pt * 16 + n: pt * 16 + n + 1]
                ccol = carry[:, pt * 16 + n: pt * 16 + n + 1]
                P.wait("scalar", "V", prev_scan[0])
                P.op("scalar", lambda e, pt=pt, nacol=nacol: e.activation(
                    out=a_t, in_=dl_a(pt), func=AF.Exp, scale=nacol), inc="Sc")
                sc_a = P.cnt["Sc"]
                P.op("vector", lambda e, pt=pt: e.tensor_tensor(
                    b_t, xs_a(pt), psum[2][:, 0:TC], OP.mult), inc="V")
                P.wait("vector", "Sc", sc_a)
                P.op("vector", lambda e, ccol=ccol: e.tensor_tensor_scan(
                    h_t, a_t, b_t, ccol, OP.mult, OP.add), inc="V")
                prev_scan[0] = P.cnt["V"]
                P.wait("vector", "V")
                P.op("vector", lambda e, ccol=ccol: e.tensor_copy(ccol, h_t[:, TC - 1:TC]))
                P.op("vector", lambda e: e.tensor_tensor(b_t, h_t, psum[3][:, 0:TC], OP.mult))
                if n == 0:
                    P.op("vector", lambda e, pt=pt: e.tensor_copy(y_a(pt), b_t), inc="V")
                else:
                    P.op("vector", lambda e, pt=pt: e.tensor_tensor(
                        y_a(pt), y_a(pt), b_t, OP.add), inc="V")
            banks.release(2, "V")
            banks.release(3, "V")
        scan_done = P.cnt["V"]

        for pt in range(NPT):
            P.op("vector", lambda e, pt=pt: e.scalar_tensor_tensor(
                y_a(pt), u_a(pt), dpar_c[:, pt:pt + 1], y_a(pt), OP.mult, OP.add), inc="V")
        for q in range(4):
            stg = fmA[:, 0:4096] if q % 2 == 0 else fmA[:, 8192:12288]
            P.wait("sync", "V")
            P.wait("sync", "T")
            wrd = P.dma("sync", stg.rearrange("p (c e) -> p c e", c=8),
                        win_t[:, :, 2048 + q * 512: 2048 + (q + 1) * 512])
            P.wait("tensor", "WD", wrd)
            for es4 in range(4):
                pt = q * 4 + es4
                banks.acquire(4)
                for kc in range(NCH):
                    P.op("tensor", lambda e, kc=kc, es4=es4, stg=stg: e.matmul(
                        psum[4][:, 0:TC], stg[:, kc * 512 + es4 * 128: kc * 512 + (es4 + 1) * 128],
                        ln2fm(kc), start=(kc == 0), stop=(kc == NCH - 1)),
                        inc="T" if kc == NCH - 1 else None)
                wr = P.cnt["T"]
                P.wait("scalar", "T", wr)
                P.op("scalar", lambda e: e.activation(
                    out=e_scr[:, 0:TC], in_=psum[4][:, 0:TC], func=AF.Silu), inc="Sc")
                banks.release(4, "Sc")
                P.wait("vector", "Sc")
                P.op("vector", lambda e, pt=pt: e.tensor_tensor(
                    y_a(pt), y_a(pt), e_scr[:, 0:TC], OP.mult), inc="V")
        y_done = P.cnt["V"]

        lnwb = s1[:, 0:2048]
        P.wait("sync", "V", y_done)
        P.wait("sync", "T")
        for wh in range(2):
            if wh == 1:
                P.wait("sync", "T")
            wrd = P.dma("sync", fmA[:, 0:8192].rearrange("p (c o) -> p c o", c=8),
                        wout_t[:, wh * 8:(wh + 1) * 8, :])
            P.wait("tensor", "WD", wrd)
            for t2 in range(2):
                for osp in range(2):
                    bank = 2 + t2 * 2 + osp
                    if wh == 0:
                        banks.acquire(bank)
                    for e8 in range(8):
                        ec = wh * 8 + e8
                        P.op("tensor", lambda e, ec=ec, e8=e8, t2=t2, osp=osp, bank=bank: e.matmul(
                            psum[bank][:, :], y_a(ec)[:, t2 * 128:(t2 + 1) * 128],
                            fmA[:, e8 * 1024 + osp * 512: e8 * 1024 + osp * 512 + 512],
                            start=(ec == 0), stop=(ec == NPT - 1)),
                            inc="T" if (e8 == 7) else None)
        wo_mm = P.cnt["T"]
        brd = P.dma("sync", lnwb, dram["ln_wb"][:, :])
        for t2 in range(2):
            P.wait("vector", "T", wo_mm)
            for osp in range(2):
                bank = 2 + t2 * 2 + osp
                P.op("vector", lambda e, t2=t2, osp=osp, bank=bank: e.tensor_tensor(
                    t_tm[:, osp * 512:(osp + 1) * 512], psum[bank][:, :],
                    ln2tm[:, t2 * 1024 + osp * 512: t2 * 1024 + (osp + 1) * 512],
                    OP.add), inc="V")
                banks.release(bank, "V")
            tile = 2 * ch + t2
            ln_tile(P, sb, t_tm, ln_scr[:, 0:1024], s2[:, 1024:2048])
            if t2 == 0:
                P.wait("vector", "WD", brd)
            P.op("vector", lambda e: e.tensor_tensor(
                ln_scr[:, 0:1024], ln_scr[:, 0:1024], lnwb[:, 0:1024], OP.mult))
            P.op("vector", lambda e: e.tensor_tensor(
                ln_scr[:, 0:1024], ln_scr[:, 0:1024], lnwb[:, 1024:2048], OP.add))
            P.op("vector", lambda e, tile=tile: e.tensor_tensor(
                xres_t(tile), xres_t(tile), ln_scr[:, 0:1024], OP.add), inc="V")

    if stage == "x3":
        store_y(P, dram, x_res)
        return

    lnH = sb["lnH"][:, :]
    lnL = sb["lnL"][:, :]
    upH = sb["upH"][:, :]
    upL = sb["upL"][:, :]
    dnH = sb["dnH"][:, :]
    dnL = sb["dnL"][:, :]
    wrtH = sb["wrtH"][:, :]
    wrtL = sb["wrtL"][:, :]
    stg = wb[:, 2176:3200]
    a16 = sb["act16"][:, :]
    wrt_sb = s1[:, 6400:6528]
    probs = s1[:, 6528:6544]
    rank_a = s1[:, 6544:6560]
    rank_b = s1[:, 6560:6576]
    gate_tm = s1[:, 6576:6592]
    gpark = s1[0:16, 6144:6272]
    gfm_all = s1[32:48, 0:2048]
    act32 = s1[:, 4096:6144]
    mask_buf = cst[:, 896:912]
    ln3tm = s2[:, 2048:4096]
    den = sb["stat"][:, 34:35]
    NSLAB = 2

    P.wait("sync", "V")
    P.wait("sync", "T")
    wrd = P.dma("sync", wrt_sb.rearrange("p (c r) -> p c r", c=NCH),
                dram["wrt_t"].rearrange("(c p) r -> p c r", p=128))
    P.wait("vector", "WD", wrd)
    P.wait("vector", "T")
    P.op("vector", lambda e: e.tensor_copy(wrtH, wrt_sb))
    P.op("vector", lambda e: e.tensor_tensor(wrtL, wrt_sb, wrtH, OP.subtract), inc="V")

    for tile in range(NT):
        P.wait("vector", "T")
        ln_tile(P, sb, xres_t(tile), ln3tm[:, 0:1024], s2[:, 1024:2048])
        P.wait("tensor", "V")
        transpose_tile(P, banks, sb, ln3tm[:, 0:1024],
                       lambda kc, tile=tile: lnH[:, kc * 2048 + tile * 128: kc * 2048 + (tile + 1) * 128],
                       bank_pair=(6, 7), evac_eng="vector",
                       dst_lo_fn=lambda kc, tile=tile: lnL[:, kc * 2048 + tile * 128: kc * 2048 + (tile + 1) * 128])
        ln_done_t = P.cnt["V"]
        banks.acquire(7)
        P.wait("tensor", "V", ln_done_t)
        nmm = 0
        for (aa, bb) in ((lnH, wrtH), (lnH, wrtL), (lnL, wrtH)):
            for kc in range(NCH):
                nmm += 1
                P.op("tensor", lambda e, aa=aa, bb=bb, kc=kc, tile=tile,
                     st=(nmm == 1), sp_=(nmm == 24): e.matmul(
                    psum[7][:, 0:NEXP],
                    aa[:, kc * 2048 + tile * 128: kc * 2048 + (tile + 1) * 128],
                    bb[:, kc * 16:(kc + 1) * 16],
                    start=st, stop=sp_), inc="T" if nmm == 24 else None)
        wr = P.cnt["T"]
        P.wait("scalar", "T", wr)
        P.op("scalar", lambda e: e.activation(
            out=probs, in_=psum[7][:, 0:NEXP], func=AF.Exp, accum_out=den), inc="Sc")
        banks.release(7, "Sc")
        P.wait("vector", "Sc")
        P.op("vector", lambda e: e.reciprocal(den, den), inc="V")
        P.wait("vector", "V")
        P.op("vector", lambda e: e.tensor_scalar(probs, probs, den, None, OP.mult), inc="V")
        P.wait("vector", "V")
        P.op("vector", lambda e: e.tensor_scalar(
            rank_a, probs, probs[:, 0:1], None, OP.is_lt))
        cur, nxt = rank_a, rank_b
        for j in range(1, NEXP):
            P.op("vector", lambda e, j=j, cur=cur, nxt=nxt: e.scalar_tensor_tensor(
                nxt, probs, probs[:, j:j + 1], cur, OP.is_lt, OP.add))
            cur, nxt = nxt, cur
        P.op("vector", lambda e: e.engine_nop(), inc="V")
        P.wait("vector", "V")
        P.op("vector", lambda e, cur=cur: e.tensor_scalar(
            mask_buf, cur, 3.5, None, OP.is_lt), inc="V")
        P.wait("vector", "V")
        P.op("vector", lambda e: e.tensor_tensor(
            gate_tm, probs, mask_buf, OP.mult), inc="V")
        gt = P.cnt["V"]
        banks.acquire(7)
        P.wait("tensor", "V", gt)
        P.op("tensor", lambda e: e.matmul(
            psum[7][0:16, 0:128], gate_tm, cst[:, 0:128],
            is_transpose=True, start=True, stop=True), inc="T")
        wr = P.cnt["T"]
        P.wait("vector", "T", wr)
        P.wait("vector", "GD")
        P.op("vector", lambda e: e.tensor_copy(gpark, psum[7][0:16, 0:128]), inc="V")
        banks.release(7, "V")
        gv = P.cnt["V"]
        P.wait("gpsimd", "V", gv)
        P.dma("gpsimd", gfm_all[:, tile * 128:(tile + 1) * 128], gpark, inc="GD")
    gate_rdy = P.cnt["GD"]
    ln_rdy = P.cnt["V"]
    dbg_tap("lnH0", lnH[:, 0:2048])
    dbg_tap("lnL0", lnL[:, 0:2048])
    P.wait("gpsimd", "GD")
    dbg_tap("gfm", gfm_all)

    roup = dram["roup_t"].rearrange("(c p) e -> p c e", p=128)
    shup = dram["shup_t"].rearrange("(c p) e -> p c e", p=128)
    rodn = dram["rodn_t"].rearrange("(r p) o -> r p o", p=128)
    shdn = dram["shdn_t"].rearrange("(r p) o -> r p o", p=128)
    NEX = NEXP + NSH

    def slab(g):
        s = (g % NSLAB) * 512
        return stg[:, s:s + 512]

    def slab_src(ex, k):
        if k < 8:
            if ex < NEXP:
                return roup[:, k:k + 1, ex * EDIM:(ex + 1) * EDIM].rearrange("p c e -> p (c e)")
            return shup[:, k:k + 1, (ex - NEXP) * EDIM:(ex - NEXP + 1) * EDIM].rearrange("p c e -> p (c e)")
        r, osp = (k - 8) // 2, (k - 8) % 2
        rr = (ex * 4 + r) if ex < NEXP else ((ex - NEXP) * 4 + r)
        srcd = rodn if ex < NEXP else shdn
        return srcd[rr:rr + 1, :, osp * 512:(osp + 1) * 512].rearrange("r p o -> p (r o)")

    def slab_dst(k):
        if k < 8:
            return (upH[:, k * 512:(k + 1) * 512], upL[:, k * 512:(k + 1) * 512])
        r, osp = (k - 8) // 2, (k - 8) % 2
        o0 = r * 1024 + osp * 512
        return (dnH[:, o0:o0 + 512], dnL[:, o0:o0 + 512])

    convV = {}

    def emit_stage(ex, ks, tgate):
        for k in ks:
            g = ex * 16 + k
            if g - NSLAB in convV:
                P.wait("sync", "V", convV[g - NSLAB])
            wd = P.dma("sync", slab(g), slab_src(ex, k))
            P.wait("vector", "WD", wd)
            if tgate:
                P.wait("vector", "T", tgate)
            hi, lo = slab_dst(k)
            P.op("vector", lambda e, hi=hi, g=g: e.tensor_copy(hi, slab(g)))
            P.op("vector", lambda e, hi=hi, lo=lo, g=g: e.tensor_tensor(
                lo, slab(g), hi, OP.subtract), inc="V")
            convV[g] = P.cnt["V"]

    emit_stage(0, range(16), P.cnt["T"])
    dbg_tap("upH0", upH)
    dbg_tap("dnH0", dnH)
    dbg_tap("stg0", stg)
    up_conv_last = {0: convV[7]}
    dn_conv_last = {0: convV[15]}
    act_guard = [0] * 4
    act_vcnt = [0] * 4
    P.wait("tensor", "GD", gate_rdy)
    for ex in range(NEX):
        routed = ex < NEXP
        up_T = 0
        for sp in range(4):
            if routed:
                banks.acquire(6)
                P.op("tensor", lambda e, ex=ex, sp=sp: e.matmul(
                    psum[6][:, 0:512], rowa[32:48, ex * 128:(ex + 1) * 128],
                    gfm_all[:, sp * 512:(sp + 1) * 512],
                    start=True, stop=True), inc="T")
                gb = P.cnt["T"]
            for et in range(4):
                ub = 4 + (et % 2)
                banks.acquire(ub)
                if sp == 0 and et == 0:
                    P.wait("tensor", "V", max(ln_rdy, up_conv_last[ex]))
                nmm = 0
                for (aa, bb) in ((upH, lnH), (upH, lnL), (upL, lnH)):
                    for kc in range(NCH):
                        nmm += 1
                        P.op("tensor", lambda e, aa=aa, bb=bb, kc=kc, et=et, sp=sp, ub=ub,
                             st=(nmm == 1), sp_=(nmm == 24): e.matmul(
                            psum[ub][:, 0:512],
                            aa[:, kc * 512 + et * 128: kc * 512 + et * 128 + 128],
                            bb[:, kc * 2048 + sp * 512: kc * 2048 + (sp + 1) * 512],
                            start=st, stop=sp_), inc="T" if nmm == 24 else None)
                if sp == 3 and et == 3:
                    up_T = P.cnt["T"]
                wr = P.cnt["T"]
                P.wait("scalar", "T", wr)
                P.wait("scalar", "V", act_vcnt[et])
                P.op("scalar", lambda e, et=et, ub=ub: e.activation(
                    out=act32[:, et * 512:(et + 1) * 512], in_=psum[ub][:, 0:512],
                    func=AF.Gelu), inc="Sc")
                banks.release(ub, "Sc")
                sc = P.cnt["Sc"]
                P.wait("vector", "Sc", sc)
                P.wait("vector", "T", act_guard[et])
                a32 = act32[:, et * 512:(et + 1) * 512]
                ahi = a16[:, et * 512:(et + 1) * 512]
                alo = a16[:, 2048 + et * 512: 2048 + (et + 1) * 512]
                if routed:
                    P.wait("vector", "T", gb)
                    P.op("vector", lambda e, a32=a32: e.tensor_tensor(
                        a32, a32, psum[6][:, 0:512], OP.mult))
                P.op("vector", lambda e, a32=a32, ahi=ahi: e.tensor_copy(ahi, a32))
                P.op("vector", lambda e, a32=a32, ahi=ahi, alo=alo: e.tensor_tensor(
                    alo, a32, ahi, OP.subtract), inc="V")
                act_vcnt[et] = P.cnt["V"]
            if routed:
                banks.release(6, "V")
            if ex == 0 and sp == 0:
                dbg_tap("act0", a16[:, 0:2048])
            if sp == 3 and ex + 1 < NEX:
                emit_stage(ex + 1, range(8), up_T)
                up_conv_last[ex + 1] = convV[(ex + 1) * 16 + 7]
            P.wait("tensor", "V", act_vcnt[3])
            if sp == 0:
                P.wait("tensor", "V", dn_conv_last[ex])
            for mt in range(4):
                tile = sp * 4 + mt
                for osp in range(2):
                    db = (mt * 2 + osp) % 4
                    banks.acquire(db)
                    nmm = 0
                    for (aa, bb) in (("h", "h"), ("h", "l"), ("l", "h")):
                        for et in range(4):
                            nmm += 1
                            ao = (0 if aa == "h" else 2048) + et * 512 + mt * 128
                            asrc = a16[:, ao:ao + 128]
                            bsrc = (dnH if bb == "h" else dnL)[
                                :, et * 1024 + osp * 512: et * 1024 + osp * 512 + 512]
                            P.op("tensor", lambda e, asrc=asrc, bsrc=bsrc, db=db,
                                 st=(nmm == 1), sp_=(nmm == 12): e.matmul(
                                psum[db][:, :], asrc, bsrc, start=st, stop=sp_),
                                inc="T" if nmm == 12 else None)
                    wr = P.cnt["T"]
                    P.wait("vector", "T", wr)
                    P.op("vector", lambda e, tile=tile, osp=osp, db=db: e.tensor_tensor(
                        xres_t(tile)[:, osp * 512:(osp + 1) * 512],
                        xres_t(tile)[:, osp * 512:(osp + 1) * 512],
                        psum[db][:, :], OP.add), inc="V")
                    banks.release(db, "V")
            for et in range(4):
                act_guard[et] = P.cnt["T"]
        if ex + 1 < NEX:
            emit_stage(ex + 1, range(8, 16), P.cnt["T"])
            dn_conv_last[ex + 1] = convV[(ex + 1) * 16 + 15]

    store_y(P, dram, x_res)


def store_y(P, dram, x_res):
    for e_ in ("V", "T", "Sc", "WD"):
        P.wait("gpsimd", e_)
    yo = dram["y"].rearrange("(t p) d -> t p d", p=128)
    for g in range(4):
        P.dma("gpsimd", yo[g * 4:(g + 1) * 4, :, :].rearrange("t p d -> p t d"),
              x_res[:, g * 4096:(g + 1) * 4096].rearrange("p (t d) -> p t d", t=4), inc="St")
    P.wait("gpsimd", "St")



def _consts_array(ins):
    c = np.zeros((128, 1024), np.float32)
    c[:, 0:128] = np.eye(128, dtype=np.float32)
    tri = (np.arange(128)[None, :] >= np.arange(128)[:, None]).astype(np.float32)
    c[:, 128:640] = np.tile(tri, (1, 4))
    return c


def _rows_array(ins):
    r = np.zeros((48, 2048), np.float32)
    for k in range(16):
        r[32 + k, k * 128:(k + 1) * 128] = 1.0
    r[0, 0:128] = 1.0
    r[0, 128:1152] = np.asarray(ins["L0_attn_bo"], np.float32)
    r[1, 0:1024] = np.asarray(ins["L0_attn_bq"], np.float32)
    r[2, 0:1024] = np.asarray(ins["L0_attn_bk"], np.float32)
    r[3, 0:1024] = np.asarray(ins["L0_attn_bv"], np.float32)
    r[4, 0:1024] = np.asarray(ins["L0_ffn_bias"], np.float32)
    return r


def _make_in_maps(ins):
    x = np.ascontiguousarray(np.asarray(ins["L0_attn_x"], np.float32))

    def T(a):
        return np.ascontiguousarray(np.asarray(a, np.float32).T)

    def colt(v, nch):
        return np.ascontiguousarray(np.asarray(v, np.float32).reshape(nch, 128).T)

    shup = np.asarray(ins["L1_ffn_shared_up"], np.float32)
    shdn = np.asarray(ins["L1_ffn_shared_down"], np.float32)
    roup = np.asarray(ins["L1_ffn_routed_up"], np.float32)
    rodn = np.asarray(ins["L1_ffn_routed_down"], np.float32)

    common = {
        "x_all": x,
        "wq": np.ascontiguousarray(np.asarray(ins["L0_attn_wq"], np.float32)),
        "wk": np.ascontiguousarray(np.asarray(ins["L0_attn_wk"], np.float32)),
        "wv": np.ascontiguousarray(np.asarray(ins["L0_attn_wv"], np.float32)),
        "wo": np.ascontiguousarray(np.asarray(ins["L0_attn_wo"], np.float32)),
        "bq_c": colt(ins["L0_attn_bq"], NCH),
        "bk_c": colt(ins["L0_attn_bk"], NCH),
        "rows_c": _rows_array(ins),
        "w_ffn0_t": T(ins["L0_ffn_weight"]),
        "w_in_t": T(ins["L1_attn_W_in"]),
        "conv_w_c": np.ascontiguousarray(
            np.asarray(ins["L1_attn_conv_w"], np.float32)[:, 0, :]
            .reshape(DI // 128, 128, 4).transpose(1, 0, 2).reshape(128, -1)),
        "conv_b_c": colt(ins["L1_attn_conv_b"], DI // 128),
        "w_x_t": T(ins["L1_attn_W_x"]),
        "w_dt_t": T(ins["L1_attn_W_dt"]),
        "b_dt_c": colt(ins["L1_attn_b_dt"], DI // 128),
        "a_log_c": np.ascontiguousarray(
            np.asarray(ins["L1_attn_A_log"], np.float32)
            .reshape(DI // 128, 128, DSTATE).transpose(1, 0, 2).reshape(128, -1)),
        "d_par_c": colt(ins["L1_attn_D_param"], DI // 128),
        "w_out_t": T(ins["L1_attn_W_out"]),
        "ln_wb": np.ascontiguousarray(np.concatenate([
            np.tile(np.asarray(ins["L1_attn_ln_w"], np.float32)[None, :], (128, 1)),
            np.tile(np.asarray(ins["L1_attn_ln_b"], np.float32)[None, :], (128, 1))], axis=1)),
        "shup_t": np.ascontiguousarray(shup.reshape(NSH * EDIM, D).T),
        "shdn_t": np.ascontiguousarray(shdn.transpose(0, 2, 1).reshape(NSH * EDIM, D)),
        "roup_t": np.ascontiguousarray(roup.reshape(NEXP * EDIM, D).T),
        "rodn_t": np.ascontiguousarray(rodn.transpose(0, 2, 1).reshape(NEXP * EDIM, D)),
        "wrt_t": T(ins["L1_ffn_router_w"]),
        "consts": _consts_array(ins),
    }
    maps = []
    for c in range(8):
        m = dict(common)
        m["x_own"] = np.ascontiguousarray(x[c])
        maps.append(m)
    return maps


def kernel(**inputs):
    ins = {k: np.asarray(v) for k, v in inputs.items()}
    nc, _ = build_program()
    res = run_bass_kernel_spmd(nc, _make_in_maps(ins), core_ids=list(range(8)))
    return np.stack([np.asarray(res.results[i]["y"]) for i in range(8)], axis=0)


# version 04 (final)
# speedup vs baseline: 1.1900x; 1.0655x over previous
import contextlib

import numpy as np

import concourse.bass as bass
import concourse.mybir as mybir
from concourse.bass_utils import run_bass_kernel_spmd

F32 = mybir.dt.float32
BF16 = mybir.dt.bfloat16
F16 = mybir.dt.float16
OP = mybir.AluOpType
AF = mybir.ActivationFunctionType
AX = mybir.AxisListType

B, S, D = 8, 2048, 1024
H, DK = 16, 64
NT = S // 128
NCH = D // 128
DI = 2048
DSTATE = 16
DTR = 64
NEXP, EDIM = 16, 512
NSH = 2
EPS = 1e-5

ENGINES = ("sync", "tensor", "vector", "scalar", "gpsimd")


class Prog:
    def __init__(self, nc, sems):
        self.nc = nc
        self.sems = sems
        self.ops = {e: [] for e in ENGINES}
        self.cnt = {name: 0 for name in sems}

    def op(self, eng, fn, inc=None, n=1):
        if inc is None:
            self.ops[eng].append(fn)
        else:
            sem = self.sems[inc]

            def g(e, fn=fn, sem=sem, n=n):
                fn(e).then_inc(sem, n)

            self.ops[eng].append(g)
            self.cnt[inc] += n
        return self.cnt[inc] if inc else None

    def wait(self, eng, sem, val=None):
        v = self.cnt[sem] if val is None else val
        s = self.sems[sem]
        self.ops[eng].append(lambda e, s=s, v=v: e.wait_ge(s, v))

    def dma(self, q, out, in_, inc=None):
        inc = inc or {"sync": "WD", "gpsimd": "GD", "scalar": "SD"}[q]
        self.op(q, lambda e, out=out, in_=in_: e.dma_start(out=out, in_=in_),
                inc=inc, n=16)
        return self.cnt[inc]

    def mark(self, eng, sem):
        if eng in ("vector", "gpsimd"):
            self.op(eng, lambda e: e.engine_nop(), inc=sem)
        else:
            self.op(eng, lambda e: e.nop(), inc=sem)
        return self.cnt[sem]


class Banks:

    def __init__(self, P):
        self.P = P
        self.state = [None] * 8

    def acquire(self, i):
        st = self.state[i]
        if st is not None:
            self.P.wait("tensor", st[0], st[1])

    def release(self, i, sem):
        self.state[i] = (sem, self.P.cnt[sem])



def build_program(debug_taps=(), stage=None):
    nc = bass.Bass("TRN2", target_bir_lowering=False, detect_race_conditions=False)
    dram = {}

    def din(name, shape, dt=F32):
        dram[name] = nc.dram_tensor(name, list(shape), dt, kind="ExternalInput").ap()

    def dout(name, shape, dt=F32):
        dram[name] = nc.dram_tensor(name, list(shape), dt, kind="ExternalOutput").ap()

    din("x_own", [S, D])
    din("x_all", [B - 1, S, D])
    din("wq", [D, D]); din("wk", [D, D]); din("wv", [D, D]); din("wo", [D, D])
    din("bq_c", [128, NCH]); din("bk_c", [128, NCH])
    din("rows_c", [48, 2048])
    din("w_ffn0_t", [D, D])
    din("w_in_t", [D, 2 * DI])
    din("conv_w_c", [128, (DI // 128) * 4])
    din("conv_b_c", [128, DI // 128])
    din("w_x_t", [DI, 96])
    din("w_dt_t", [DTR, DI])
    din("b_dt_c", [128, DI // 128])
    din("a_log_c", [128, (DI // 128) * DSTATE])
    din("d_par_c", [128, DI // 128])
    din("w_out_t", [DI, D])
    din("ln_wb", [128, 2 * D])
    din("shup_t", [D, NSH * EDIM]); din("shdn_t", [NSH * EDIM, D])
    din("roup_t", [D, NEXP * EDIM]); din("rodn_t", [NEXP * EDIM, D])
    din("wrt_t", [D, NEXP])
    din("consts", [128, 1024])
    dout("y", [S, D])

    dbg_names = []
    es = contextlib.ExitStack()
    with es:
        sb = {}
        sb["x_res"] = es.enter_context(nc.sbuf_tensor("x_res", [128, NT * 1024], F32))
        sb["fmA"] = es.enter_context(nc.sbuf_tensor("fmA", [128, NT * 1024], F32))
        sb["s1"] = es.enter_context(nc.sbuf_tensor("s1", [128, 6912], F32))
        sb["s2"] = es.enter_context(nc.sbuf_tensor("s2", [128, 6144], F32))
        sb["wb"] = es.enter_context(nc.sbuf_tensor("wb", [128, 3584], F32))
        sb["cst"] = es.enter_context(nc.sbuf_tensor("cst", [128, 1024], F32))
        sb["rowa"] = es.enter_context(nc.sbuf_tensor("rowa", [128, 2048], F32))
        sb["stat"] = es.enter_context(nc.sbuf_tensor("stat", [128, 128], F32))
        psum = [es.enter_context(nc.psum_tensor(f"pb{i}", [128, 512], F32)) for i in range(8)]
        sb["psum"] = psum

        def _alias(name, shape, dtype, arena, byte_off):
            return nc.alloc_sbuf_tensor_at(
                name, shape, dtype, offset=nc.lookup_mloc(sb[arena]).addr + byte_off)
        sb["lnj16"] = _alias("lnj16", [128, 16384], F16, "fmA", 0)
        sb["att16"] = _alias("att16", [128, 16384], F16, "fmA", 32768)
        sb["kq16"] = _alias("kq16", [128, 4096], F16, "s1", 0)
        sb["v16"] = _alias("v16", [128, 2080], F16, "s1", 8192)
        sb["er16"] = _alias("er16", [128, 4096], F16, "s1", 12544)
        sb["w16"] = _alias("w16", [128, 4096], F16, "wb", 4096)
        sb["mask16"] = _alias("mask16", [128, 512], F16, "cst", 2560)
        sb["wqB"] = _alias("wqB", [128, 8192], BF16, "s2", 0)
        sb["xqB"] = _alias("xqB", [128, 4096], BF16, "s1", 0)
        sb["lnH"] = _alias("lnH", [128, 16384], BF16, "fmA", 0)
        sb["lnL"] = _alias("lnL", [128, 16384], BF16, "fmA", 32768)
        sb["upH"] = _alias("upH", [128, 4096], BF16, "s2", 0)
        sb["upL"] = _alias("upL", [128, 4096], BF16, "s2", 8192)
        sb["dnH"] = _alias("dnH", [128, 4096], BF16, "s2", 16384)
        sb["dnL"] = _alias("dnL", [128, 4096], BF16, "wb", 0)
        sb["wrtH"] = _alias("wrtH", [128, 128], BF16, "wb", 8192)
        sb["wrtL"] = _alias("wrtL", [128, 128], BF16, "wb", 8448)
        sb["act16"] = _alias("act16", [128, 4096], BF16, "s1", 8192)

        sems = {}
        for name in ("WD", "GD", "SD", "T", "V", "Sc", "G", "St"):
            sems[name] = es.enter_context(nc.semaphore(name))
        P = Prog(nc, sems)
        banks = Banks(P)

        def dbg_tap(name, ap):
            if name in debug_taps:
                dout("dbg_" + name, [ap.shape[0], ap.shape[-1]])
                dbg_names.append(name)
                for e_ in ("V", "T", "Sc"):
                    P.wait("gpsimd", e_)
                P.dma("gpsimd", dram["dbg_" + name][:, :], ap, inc="St")
                P.wait("vector", "St")
                P.wait("scalar", "St")

        emit_all(P, banks, dram, sb, dbg_tap, stage)

        block = es.enter_context(nc.Block())
        if True:
            @block.sync
            def _(eng):
                for f in P.ops["sync"]:
                    f(eng)

            @block.tensor
            def _(eng):
                for f in P.ops["tensor"]:
                    f(eng)

            @block.vector
            def _(eng):
                for f in P.ops["vector"]:
                    f(eng)

            @block.scalar
            def _(eng):
                for f in P.ops["scalar"]:
                    f(eng)

            @block.gpsimd
            def _(eng):
                for f in P.ops["gpsimd"]:
                    f(eng)

    return nc, dbg_names



def vfence(P):
    P.mark("vector", "V")
    P.wait("vector", "V")


def ln_tile(P, sb, src_ap, dst_ap, scratch_ap, gamma=None, beta=None):
    stat = sb["stat"]
    mean = stat[:, 0:1]
    vsum = stat[:, 1:2]
    rs = stat[:, 2:3]
    W = src_ap.shape[-1]
    P.op("vector", lambda e: e.tensor_reduce(mean, src_ap, axis=AX.X, op=OP.add), inc="V")
    P.wait("vector", "V")
    P.op("vector", lambda e: e.tensor_scalar(mean, mean, 1.0 / W, None, OP.mult), inc="V")
    P.wait("vector", "V")
    P.op("vector", lambda e: e.tensor_scalar(dst_ap, src_ap, mean, None, OP.subtract))
    P.op("vector", lambda e: e.scalar_tensor_tensor(
        scratch_ap, dst_ap, 1.0, dst_ap, OP.mult, OP.mult, accum_out=vsum), inc="V")
    P.wait("scalar", "V")
    P.op("scalar", lambda e: e.activation(out=rs, in_=vsum, func=AF.Sqrt,
                                          bias=stat[:, 3:4], scale=1.0 / W), inc="Sc")
    P.wait("vector", "Sc")
    P.op("vector", lambda e: e.reciprocal(rs, rs), inc="V")
    P.wait("vector", "V")
    last_ops = [lambda e: e.tensor_scalar(dst_ap, dst_ap, rs, None, OP.mult)]
    if gamma is not None:
        last_ops.append(lambda e: e.tensor_scalar(dst_ap, dst_ap, gamma, None, OP.mult))
    if beta is not None:
        last_ops.append(lambda e: e.tensor_scalar(dst_ap, dst_ap, beta, None, OP.add))
    for i, f in enumerate(last_ops):
        P.op("vector", f, inc="V" if i == len(last_ops) - 1 else None)
    return P.cnt["V"]


def ln_tile_c(P, sb, src_ap, dst_ap, scratch_ap, mean, vsum, rs):
    W = 1024
    P.op("vector", lambda e: e.tensor_reduce(mean, src_ap, axis=AX.X, op=OP.add), inc="V")
    P.wait("vector", "V")
    P.op("vector", lambda e: e.tensor_scalar(mean, mean, 1.0 / W, None, OP.mult), inc="V")
    P.wait("vector", "V")
    P.op("vector", lambda e: e.tensor_scalar(dst_ap, src_ap, mean, None, OP.subtract))
    P.op("vector", lambda e: e.scalar_tensor_tensor(
        scratch_ap, dst_ap, 1.0, dst_ap, OP.mult, OP.mult, accum_out=vsum), inc="V")
    P.wait("scalar", "V")
    P.op("scalar", lambda e: e.activation(out=rs, in_=vsum, func=AF.Sqrt,
                                          bias=sb["stat"][:, 3:4], scale=1.0 / W), inc="Sc")
    P.wait("vector", "Sc")
    P.op("vector", lambda e: e.reciprocal(rs, rs), inc="V")
    P.wait("vector", "V")
    P.op("vector", lambda e: e.tensor_scalar(dst_ap, dst_ap, rs, None, OP.mult), inc="V")
    return P.cnt["V"]


def transpose_tile(P, banks, sb, src_ap_1024, dst_fn, bank_pair=(6, 7),
                   evac_eng="scalar", out_scale=None, ident=None, dst_lo_fn=None):
    psum = sb["psum"]
    if ident is None:
        ident = sb["cst"][:, 0:128]
    is_16 = (ident.dtype in (BF16, F16))

    def pslot(bank, j):
        if is_16:
            return psum[bank][:, j * 64:(j + 1) * 64].bitcast(ident.dtype)
        return psum[bank][:, j * 128:(j + 1) * 128]

    sem = {"scalar": "Sc", "vector": "V"}[evac_eng]
    for half, bank in ((0, bank_pair[0]), (1, bank_pair[1])):
        banks.acquire(bank)
        for j in range(4):
            kc = half * 4 + j
            P.op("tensor",
                 lambda e, kc=kc, j=j, bank=bank: e.matmul(
                     pslot(bank, j),
                     src_ap_1024[:, kc * 128:(kc + 1) * 128], ident,
                     is_transpose=True, start=True, stop=True),
                 inc="T" if j == 3 else None)
        wr = P.cnt["T"]
        P.wait(evac_eng, "T", wr)
        for j in range(4):
            kc = half * 4 + j
            if evac_eng == "scalar":
                if out_scale is None:
                    P.op("scalar", lambda e, kc=kc, j=j, bank=bank: e.copy(
                        out=dst_fn(kc), in_=pslot(bank, j)),
                        inc=sem if j == 3 else None)
                else:
                    P.op("scalar", lambda e, kc=kc, j=j, bank=bank: e.activation(
                        out=dst_fn(kc), in_=pslot(bank, j),
                        func=AF.Copy, scale=out_scale), inc=sem if j == 3 else None)
            else:
                P.op("vector", lambda e, kc=kc, j=j, bank=bank: e.tensor_copy(
                    dst_fn(kc), pslot(bank, j)),
                    inc=None if dst_lo_fn is not None else (sem if j == 3 else None))
                if dst_lo_fn is not None:
                    P.op("vector", lambda e, kc=kc, j=j, bank=bank: e.tensor_tensor(
                        dst_lo_fn(kc), pslot(bank, j), dst_fn(kc), OP.subtract),
                        inc=sem if j == 3 else None)
        banks.release(bank, sem)
    return P.cnt[sem]




def scalar_finalize(P, banks, sb, col_ap, out_row_ap, op=OP.max):
    psum = sb["psum"]
    ident = sb["cst"][:, 0:128]
    banks.acquire(6)
    P.wait("tensor", "V")
    P.op("tensor", lambda e: e.matmul(psum[6][0:1, 0:128], col_ap, ident,
                                      is_transpose=True, start=True, stop=True), inc="T")
    wr = P.cnt["T"]
    P.wait("vector", "T", wr)
    row = sb["rowa"][0:1, 1536:1664]
    P.op("vector", lambda e: e.tensor_copy(row, psum[6][0:1, 0:128]), inc="V")
    banks.release(6, "V")
    P.wait("vector", "V")
    P.op("vector", lambda e: e.tensor_reduce(out_row_ap, row, axis=AX.X, op=op), inc="V")
    P.wait("vector", "V")


def bcast_p0(P, banks, sb, src_row_ap, dst_ap, n):
    psum = sb["psum"]
    ones_row = sb["rowa"][0:1, 0:128]
    banks.acquire(6)
    P.wait("tensor", "V")
    P.op("tensor", lambda e: e.matmul(psum[6][:, 0:n], ones_row, src_row_ap,
                                      start=True, stop=True), inc="T")
    wr = P.cnt["T"]
    P.wait("vector", "T", wr)
    P.op("vector", lambda e: e.tensor_copy(dst_ap, psum[6][:, 0:n]), inc="V")
    banks.release(6, "V")
    P.wait("vector", "V")


def quant_grid_tile(P, sb, y_ap, out_ap, scr, c4col, mcol):
    c, d_, f1, msk = scr
    P.op("vector", lambda e: e.tensor_scalar(c, y_ap, c4col, None, OP.mult))
    P.op("vector", lambda e: e.tensor_tensor(d_, c, y_ap, OP.subtract))
    P.op("vector", lambda e: e.tensor_tensor(c, c, d_, OP.subtract))
    P.op("vector", lambda e: e.tensor_scalar(f1, y_ap, mcol, None, OP.add))
    P.op("vector", lambda e: e.tensor_scalar(f1, f1, mcol, None, OP.subtract))
    P.op("vector", lambda e: e.tensor_tensor(msk, y_ap, y_ap, OP.mult))
    P.op("vector", lambda e: e.tensor_scalar(msk, msk, 2.0 ** -12, None, OP.is_ge))
    P.op("vector", lambda e: e.tensor_tensor(d_, c, f1, OP.subtract))
    P.op("vector", lambda e: e.tensor_tensor(d_, d_, msk, OP.mult))
    P.op("vector", lambda e: e.tensor_tensor(out_ap, d_, f1, OP.add), inc="V")



def emit_all(P, banks, dram, sb, dbg_tap, stage=None):
    psum = sb["psum"]
    x_res = sb["x_res"]
    fmA = sb["fmA"]
    s1 = sb["s1"]
    s2 = sb["s2"]
    wb = sb["wb"]
    cst = sb["cst"]
    rowa = sb["rowa"]

    def xres_t(tt):
        return x_res[:, tt * 1024:(tt + 1) * 1024]

    def fm_ap(kc, tt):
        return fmA[:, kc * 2048 + tt * 128: kc * 2048 + (tt + 1) * 128]

    def fm_span(kc, q0, w):
        return fmA[:, kc * 2048 + q0: kc * 2048 + q0 + w]

    ones_row = rowa[0:1, 0:128]
    bias_row = rowa[0:1, 128:1152]
    bv_row = rowa[0:1, 1408:1536]
    recip_row = rowa[0:1, 1536:2048]
    maskrep = cst[:, 128:640]

    xo = dram["x_own"].rearrange("(t p) d -> t p d", p=128)
    for g in range(4):
        P.dma("sync", x_res[:, g * 4096:(g + 1) * 4096].rearrange("p (t d) -> p t d", t=4),
              xo[g * 4:(g + 1) * 4, :, :].rearrange("t p d -> p t d"))
    P.dma("sync", cst[:, :], dram["consts"][:, :])
    P.dma("sync", rowa[0:1, 0:1152], dram["rows_c"][0:1, 0:1152])
    P.dma("sync", rowa[32:48, :], dram["rows_c"][32:48, :])
    P.dma("sync", sb["stat"][:, 8:8 + NCH], dram["bq_c"][:, :])
    P.dma("sync", sb["stat"][:, 16:16 + NCH], dram["bk_c"][:, :])
    ldx = P.cnt["WD"]
    if stage == "x0":
        store_y(P, dram, x_res)
        return
    P.op("vector", lambda e: e.memset(sb["stat"][:, :], 0.0))
    P.op("vector", lambda e: e.memset(sb["stat"][:, 3:4], EPS))
    P.wait("vector", "WD", ldx)

    ln_tmp = s2[:, 0:1024]
    ln_sq = s2[:, 1024:2048]
    P.wait("vector", "WD", ldx)
    P.wait("tensor", "WD", ldx)
    for tt in range(NT):
        if tt > 0:
            P.wait("vector", "Sc")
        ln_tile(P, sb, xres_t(tt), ln_tmp, ln_sq)
        if stage == "ln0_notr":
            P.op("vector", lambda e, tt=tt: e.tensor_copy(
                fmA[:, tt * 1024:(tt + 1) * 1024], ln_tmp), inc="V")
            P.mark("scalar", "Sc")
            continue
        P.wait("tensor", "V")
        transpose_tile(P, banks, sb, ln_tmp, lambda kc, tt=tt: fm_ap(kc, tt))
    fm_done = P.cnt["Sc"]
    dbg_tap("ln0_fm", fmA[:, :])
    if stage in ("ln0", "ln0_notr"):
        store_y(P, dram, x_res)
        return

    k_pair = s1[:, 0:2048]
    q_pair = s1[:, 2048:4096]
    v_pair = s1[:, 4096:4096 + 16 * 130]
    rb = s1[0:64, 6176:6688]
    e_ring = s2[:, 0:4096]
    att_fm = s2[:, 4096:6144]

    wqs = wb[:, 0:1024]
    wks = wb[:, 1024:2048]
    wvs = wb[:, 2048:3072]

    def wslice_dma(dst, wname, hg):
        src = dram[wname][:, hg * 128:(hg + 1) * 128].rearrange("(c p) o -> p c o", p=128)
        return P.dma("sync", dst.rearrange("p (c o) -> p c o", c=NCH), src)

    last_hg_T = None
    for hg in range(8):
        if last_hg_T is not None:
            P.wait("sync", "T", last_hg_T)
        wslice_dma(wqs, "wq", hg)
        wslice_dma(wks, "wk", hg)
        P.dma("sync", bv_row, dram["rows_c"][3:4, hg * 128:(hg + 1) * 128])
        w_ready_kq = P.cnt["WD"]
        wslice_dma(wvs, "wv", hg)
        w_ready_v = P.cnt["WD"]

        P.wait("tensor", "WD", w_ready_kq)
        P.wait("tensor", "Sc", fm_done)

        def proj_fm(wsl, bcol, dst, scale, bank):
            for sp in range(4):
                banks.acquire(bank)
                for kc in range(NCH):
                    P.op("tensor", lambda e, kc=kc, sp=sp, wsl=wsl, bank=bank: e.matmul(
                        psum[bank][:, :], wsl[:, kc * 128:(kc + 1) * 128],
                        fm_span(kc, sp * 512, 512), start=(kc == 0),
                        stop=(kc == NCH - 1)), inc="T" if kc == NCH - 1 else None)
                wr = P.cnt["T"]
                P.wait("vector", "T", wr)
                if scale == 1.0:
                    P.op("vector", lambda e, sp=sp, dst=dst, bank=bank, bcol=bcol: e.tensor_scalar(
                        dst[:, sp * 512:(sp + 1) * 512], psum[bank][:, :],
                        bcol, None, OP.add), inc="V")
                else:
                    P.op("vector", lambda e, sp=sp, dst=dst, scale=scale, bank=bank, bcol=bcol: e.tensor_scalar(
                        dst[:, sp * 512:(sp + 1) * 512], psum[bank][:, :],
                        bcol, scale, OP.add, OP.mult), inc="V")
                banks.release(bank, "V")

        proj_fm(wks, sb["stat"][:, 16 + hg:17 + hg], k_pair, 1.0, 0)
        proj_fm(wqs, sb["stat"][:, 8 + hg:9 + hg], q_pair, 0.125, 1)

        P.wait("tensor", "WD", w_ready_v)
        for tt in range(NT):
            banks.acquire(2)
            for kc in range(NCH):
                P.op("tensor", lambda e, kc=kc, tt=tt: e.matmul(
                    psum[2][:, 0:128], fm_ap(kc, tt), wvs[:, kc * 128:(kc + 1) * 128],
                    start=(kc == 0), stop=False))
            P.op("tensor", lambda e: e.matmul(
                psum[2][:, 0:128], ones_row, bv_row,
                start=False, stop=True), inc="T")
            wr = P.cnt["T"]
            P.wait("vector", "T", wr)
            P.op("vector", lambda e, tt=tt: e.tensor_copy(
                v_pair[:, tt * 130:(tt + 1) * 130].rearrange("p (h c) -> p h c", h=2)[:, :, 0:64],
                psum[2][:, 0:128].rearrange("p (h c) -> p h c", h=2)), inc="V")
            banks.release(2, "V")
        P.op("vector", lambda e: e.memset(
            v_pair.rearrange("p (t h c) -> p t h c", t=16, h=2)[:, :, :, 64:65], 1.0),
            inc="V")
        v_done = P.cnt["V"]

        P.wait("tensor", "V")
        slot_guard = [0] * 8
        for h2 in range(2):
            hr0 = 64 * h2
            pv_bank = 3 + h2
            sc_bank = 0 + h2
            for sp in range(4):
                banks.acquire(pv_bank)
                nkc = 4 * (sp + 1)
                for kc in range(nkc):
                    is_diag = kc >= 4 * sp
                    j = kc - 4 * sp
                    q0 = 128 * j if is_diag else 0
                    qw = 512 - q0
                    slot = (4 + j) if is_diag else (kc % 2)
                    banks.acquire(sc_bank)
                    P.op("tensor", lambda e, kc=kc, sp=sp, hr0=hr0, q0=q0, qw=qw, b=sc_bank: e.matmul(
                        psum[b][:, 0:qw],
                        k_pair[hr0:hr0 + 64, kc * 128:(kc + 1) * 128],
                        q_pair[hr0:hr0 + 64, sp * 512 + q0: sp * 512 + q0 + qw],
                        start=True, stop=True), inc="T")
                    wr = P.cnt["T"]
                    P.wait("scalar", "T", max(wr, slot_guard[slot]))
                    P.op("scalar", lambda e, slot=slot, qw=qw, b=sc_bank: e.activation(
                        out=e_ring[:, slot * 512: slot * 512 + qw],
                        in_=psum[b][:, 0:qw], func=AF.Exp), inc="Sc")
                    banks.release(sc_bank, "Sc")
                    if not is_diag:
                        P.wait("tensor", "Sc")
                        if kc == 0:
                            P.wait("tensor", "V", v_done)
                        P.op("tensor", lambda e, kc=kc, h2=h2, pb=pv_bank, slot=slot: e.matmul(
                            psum[pb][0:65, 0:512],
                            v_pair[:, kc * 130 + 65 * h2: kc * 130 + 65 * h2 + 65],
                            e_ring[:, slot * 512: slot * 512 + 512],
                            start=(kc == 0), stop=False), inc="T")
                        slot_guard[slot] = P.cnt["T"]
                P.wait("vector", "Sc")
                P.op("vector", lambda e: e.tensor_tensor(
                    e_ring[:, 4 * 512:8 * 512].rearrange("p (s c) -> p s c", s=4)[:, :, 0:128],
                    e_ring[:, 4 * 512:8 * 512].rearrange("p (s c) -> p s c", s=4)[:, :, 0:128],
                    maskrep.rearrange("p (s c) -> p s c", s=4), OP.mult), inc="V")
                mk = P.cnt["V"]
                P.wait("tensor", "V", mk)
                if sp == 0:
                    P.wait("tensor", "V", v_done)
                for j in range(4):
                    kc = 4 * sp + j
                    q0 = 128 * j
                    P.op("tensor", lambda e, j=j, kc=kc, q0=q0, h2=h2, pb=pv_bank, sp=sp: e.matmul(
                        psum[pb][0:65, q0:512],
                        v_pair[:, kc * 130 + 65 * h2: kc * 130 + 65 * h2 + 65],
                        e_ring[:, (4 + j) * 512: (4 + j) * 512 + (512 - q0)],
                        start=(sp == 0 and j == 0), stop=(j == 3)), inc="T")
                    slot_guard[4 + j] = P.cnt["T"]
                wr = P.cnt["T"]
                P.wait("vector", "T", wr)
                P.op("vector", lambda e, pb=pv_bank: e.reciprocal(
                    recip_row, psum[pb][64:65, 0:512]), inc="V")
                rp = P.cnt["V"]
                banks.acquire(5)
                P.wait("tensor", "V", rp)
                P.op("tensor", lambda e: e.matmul(
                    psum[5][0:64, 0:512], ones_row[:, 0:64], recip_row,
                    start=True, stop=True), inc="T")
                wr2 = P.cnt["T"]
                P.wait("vector", "T", wr2)
                P.op("vector", lambda e: e.tensor_copy(rb, psum[5][0:64, 0:512]), inc="V")
                banks.release(5, "V")
                P.op("vector", lambda e, hr0=hr0, sp=sp, pb=pv_bank: e.tensor_tensor(
                    att_fm[hr0:hr0 + 64, sp * 512:(sp + 1) * 512],
                    psum[pb][0:64, 0:512], rb, OP.mult), inc="V")
                banks.release(pv_bank, "V")
        att_done = P.cnt["V"]

        P.wait("sync", "V", att_done)
        wos = wb[:, 0:1024]
        wo_rdy = P.dma("sync", wos, dram["wo"][hg * 128:(hg + 1) * 128, :])
        P.wait("tensor", "WD", wo_rdy)
        for tt in range(NT):
            for osp in range(2):
                banks.acquire(5)
                P.op("tensor", lambda e, tt=tt, osp=osp, hg=hg: e.matmul(
                    psum[5][:, :], att_fm[:, tt * 128:(tt + 1) * 128],
                    wos[:, osp * 512:(osp + 1) * 512],
                    start=True, stop=(hg != 0)), inc=None if hg == 0 else "T")
                if hg == 0:
                    P.op("tensor", lambda e, osp=osp: e.matmul(
                        psum[5][:, :], ones_row,
                        bias_row[:, osp * 512:(osp + 1) * 512],
                        start=False, stop=True), inc="T")
                wr = P.cnt["T"]
                P.wait("vector", "T", wr)
                P.op("vector", lambda e, tt=tt, osp=osp: e.tensor_tensor(
                    xres_t(tt)[:, osp * 512:(osp + 1) * 512],
                    xres_t(tt)[:, osp * 512:(osp + 1) * 512],
                    psum[5][:, :], OP.add), inc="V")
                banks.release(5, "V")
        last_hg_T = P.cnt["T"]



    stat = sb["stat"]
    amax_run = stat[:, 32:33]
    tmax = stat[:, 33:34]
    mean_c = stat[:, 35:36]
    vsum_c = stat[:, 36:37]
    rs_c = stat[:, 37:38]
    P.op("vector", lambda e: e.memset(amax_run, 0.0))
    lnj_bf = sb["lnj16"][:, :]
    att_bf = sb["att16"][:, :]
    kq_bf = sb["kq16"][:, :]
    v_bf = sb["v16"][:, :]
    er_bf = sb["er16"][:, :]
    w_bf = sb["w16"][:, :]
    xt32 = s2[:, 0:1024]
    lntm32 = s2[:, 2048:3072]
    x1t = s2[:, 1024:2048]
    mask_bf = sb["mask16"][:, :]
    rb_c = s1[0:64, 6176:6688]
    P.op("vector", lambda e: e.tensor_copy(mask_bf, cst[:, 128:640]), inc="V")

    xall = dram["x_all"].rearrange("b (t p) d -> b t p d", p=128)
    for j in range(B - 1):
        for tt in range(NT):
            P.wait("gpsimd", "V")
            xrd = P.dma("gpsimd", xt32, xall[j, tt, :, :])
            P.wait("vector", "GD", xrd)
            ln_tile_c(P, sb, xt32, lntm32, s2[:, 1024:2048], mean_c, vsum_c, rs_c)
            P.wait("tensor", "V")
            transpose_tile(P, banks, sb, lntm32,
                           lambda kc, tt=tt: lnj_bf[:, kc * 2048 + tt * 128: kc * 2048 + (tt + 1) * 128],
                           bank_pair=(6, 7), evac_eng="vector")
        lnj_done = P.cnt["V"]

        last_T = None
        for hg in range(8):
            if last_T is not None:
                P.wait("gpsimd", "T", last_T)
            for wi, wname in enumerate(("wq", "wk", "wv")):
                P.dma("gpsimd", w_bf[:, wi * 1024:(wi + 1) * 1024].rearrange("p (c o) -> p c o", c=NCH),
                      dram[wname][:, hg * 128:(hg + 1) * 128].rearrange("(c p) o -> p c o", p=128))
            P.dma("gpsimd", bv_row, dram["rows_c"][3:4, hg * 128:(hg + 1) * 128])
            w_rdy = P.cnt["GD"]
            P.wait("tensor", "GD", w_rdy)
            P.wait("tensor", "V", lnj_done)
            for which, wofs, bcol, dst_ofs, sc_ in (
                    ("k", 1024, sb["stat"][:, 16 + hg:17 + hg], 0, 1.0),
                    ("q", 0, sb["stat"][:, 8 + hg:9 + hg], 2048, 0.125)):
                for sp in range(4):
                    banks.acquire(0)
                    for kc in range(NCH):
                        P.op("tensor", lambda e, kc=kc, sp=sp, wofs=wofs: e.matmul(
                            psum[0][:, :], w_bf[:, wofs + kc * 128:(wofs + kc * 128) + 128],
                            lnj_bf[:, kc * 2048 + sp * 512: kc * 2048 + (sp + 1) * 512],
                            start=(kc == 0), stop=(kc == NCH - 1)),
                            inc="T" if kc == NCH - 1 else None)
                    wr = P.cnt["T"]
                    P.wait("vector", "T", wr)
                    P.op("vector", lambda e, sp=sp, dst_ofs=dst_ofs, bcol=bcol, sc_=sc_: e.tensor_scalar(
                        kq_bf[:, dst_ofs + sp * 512: dst_ofs + (sp + 1) * 512],
                        psum[0][:, :], bcol, sc_, OP.add, OP.mult), inc="V")
                    banks.release(0, "V")
            for tt in range(NT):
                banks.acquire(2)
                for kc in range(NCH):
                    P.op("tensor", lambda e, kc=kc, tt=tt: e.matmul(
                        psum[2][:, 0:128],
                        lnj_bf[:, kc * 2048 + tt * 128: kc * 2048 + (tt + 1) * 128],
                        w_bf[:, 2048 + kc * 128: 2048 + (kc + 1) * 128],
                        start=(kc == 0), stop=False))
                P.op("tensor", lambda e: e.matmul(
                    psum[2][:, 0:128], ones_row, bv_row, start=False, stop=True), inc="T")
                wr = P.cnt["T"]
                P.wait("vector", "T", wr)
                P.op("vector", lambda e, tt=tt: e.tensor_copy(
                    v_bf[:, tt * 130:(tt + 1) * 130].rearrange("p (h c) -> p h c", h=2)[:, :, 0:64],
                    psum[2][:, 0:128].rearrange("p (h c) -> p h c", h=2)), inc="V")
                banks.release(2, "V")
            P.op("vector", lambda e: e.memset(
                v_bf.rearrange("p (t h c) -> p t h c", t=16, h=2)[:, :, :, 64:65], 1.0), inc="V")
            v_done = P.cnt["V"]
            P.wait("tensor", "V", v_done)
            slot_guard = [0] * 8
            for h2 in range(2):
                hr0 = 64 * h2
                pv_bank = 3 + h2
                sc_bank = 0 + h2
                for sp in range(4):
                    banks.acquire(pv_bank)
                    nkc = 4 * (sp + 1)
                    for kc in range(nkc):
                        is_diag = kc >= 4 * sp
                        jj = kc - 4 * sp
                        q0 = 128 * jj if is_diag else 0
                        qw = 512 - q0
                        slot = (4 + jj) if is_diag else (kc % 2)
                        banks.acquire(sc_bank)
                        P.op("tensor", lambda e, kc=kc, sp=sp, hr0=hr0, q0=q0, qw=qw, b=sc_bank: e.matmul(
                            psum[b][:, 0:qw],
                            kq_bf[hr0:hr0 + 64, kc * 128:(kc + 1) * 128],
                            kq_bf[hr0:hr0 + 64, 2048 + sp * 512 + q0: 2048 + sp * 512 + q0 + qw],
                            start=True, stop=True), inc="T")
                        wr = P.cnt["T"]
                        P.wait("scalar", "T", max(wr, slot_guard[slot]))
                        P.op("scalar", lambda e, slot=slot, qw=qw, b=sc_bank: e.activation(
                            out=er_bf[:, slot * 512: slot * 512 + qw],
                            in_=psum[b][:, 0:qw], func=AF.Exp), inc="Sc")
                        banks.release(sc_bank, "Sc")
                        if not is_diag:
                            P.wait("tensor", "Sc")
                            P.op("tensor", lambda e, kc=kc, h2=h2, pb=pv_bank, slot=slot: e.matmul(
                                psum[pb][0:65, 0:512],
                                v_bf[:, kc * 130 + 65 * h2: kc * 130 + 65 * h2 + 65],
                                er_bf[:, slot * 512: slot * 512 + 512],
                                start=(kc == 0), stop=False), inc="T")
                            slot_guard[slot] = P.cnt["T"]
                    P.wait("vector", "Sc")
                    P.op("vector", lambda e: e.tensor_tensor(
                        er_bf[:, 4 * 512:8 * 512].rearrange("p (s c) -> p s c", s=4)[:, :, 0:128],
                        er_bf[:, 4 * 512:8 * 512].rearrange("p (s c) -> p s c", s=4)[:, :, 0:128],
                        mask_bf.rearrange("p (s c) -> p s c", s=4), OP.mult), inc="V")
                    mk = P.cnt["V"]
                    P.wait("tensor", "V", mk)
                    for jj in range(4):
                        kc = 4 * sp + jj
                        q0 = 128 * jj
                        P.op("tensor", lambda e, jj=jj, kc=kc, q0=q0, h2=h2, pb=pv_bank, sp=sp: e.matmul(
                            psum[pb][0:65, q0:512],
                            v_bf[:, kc * 130 + 65 * h2: kc * 130 + 65 * h2 + 65],
                            er_bf[:, (4 + jj) * 512: (4 + jj) * 512 + (512 - q0)],
                            start=(sp == 0 and jj == 0), stop=(jj == 3)), inc="T")
                        slot_guard[4 + jj] = P.cnt["T"]
                    wr = P.cnt["T"]
                    P.wait("vector", "T", wr)
                    P.op("vector", lambda e, pb=pv_bank: e.reciprocal(
                        recip_row, psum[pb][64:65, 0:512]), inc="V")
                    rp = P.cnt["V"]
                    banks.acquire(5)
                    P.wait("tensor", "V", rp)
                    P.op("tensor", lambda e: e.matmul(
                        psum[5][0:64, 0:512], ones_row[:, 0:64], recip_row,
                        start=True, stop=True), inc="T")
                    wr2 = P.cnt["T"]
                    P.wait("vector", "T", wr2)
                    P.op("vector", lambda e: e.tensor_copy(rb_c, psum[5][0:64, 0:512]), inc="V")
                    banks.release(5, "V")
                    P.op("vector", lambda e, hg=hg, hr0=hr0, sp=sp, pb=pv_bank: e.tensor_tensor(
                        att_bf[hr0:hr0 + 64, hg * 2048 + sp * 512: hg * 2048 + (sp + 1) * 512],
                        psum[pb][0:64, 0:512], rb_c, OP.mult), inc="V")
                    banks.release(pv_bank, "V")
            last_T = P.cnt["T"]
        att_done_c = P.cnt["V"]

        wo_bf0 = sb["w16"][:, :]
        wo_bf1 = sb["er16"][:, :]
        P.wait("gpsimd", "T", last_T)
        P.wait("gpsimd", "Sc")
        P.dma("gpsimd", wo_bf0.rearrange("p (c o) -> p c o", c=NCH),
              dram["wo"][:, 0:512].rearrange("(c p) o -> p c o", p=128))
        P.dma("gpsimd", wo_bf1.rearrange("p (c o) -> p c o", c=NCH),
              dram["wo"][:, 512:1024].rearrange("(c p) o -> p c o", p=128))
        wo_rdy = P.cnt["GD"]
        P.wait("tensor", "GD", wo_rdy)
        P.wait("tensor", "V", att_done_c)
        for tt in range(NT):
            P.wait("gpsimd", "V")
            xrd = P.dma("gpsimd", xt32, xall[j, tt, :, :])
            for osp in range(2):
                banks.acquire(5)
                wo_h = wo_bf0 if osp == 0 else wo_bf1
                for kc in range(NCH):
                    P.op("tensor", lambda e, kc=kc, tt=tt, wo_h=wo_h: e.matmul(
                        psum[5][:, :], att_bf[:, kc * 2048 + tt * 128:(kc * 2048 + tt * 128) + 128],
                        wo_h[:, kc * 512:(kc + 1) * 512],
                        start=(kc == 0), stop=False))
                P.op("tensor", lambda e, osp=osp: e.matmul(
                    psum[5][:, :], ones_row, bias_row[:, osp * 512:(osp + 1) * 512],
                    start=False, stop=True), inc="T")
                wr = P.cnt["T"]
                if osp == 0:
                    P.wait("vector", "GD", xrd)
                P.wait("vector", "T", wr)
                P.op("vector", lambda e, osp=osp: e.tensor_tensor(
                    x1t[:, osp * 512:(osp + 1) * 512], psum[5][:, :],
                    xt32[:, osp * 512:(osp + 1) * 512], OP.add), inc="V")
                banks.release(5, "V")
            P.op("vector", lambda e: e.tensor_reduce(mean_c, x1t, axis=AX.X, op=OP.add), inc="V")
            P.wait("vector", "V")
            P.op("vector", lambda e: e.tensor_scalar(mean_c, mean_c, 1.0 / D, None, OP.mult), inc="V")
            P.wait("vector", "V")
            P.op("vector", lambda e: e.tensor_scalar(
                s2[:, 0:1024], x1t, mean_c, None, OP.subtract))
            P.op("vector", lambda e: e.scalar_tensor_tensor(
                s2[:, 4096:5120], s2[:, 0:1024], 1.0, s2[:, 0:1024], OP.mult, OP.mult,
                accum_out=vsum_c), inc="V")
            P.wait("scalar", "V")
            P.op("scalar", lambda e: e.activation(out=rs_c, in_=vsum_c, func=AF.Sqrt,
                                                  bias=stat[:, 3:4], scale=1.0 / D), inc="Sc")
            P.wait("vector", "Sc")
            P.op("vector", lambda e: e.reciprocal(rs_c, rs_c), inc="V")
            P.wait("vector", "V")
            P.op("vector", lambda e: e.tensor_reduce(
                tmax, s2[:, 0:1024], axis=AX.X, op=OP.max, apply_absolute_value=True), inc="V")
            P.wait("vector", "V")
            P.op("vector", lambda e: e.tensor_tensor(tmax, tmax, rs_c, OP.mult), inc="V")
            P.wait("vector", "V")
            P.op("vector", lambda e: e.tensor_tensor(amax_run, amax_run, tmax, OP.max), inc="V")

    if stage == "amax":
        store_y(P, dram, x_res)
        return

    stat = sb["stat"]
    ind = stat[:, 24:25]
    c4col = stat[:, 25:26]
    mcol = stat[:, 26:27]
    sc_x = stat[:, 27:28]
    inv_x = stat[:, 28:29]
    sc_w = stat[:, 29:30]
    inv_w = stat[:, 30:31]
    inv2 = stat[:, 31:32]
    amax_run = stat[:, 32:33]
    tmax = stat[:, 33:34]

    def vf():
        P.mark("vector", "V")
        P.wait("vector", "V")

    P.op("vector", lambda e: e.memset(ind, 2.0 ** -30))
    vf()
    P.op("vector", lambda e: e.tensor_scalar(ind, ind, 1.0, None, OP.add))
    vf()
    P.op("vector", lambda e: e.tensor_scalar(ind, ind, 1.0, None, OP.subtract))
    vf()
    P.op("vector", lambda e: e.tensor_scalar(ind, ind, 2.0 ** 30, None, OP.mult))
    vf()
    P.op("vector", lambda e: e.tensor_scalar(c4col, ind, 2.0 ** 49, None, OP.mult))
    P.op("vector", lambda e: e.tensor_scalar(c4col, c4col, float(2 ** 20 + 1), None, OP.add))
    vf()
    P.op("vector", lambda e: e.tensor_scalar(tmax, ind, 2.0 ** 20, None, OP.mult))
    vf()
    P.op("vector", lambda e: e.tensor_tensor(c4col, c4col, tmax, OP.subtract))
    vf()
    P.op("vector", lambda e: e.tensor_scalar(mcol, ind, 1.5 * 2.0 ** 43, None, OP.mult))
    P.op("vector", lambda e: e.tensor_scalar(mcol, mcol, 1.5 * 2.0 ** 14, None, OP.add))
    vf()
    P.op("vector", lambda e: e.tensor_scalar(tmax, ind, 1.5 * 2.0 ** 14, None, OP.mult))
    vf()
    P.op("vector", lambda e: e.tensor_tensor(mcol, mcol, tmax, OP.subtract))
    vf()

    for tt in range(NT):
        if tt > 0:
            P.wait("vector", "Sc")
        ln_tile(P, sb, xres_t(tt), ln_tmp, ln_sq)
        P.op("vector", lambda e: e.tensor_reduce(
            tmax, ln_tmp, axis=AX.X, op=OP.max, apply_absolute_value=True), inc="V")
        P.wait("vector", "V")
        P.op("vector", lambda e: e.tensor_tensor(amax_run, amax_run, tmax, OP.max))
        P.wait("tensor", "V")
        transpose_tile(P, banks, sb, ln_tmp, lambda kc, tt=tt: fm_ap(kc, tt))
    ln1_done = P.cnt["Sc"]
    dbg_tap("ln1_fm", fmA[:, :])

    vf()
    arow = rowa[0:1, 1664:1665]
    scrow = rowa[0:1, 1665:1667]
    scalar_finalize(P, banks, sb, amax_run, arow, op=OP.max)
    P.op("vector", lambda e: e.tensor_scalar(arow, arow, 1e-12, None, OP.max))
    vf()
    P.op("vector", lambda e: e.reciprocal(scrow[:, 0:1], arow))
    vf()
    P.op("vector", lambda e: e.tensor_scalar(scrow[:, 0:1], scrow[:, 0:1], 448.0, None, OP.mult))
    P.op("vector", lambda e: e.tensor_scalar(scrow[:, 1:2], arow, 1.0 / 448.0, None, OP.mult))
    vf()
    bcast_p0(P, banks, sb, scrow, stat[:, 27:29], 2)

    wstage = s1[:, 2048:6144]
    P.op("vector", lambda e: e.memset(amax_run, 0.0))
    wf_t = dram["w_ffn0_t"].rearrange("(c p) o -> p c o", p=128)
    P.wait("sync", "GD")
    P.wait("sync", "V")
    P.wait("sync", "T")
    for half in range(2):
        if half > 0:
            P.wait("sync", "V")
        wrd = P.dma("sync", wstage.rearrange("p (c o) -> p c o", c=4),
                    wf_t[:, half * 4:(half + 1) * 4, :])
        P.wait("vector", "WD", wrd)
        P.op("vector", lambda e: e.tensor_reduce(
            tmax, wstage, axis=AX.X, op=OP.max, apply_absolute_value=True), inc="V")
        P.wait("vector", "V")
        P.op("vector", lambda e: e.tensor_tensor(amax_run, amax_run, tmax, OP.max), inc="V")
    vf()
    scalar_finalize(P, banks, sb, amax_run, arow, op=OP.max)
    P.op("vector", lambda e: e.tensor_scalar(arow, arow, 1e-12, None, OP.max))
    vf()
    P.op("vector", lambda e: e.reciprocal(scrow[:, 0:1], arow))
    vf()
    P.op("vector", lambda e: e.tensor_scalar(scrow[:, 0:1], scrow[:, 0:1], 448.0, None, OP.mult))
    P.op("vector", lambda e: e.tensor_scalar(scrow[:, 1:2], arow, 1.0 / 448.0, None, OP.mult))
    vf()
    bcast_p0(P, banks, sb, scrow, stat[:, 29:31], 2)
    P.op("vector", lambda e: e.tensor_tensor(inv2, inv_x, inv_w, OP.mult), inc="V")
    P.wait("vector", "V")

    wq_bf = sb["wqB"][:, :]
    qscr = [s2[:, 4096 + i * 512:4096 + (i + 1) * 512] for i in range(4)]
    yq = s1[:, 0:2048]
    for half in range(2):
        P.wait("sync", "V")
        wrd = P.dma("sync", wstage.rearrange("p (c o) -> p c o", c=4),
                    wf_t[:, half * 4:(half + 1) * 4, :])
        P.wait("vector", "WD", wrd)
        for sub in range(8):
            kc = half * 4 + sub // 2
            off = (sub % 2) * 512
            ysl_w = yq[:, 0:512]
            P.op("vector", lambda e, sub=sub, ysl_w=ysl_w: e.tensor_scalar(
                ysl_w, wstage[:, sub * 512:(sub + 1) * 512], sc_w, None, OP.mult))
            quant_grid_tile(P, sb, ysl_w, wq_bf[:, kc * 1024 + off: kc * 1024 + off + 512],
                            qscr, c4col, mcol)
    wq_done = P.cnt["V"]
    dbg_tap("wq_s2", s2[:, 0:4096])

    brd = P.dma("sync", bias_row, dram["rows_c"][4:5, 0:1024])
    P.wait("vector", "WD", brd)
    P.op("vector", lambda e: e.tensor_tensor(tmax, sc_x, sc_w, OP.mult), inc="V")
    P.wait("vector", "V")
    P.op("vector", lambda e: e.tensor_scalar(bias_row, bias_row, tmax[0:1, 0:1], None, OP.mult), inc="V")
    bias_done = P.cnt["V"]

    xq_bf = sb["xqB"][:, :]
    for sp in range(4):
        for kc in range(NCH):
            ysl_x = s1[:, 6400:6912]
            P.op("vector", lambda e, kc=kc, sp=sp, ysl_x=ysl_x: e.tensor_scalar(
                ysl_x, fm_span(kc, sp * 512, 512), sc_x, None, OP.mult))
            quant_grid_tile(P, sb, ysl_x, xq_bf[:, kc * 512:(kc + 1) * 512],
                            qscr, c4col, mcol)
        xq_done = P.cnt["V"]
        P.wait("tensor", "V", xq_done)
        P.wait("tensor", "Sc", ln1_done)
        for t4 in range(4):
            tt = sp * 4 + t4
            for osp in range(2):
                banks.acquire(5)
                for kc in range(NCH):
                    P.op("tensor", lambda e, kc=kc, t4=t4, osp=osp: e.matmul(
                        psum[5][:, :],
                        xq_bf[:, kc * 512 + t4 * 128: kc * 512 + (t4 + 1) * 128],
                        wq_bf[:, kc * 1024 + osp * 512: kc * 1024 + (osp + 1) * 512],
                        start=(kc == 0), stop=False))
                P.op("tensor", lambda e, osp=osp: e.matmul(
                    psum[5][:, :], ones_row, bias_row[:, osp * 512:(osp + 1) * 512],
                    start=False, stop=True), inc="T")
                wr = P.cnt["T"]
                P.wait("vector", "T", wr)
                P.op("vector", lambda e, tt=tt, osp=osp: e.scalar_tensor_tensor(
                    xres_t(tt)[:, osp * 512:(osp + 1) * 512],
                    psum[5][:, :], inv2,
                    xres_t(tt)[:, osp * 512:(osp + 1) * 512],
                    OP.mult, OP.add), inc="V")
                banks.release(5, "V")

    dbg_tap("statd", sb["stat"][:, :])
    if stage == "x2":
        store_y(P, dram, x_res)
        return

    TC = 256
    NCHK = S // TC
    NPT = DI // 128
    xs_a = lambda pt: fmA[:, pt * TC:(pt + 1) * TC]
    u_a = lambda pt: fmA[:, 4096 + pt * TC: 4096 + (pt + 1) * TC]
    dl_a = lambda pt: fmA[:, 8192 + pt * TC: 8192 + (pt + 1) * TC]
    y_a = lambda pt: fmA[:, 12288 + pt * TC: 12288 + (pt + 1) * TC]
    wstg1 = fmA[:, 4096:12288]
    ln2fm = lambda kc: s1[:, kc * TC:(kc + 1) * TC]
    halo = s1[:, 4096:4144]
    negA = s1[:, 5424:5680]
    convw = s1[:, 5680:5744]
    convb = s1[:, 5744:5760]
    bdt_c = s1[:, 5760:5776]
    dpar_c = s1[:, 5776:5792]
    carry = s1[:, 6144:6400]
    a_t = s2[:, 0:256]
    b_t = s2[:, 256:512]
    h_t = s2[:, 512:768]
    xdbl = s2[:, 768:1024]
    e_scr = s2[:, 1024:2048]
    ln2tm = s2[:, 2048:4096]
    t_tm = s2[:, 4096:5120]
    ln_scr = s2[:, 5120:6144]
    bc_b = s1[32:48, 6400:6656]
    bc_c = s1[32:48, 6656:6912]
    onehot16 = rowa[32:48, :]

    P.wait("sync", "V")
    P.wait("sync", "T")
    P.dma("sync", s1[:, 5424:5680], dram["a_log_c"][:, :])
    P.dma("sync", convw, dram["conv_w_c"][:, :])
    P.dma("sync", convb, dram["conv_b_c"][:, :])
    P.dma("sync", bdt_c, dram["b_dt_c"][:, :])
    P.dma("sync", dpar_c, dram["d_par_c"][:, :])
    P.dma("sync", wb[:, 0:1536].rearrange("p (c r) -> p c r", c=16),
          dram["w_x_t"].rearrange("(c p) r -> p c r", p=128))
    P.dma("sync", wb[0:DTR, 1536:3584], dram["w_dt_t"][:, :])
    ld_e = P.cnt["WD"]
    P.wait("scalar", "WD", ld_e)
    P.op("scalar", lambda e: e.activation(out=negA, in_=negA, func=AF.Exp, scale=1.0), inc="Sc")
    P.wait("vector", "Sc")
    P.op("vector", lambda e: e.tensor_scalar(negA, negA, -1.0, None, OP.mult), inc="V")
    P.op("vector", lambda e: e.memset(halo, 0.0))
    P.op("vector", lambda e: e.memset(carry, 0.0), inc="V")

    win_t = dram["w_in_t"].rearrange("(c p) e -> p c e", p=128)
    wout_t = dram["w_out_t"].rearrange("(c p) o -> p c o", p=128)

    for ch in range(NCHK):
        P.wait("vector", "T")
        for t2 in range(2):
            tile = 2 * ch + t2
            ln_tile(P, sb, xres_t(tile), ln2tm[:, t2 * 1024:(t2 + 1) * 1024], ln_scr)
            P.wait("tensor", "V")
            transpose_tile(P, banks, sb, ln2tm[:, t2 * 1024:(t2 + 1) * 1024],
                           lambda kc, t2=t2: ln2fm(kc)[:, t2 * 128:(t2 + 1) * 128],
                           bank_pair=(6, 7), evac_eng="vector")
        ln2_done = P.cnt["V"]

        for half in range(2):
            P.wait("sync", "V")
            wrd = P.dma("sync", wstg1, win_t[:, :, half * 1024:(half + 1) * 1024])
            P.wait("tensor", "WD", wrd)
            P.wait("tensor", "V", ln2_done)
            for es8 in range(8):
                es = half * 8 + es8
                banks.acquire(4)
                for kc in range(NCH):
                    P.op("tensor", lambda e, kc=kc, es8=es8: e.matmul(
                        psum[4][:, 0:TC], wstg1[:, kc * 1024 + es8 * 128: kc * 1024 + (es8 + 1) * 128],
                        ln2fm(kc), start=(kc == 0), stop=(kc == NCH - 1)),
                        inc="T" if kc == NCH - 1 else None)
                wr = P.cnt["T"]
                P.wait("vector", "T", wr)
                P.op("vector", lambda e, es=es: e.tensor_copy(xs_a(es), psum[4][:, 0:TC]), inc="V")
                banks.release(4, "V")

        for pt in range(NPT):
            if pt > 0:
                P.wait("vector", "Sc")
            acc = s2[:, 1024:1280]
            acc2 = s2[:, 1280:1536]
            for k in range(4):
                def shifted(e_unused, pt=pt, k=k):
                    pass
                if k < 3:
                    nh = 3 - k
                    srcs = [(halo[:, pt * 3 + (3 - nh): pt * 3 + 3], 0, nh),
                            (xs_a(pt)[:, 0:TC - nh], nh, TC - nh)]
                else:
                    srcs = [(xs_a(pt), 0, TC)]
                wcol = convw[:, pt * 4 + k: pt * 4 + k + 1]
                for (src_ap, off, w_) in srcs:
                    if k == 0 and off == 0:
                        P.op("vector", lambda e, src_ap=src_ap, off=off, w_=w_, wcol=wcol, acc=acc: e.tensor_scalar(
                            acc[:, off:off + w_], src_ap, wcol, None, OP.mult))
                    elif k == 0:
                        P.op("vector", lambda e, src_ap=src_ap, off=off, w_=w_, wcol=wcol, acc=acc: e.tensor_scalar(
                            acc[:, off:off + w_], src_ap, wcol, None, OP.mult))
                    else:
                        P.op("vector", lambda e, src_ap=src_ap, off=off, w_=w_, wcol=wcol, acc=acc, acc2=acc2: e.scalar_tensor_tensor(
                            acc2[:, off:off + w_], src_ap, wcol, acc[:, off:off + w_],
                            OP.mult, OP.add))
                if k > 0:
                    acc, acc2 = acc2, acc
            P.op("vector", lambda e, acc=acc: e.engine_nop(), inc="V")
            P.wait("scalar", "V")
            P.op("scalar", lambda e, pt=pt, acc=acc: e.activation(
                out=u_a(pt), in_=acc, func=AF.Silu,
                bias=convb[:, pt:pt + 1], scale=1.0), inc="Sc")
        P.op("vector", lambda e: e.tensor_copy(
            halo.rearrange("p (t c) -> p t c", t=16),
            fmA[:, 0:4096].rearrange("p (t c) -> p t c", t=16)[:, :, TC - 3:TC]), inc="V")
        u_done = P.cnt["Sc"]

        P.wait("tensor", "Sc", u_done)
        P.wait("tensor", "WD", ld_e)
        banks.acquire(5)
        for ec in range(NPT):
            P.op("tensor", lambda e, ec=ec: e.matmul(
                psum[5][0:96, 0:TC], wb[:, ec * 96:(ec + 1) * 96], u_a(ec),
                start=(ec == 0), stop=(ec == NPT - 1)), inc="T" if ec == NPT - 1 else None)
        wr = P.cnt["T"]
        P.wait("vector", "T", wr)
        P.op("vector", lambda e: e.tensor_copy(xdbl[0:96, :], psum[5][0:96, 0:TC]), inc="V")
        banks.release(5, "V")
        P.wait("gpsimd", "V")
        P.dma("gpsimd", bc_b, xdbl[64:80, :], inc="GD")
        P.dma("gpsimd", bc_c, xdbl[80:96, :], inc="GD")
        bc_done = P.cnt["GD"]

        for pt in range(NPT):
            banks.acquire(5)
            wsl = wb[0:DTR, 1536 + pt * 128: 1536 + (pt + 1) * 128]
            P.op("tensor", lambda e, wsl=wsl: e.matmul(
                psum[5][:, 0:TC], wsl, xdbl[0:64, :], start=True, stop=True), inc="T")
            wr = P.cnt["T"]
            P.wait("scalar", "T", wr)
            P.op("scalar", lambda e, pt=pt: e.activation(
                out=e_scr[:, 0:TC], in_=psum[5][:, 0:TC], func=AF.Exp,
                bias=bdt_c[:, pt:pt + 1], scale=1.0), inc="Sc")
            banks.release(5, "Sc")
            P.op("scalar", lambda e, pt=pt: e.activation(
                out=dl_a(pt), in_=e_scr[:, 0:TC], func=AF.Ln, bias=1.0, scale=1.0), inc="Sc")
        dl_done = P.cnt["Sc"]

        P.wait("vector", "Sc", dl_done)
        for pt in range(NPT):
            P.op("vector", lambda e, pt=pt: e.tensor_tensor(xs_a(pt), dl_a(pt), u_a(pt), OP.mult))
        P.op("vector", lambda e: e.engine_nop(), inc="V")
        du_done = P.cnt["V"]

        prev_scan = [P.cnt["V"]]
        P.wait("tensor", "GD", bc_done)
        for n in range(DSTATE):
            banks.acquire(2)
            P.op("tensor", lambda e, n=n: e.matmul(
                psum[2][:, 0:TC], onehot16[:, n * 128:(n + 1) * 128], bc_b,
                start=True, stop=True), inc="T")
            banks.acquire(3)
            P.op("tensor", lambda e, n=n: e.matmul(
                psum[3][:, 0:TC], onehot16[:, n * 128:(n + 1) * 128], bc_c,
                start=True, stop=True), inc="T")
            wr = P.cnt["T"]
            P.wait("vector", "T", wr)
            P.wait("scalar", "T", wr)
            if n == 0:
                P.wait("vector", "V", du_done)
                P.wait("scalar", "V", du_done)
            for pt in range(NPT):
                nacol = negA[:, pt * 16 + n: pt * 16 + n + 1]
                ccol = carry[:, pt * 16 + n: pt * 16 + n + 1]
                P.wait("scalar", "V", prev_scan[0])
                P.op("scalar", lambda e, pt=pt, nacol=nacol: e.activation(
                    out=a_t, in_=dl_a(pt), func=AF.Exp, scale=nacol), inc="Sc")
                sc_a = P.cnt["Sc"]
                P.op("vector", lambda e, pt=pt: e.tensor_tensor(
                    b_t, xs_a(pt), psum[2][:, 0:TC], OP.mult), inc="V")
                P.wait("vector", "Sc", sc_a)
                P.op("vector", lambda e, ccol=ccol: e.tensor_tensor_scan(
                    h_t, a_t, b_t, ccol, OP.mult, OP.add), inc="V")
                prev_scan[0] = P.cnt["V"]
                P.wait("vector", "V")
                P.op("vector", lambda e, ccol=ccol: e.tensor_copy(ccol, h_t[:, TC - 1:TC]))
                P.op("vector", lambda e: e.tensor_tensor(b_t, h_t, psum[3][:, 0:TC], OP.mult))
                if n == 0:
                    P.op("vector", lambda e, pt=pt: e.tensor_copy(y_a(pt), b_t), inc="V")
                else:
                    P.op("vector", lambda e, pt=pt: e.tensor_tensor(
                        y_a(pt), y_a(pt), b_t, OP.add), inc="V")
            banks.release(2, "V")
            banks.release(3, "V")
        scan_done = P.cnt["V"]

        for pt in range(NPT):
            P.op("vector", lambda e, pt=pt: e.scalar_tensor_tensor(
                y_a(pt), u_a(pt), dpar_c[:, pt:pt + 1], y_a(pt), OP.mult, OP.add), inc="V")
        for q in range(4):
            stg = fmA[:, 0:4096] if q % 2 == 0 else fmA[:, 8192:12288]
            P.wait("sync", "V")
            P.wait("sync", "T")
            wrd = P.dma("sync", stg.rearrange("p (c e) -> p c e", c=8),
                        win_t[:, :, 2048 + q * 512: 2048 + (q + 1) * 512])
            P.wait("tensor", "WD", wrd)
            for es4 in range(4):
                pt = q * 4 + es4
                banks.acquire(4)
                for kc in range(NCH):
                    P.op("tensor", lambda e, kc=kc, es4=es4, stg=stg: e.matmul(
                        psum[4][:, 0:TC], stg[:, kc * 512 + es4 * 128: kc * 512 + (es4 + 1) * 128],
                        ln2fm(kc), start=(kc == 0), stop=(kc == NCH - 1)),
                        inc="T" if kc == NCH - 1 else None)
                wr = P.cnt["T"]
                P.wait("scalar", "T", wr)
                P.op("scalar", lambda e: e.activation(
                    out=e_scr[:, 0:TC], in_=psum[4][:, 0:TC], func=AF.Silu), inc="Sc")
                banks.release(4, "Sc")
                P.wait("vector", "Sc")
                P.op("vector", lambda e, pt=pt: e.tensor_tensor(
                    y_a(pt), y_a(pt), e_scr[:, 0:TC], OP.mult), inc="V")
        y_done = P.cnt["V"]

        lnwb = s1[:, 0:2048]
        P.wait("sync", "V", y_done)
        P.wait("sync", "T")
        for wh in range(2):
            if wh == 1:
                P.wait("sync", "T")
            wrd = P.dma("sync", fmA[:, 0:8192].rearrange("p (c o) -> p c o", c=8),
                        wout_t[:, wh * 8:(wh + 1) * 8, :])
            P.wait("tensor", "WD", wrd)
            for t2 in range(2):
                for osp in range(2):
                    bank = 2 + t2 * 2 + osp
                    if wh == 0:
                        banks.acquire(bank)
                    for e8 in range(8):
                        ec = wh * 8 + e8
                        P.op("tensor", lambda e, ec=ec, e8=e8, t2=t2, osp=osp, bank=bank: e.matmul(
                            psum[bank][:, :], y_a(ec)[:, t2 * 128:(t2 + 1) * 128],
                            fmA[:, e8 * 1024 + osp * 512: e8 * 1024 + osp * 512 + 512],
                            start=(ec == 0), stop=(ec == NPT - 1)),
                            inc="T" if (e8 == 7) else None)
        wo_mm = P.cnt["T"]
        brd = P.dma("sync", lnwb, dram["ln_wb"][:, :])
        for t2 in range(2):
            P.wait("vector", "T", wo_mm)
            for osp in range(2):
                bank = 2 + t2 * 2 + osp
                P.op("vector", lambda e, t2=t2, osp=osp, bank=bank: e.tensor_tensor(
                    t_tm[:, osp * 512:(osp + 1) * 512], psum[bank][:, :],
                    ln2tm[:, t2 * 1024 + osp * 512: t2 * 1024 + (osp + 1) * 512],
                    OP.add), inc="V")
                banks.release(bank, "V")
            tile = 2 * ch + t2
            ln_tile(P, sb, t_tm, ln_scr[:, 0:1024], s2[:, 1024:2048])
            if t2 == 0:
                P.wait("vector", "WD", brd)
            P.op("vector", lambda e: e.tensor_tensor(
                ln_scr[:, 0:1024], ln_scr[:, 0:1024], lnwb[:, 0:1024], OP.mult))
            P.op("vector", lambda e: e.tensor_tensor(
                ln_scr[:, 0:1024], ln_scr[:, 0:1024], lnwb[:, 1024:2048], OP.add))
            P.op("vector", lambda e, tile=tile: e.tensor_tensor(
                xres_t(tile), xres_t(tile), ln_scr[:, 0:1024], OP.add), inc="V")

    if stage == "x3":
        store_y(P, dram, x_res)
        return

    lnH = sb["lnH"][:, :]
    lnL = sb["lnL"][:, :]
    upH = sb["upH"][:, :]
    upL = sb["upL"][:, :]
    dnH = sb["dnH"][:, :]
    dnL = sb["dnL"][:, :]
    wrtH = sb["wrtH"][:, :]
    wrtL = sb["wrtL"][:, :]
    stg = wb[:, 2176:3200]
    a16 = sb["act16"][:, :]
    wrt_sb = s1[:, 6400:6528]
    probs = s1[:, 6528:6544]
    rank_a = s1[:, 6544:6560]
    rank_b = s1[:, 6560:6576]
    gate_tm = s1[:, 6576:6592]
    gpark = s1[0:16, 6144:6272]
    gfm_all = s1[32:48, 0:2048]
    act32 = s1[:, 4096:6144]
    mask_buf = cst[:, 896:912]
    ln3tm = s2[:, 2048:4096]
    den = sb["stat"][:, 34:35]
    NSLAB = 2

    P.wait("sync", "V")
    P.wait("sync", "T")
    wrd = P.dma("sync", wrt_sb.rearrange("p (c r) -> p c r", c=NCH),
                dram["wrt_t"].rearrange("(c p) r -> p c r", p=128))
    P.wait("vector", "WD", wrd)
    P.wait("vector", "T")
    P.op("vector", lambda e: e.tensor_copy(wrtH, wrt_sb))
    P.op("vector", lambda e: e.tensor_tensor(wrtL, wrt_sb, wrtH, OP.subtract), inc="V")

    for tile in range(NT):
        P.wait("vector", "T")
        ln_tile(P, sb, xres_t(tile), ln3tm[:, 0:1024], s2[:, 1024:2048])
        P.wait("tensor", "V")
        transpose_tile(P, banks, sb, ln3tm[:, 0:1024],
                       lambda kc, tile=tile: lnH[:, kc * 2048 + tile * 128: kc * 2048 + (tile + 1) * 128],
                       bank_pair=(6, 7), evac_eng="vector",
                       dst_lo_fn=lambda kc, tile=tile: lnL[:, kc * 2048 + tile * 128: kc * 2048 + (tile + 1) * 128])
        ln_done_t = P.cnt["V"]
        banks.acquire(7)
        P.wait("tensor", "V", ln_done_t)
        nmm = 0
        for (aa, bb) in ((lnH, wrtH), (lnH, wrtL), (lnL, wrtH)):
            for kc in range(NCH):
                nmm += 1
                P.op("tensor", lambda e, aa=aa, bb=bb, kc=kc, tile=tile,
                     st=(nmm == 1), sp_=(nmm == 24): e.matmul(
                    psum[7][:, 0:NEXP],
                    aa[:, kc * 2048 + tile * 128: kc * 2048 + (tile + 1) * 128],
                    bb[:, kc * 16:(kc + 1) * 16],
                    start=st, stop=sp_), inc="T" if nmm == 24 else None)
        wr = P.cnt["T"]
        P.wait("scalar", "T", wr)
        P.op("scalar", lambda e: e.activation(
            out=probs, in_=psum[7][:, 0:NEXP], func=AF.Exp, accum_out=den), inc="Sc")
        banks.release(7, "Sc")
        P.wait("vector", "Sc")
        P.op("vector", lambda e: e.reciprocal(den, den), inc="V")
        P.wait("vector", "V")
        P.op("vector", lambda e: e.tensor_scalar(probs, probs, den, None, OP.mult), inc="V")
        P.wait("vector", "V")
        P.op("vector", lambda e: e.tensor_scalar(
            rank_a, probs, probs[:, 0:1], None, OP.is_lt))
        cur, nxt = rank_a, rank_b
        for j in range(1, NEXP):
            P.op("vector", lambda e, j=j, cur=cur, nxt=nxt: e.scalar_tensor_tensor(
                nxt, probs, probs[:, j:j + 1], cur, OP.is_lt, OP.add))
            cur, nxt = nxt, cur
        P.op("vector", lambda e: e.engine_nop(), inc="V")
        P.wait("vector", "V")
        P.op("vector", lambda e, cur=cur: e.tensor_scalar(
            mask_buf, cur, 3.5, None, OP.is_lt), inc="V")
        P.wait("vector", "V")
        P.op("vector", lambda e: e.tensor_tensor(
            gate_tm, probs, mask_buf, OP.mult), inc="V")
        gt = P.cnt["V"]
        banks.acquire(7)
        P.wait("tensor", "V", gt)
        P.op("tensor", lambda e: e.matmul(
            psum[7][0:16, 0:128], gate_tm, cst[:, 0:128],
            is_transpose=True, start=True, stop=True), inc="T")
        wr = P.cnt["T"]
        P.wait("vector", "T", wr)
        P.wait("vector", "GD")
        P.op("vector", lambda e: e.tensor_copy(gpark, psum[7][0:16, 0:128]), inc="V")
        banks.release(7, "V")
        gv = P.cnt["V"]
        P.wait("gpsimd", "V", gv)
        P.dma("gpsimd", gfm_all[:, tile * 128:(tile + 1) * 128], gpark, inc="GD")
    gate_rdy = P.cnt["GD"]
    ln_rdy = P.cnt["V"]
    dbg_tap("lnH0", lnH[:, 0:2048])
    dbg_tap("lnL0", lnL[:, 0:2048])
    P.wait("gpsimd", "GD")
    dbg_tap("gfm", gfm_all)

    roup = dram["roup_t"].rearrange("(c p) e -> p c e", p=128)
    shup = dram["shup_t"].rearrange("(c p) e -> p c e", p=128)
    rodn = dram["rodn_t"].rearrange("(r p) o -> r p o", p=128)
    shdn = dram["shdn_t"].rearrange("(r p) o -> r p o", p=128)
    NEX = NEXP + NSH

    def slab(g):
        s = (g % NSLAB) * 512
        return stg[:, s:s + 512]

    def slab_src(ex, k):
        if k < 8:
            if ex < NEXP:
                return roup[:, k:k + 1, ex * EDIM:(ex + 1) * EDIM].rearrange("p c e -> p (c e)")
            return shup[:, k:k + 1, (ex - NEXP) * EDIM:(ex - NEXP + 1) * EDIM].rearrange("p c e -> p (c e)")
        r, osp = (k - 8) // 2, (k - 8) % 2
        rr = (ex * 4 + r) if ex < NEXP else ((ex - NEXP) * 4 + r)
        srcd = rodn if ex < NEXP else shdn
        return srcd[rr:rr + 1, :, osp * 512:(osp + 1) * 512].rearrange("r p o -> p (r o)")

    def slab_dst(k):
        if k < 8:
            return (upH[:, k * 512:(k + 1) * 512], upL[:, k * 512:(k + 1) * 512])
        r, osp = (k - 8) // 2, (k - 8) % 2
        o0 = r * 1024 + osp * 512
        return (dnH[:, o0:o0 + 512], dnL[:, o0:o0 + 512])

    convV = {}

    def emit_stage(ex, ks, tgate):
        for k in ks:
            g = ex * 16 + k
            if g - NSLAB in convV:
                P.wait("sync", "V", convV[g - NSLAB])
            wd = P.dma("sync", slab(g), slab_src(ex, k))
            P.wait("vector", "WD", wd)
            if tgate:
                P.wait("vector", "T", tgate)
            hi, lo = slab_dst(k)
            P.op("vector", lambda e, hi=hi, g=g: e.tensor_copy(hi, slab(g)))
            P.op("vector", lambda e, hi=hi, lo=lo, g=g: e.tensor_tensor(
                lo, slab(g), hi, OP.subtract), inc="V")
            convV[g] = P.cnt["V"]

    emit_stage(0, range(16), P.cnt["T"])
    dbg_tap("upH0", upH)
    dbg_tap("dnH0", dnH)
    dbg_tap("stg0", stg)
    up_conv_last = {0: convV[7]}
    dn_conv_last = {0: convV[15]}
    act_guard = [0] * 4
    act_vcnt = [0] * 4
    P.wait("tensor", "GD", gate_rdy)
    for ex in range(NEX):
        routed = ex < NEXP
        up_T = 0
        for sp in range(4):
            if routed:
                banks.acquire(6)
                P.op("tensor", lambda e, ex=ex, sp=sp: e.matmul(
                    psum[6][:, 0:512], rowa[32:48, ex * 128:(ex + 1) * 128],
                    gfm_all[:, sp * 512:(sp + 1) * 512],
                    start=True, stop=True), inc="T")
                gb = P.cnt["T"]
            for et in range(4):
                ub = 4 + (et % 2)
                banks.acquire(ub)
                if sp == 0 and et == 0:
                    P.wait("tensor", "V", max(ln_rdy, up_conv_last[ex]))
                nmm = 0
                for (aa, bb) in ((upH, lnH), (upH, lnL), (upL, lnH)):
                    for kc in range(NCH):
                        nmm += 1
                        P.op("tensor", lambda e, aa=aa, bb=bb, kc=kc, et=et, sp=sp, ub=ub,
                             st=(nmm == 1), sp_=(nmm == 24): e.matmul(
                            psum[ub][:, 0:512],
                            aa[:, kc * 512 + et * 128: kc * 512 + et * 128 + 128],
                            bb[:, kc * 2048 + sp * 512: kc * 2048 + (sp + 1) * 512],
                            start=st, stop=sp_), inc="T" if nmm == 24 else None)
                if sp == 3 and et == 3:
                    up_T = P.cnt["T"]
                wr = P.cnt["T"]
                P.wait("scalar", "T", wr)
                P.wait("scalar", "V", act_vcnt[et])
                P.op("scalar", lambda e, et=et, ub=ub: e.activation(
                    out=act32[:, et * 512:(et + 1) * 512], in_=psum[ub][:, 0:512],
                    func=AF.Gelu), inc="Sc")
                banks.release(ub, "Sc")
                sc = P.cnt["Sc"]
                P.wait("vector", "Sc", sc)
                P.wait("vector", "T", act_guard[et])
                a32 = act32[:, et * 512:(et + 1) * 512]
                ahi = a16[:, et * 512:(et + 1) * 512]
                alo = a16[:, 2048 + et * 512: 2048 + (et + 1) * 512]
                if routed:
                    P.wait("vector", "T", gb)
                    P.op("vector", lambda e, a32=a32: e.tensor_tensor(
                        a32, a32, psum[6][:, 0:512], OP.mult))
                P.op("vector", lambda e, a32=a32, ahi=ahi: e.tensor_copy(ahi, a32))
                P.op("vector", lambda e, a32=a32, ahi=ahi, alo=alo: e.tensor_tensor(
                    alo, a32, ahi, OP.subtract), inc="V")
                act_vcnt[et] = P.cnt["V"]
            if routed:
                banks.release(6, "V")
            if ex == 0 and sp == 0:
                dbg_tap("act0", a16[:, 0:2048])
            if sp == 3 and ex + 1 < NEX:
                emit_stage(ex + 1, range(8), up_T)
                up_conv_last[ex + 1] = convV[(ex + 1) * 16 + 7]
            P.wait("tensor", "V", act_vcnt[3])
            if sp == 0:
                P.wait("tensor", "V", dn_conv_last[ex])
            for mt in range(4):
                tile = sp * 4 + mt
                for osp in range(2):
                    db = (mt * 2 + osp) % 4
                    banks.acquire(db)
                    nmm = 0
                    for (aa, bb) in (("h", "h"), ("h", "l"), ("l", "h")):
                        for et in range(4):
                            nmm += 1
                            ao = (0 if aa == "h" else 2048) + et * 512 + mt * 128
                            asrc = a16[:, ao:ao + 128]
                            bsrc = (dnH if bb == "h" else dnL)[
                                :, et * 1024 + osp * 512: et * 1024 + osp * 512 + 512]
                            P.op("tensor", lambda e, asrc=asrc, bsrc=bsrc, db=db,
                                 st=(nmm == 1), sp_=(nmm == 12): e.matmul(
                                psum[db][:, :], asrc, bsrc, start=st, stop=sp_),
                                inc="T" if nmm == 12 else None)
                    wr = P.cnt["T"]
                    P.wait("vector", "T", wr)
                    P.op("vector", lambda e, tile=tile, osp=osp, db=db: e.tensor_tensor(
                        xres_t(tile)[:, osp * 512:(osp + 1) * 512],
                        xres_t(tile)[:, osp * 512:(osp + 1) * 512],
                        psum[db][:, :], OP.add), inc="V")
                    banks.release(db, "V")
            for et in range(4):
                act_guard[et] = P.cnt["T"]
        if ex + 1 < NEX:
            emit_stage(ex + 1, range(8, 16), P.cnt["T"])
            dn_conv_last[ex + 1] = convV[(ex + 1) * 16 + 15]

    store_y(P, dram, x_res)


def store_y(P, dram, x_res):
    for e_ in ("V", "T", "Sc", "WD"):
        P.wait("gpsimd", e_)
    yo = dram["y"].rearrange("(t p) d -> t p d", p=128)
    for g in range(4):
        P.dma("gpsimd", yo[g * 4:(g + 1) * 4, :, :].rearrange("t p d -> p t d"),
              x_res[:, g * 4096:(g + 1) * 4096].rearrange("p (t d) -> p t d", t=4), inc="St")
    P.wait("gpsimd", "St")



def _consts_array(ins):
    c = np.zeros((128, 1024), np.float32)
    c[:, 0:128] = np.eye(128, dtype=np.float32)
    tri = (np.arange(128)[None, :] >= np.arange(128)[:, None]).astype(np.float32)
    c[:, 128:640] = np.tile(tri, (1, 4))
    return c


def _rows_array(ins):
    r = np.zeros((48, 2048), np.float32)
    for k in range(16):
        r[32 + k, k * 128:(k + 1) * 128] = 1.0
    r[0, 0:128] = 1.0
    r[0, 128:1152] = np.asarray(ins["L0_attn_bo"], np.float32)
    r[1, 0:1024] = np.asarray(ins["L0_attn_bq"], np.float32)
    r[2, 0:1024] = np.asarray(ins["L0_attn_bk"], np.float32)
    r[3, 0:1024] = np.asarray(ins["L0_attn_bv"], np.float32)
    r[4, 0:1024] = np.asarray(ins["L0_ffn_bias"], np.float32)
    return r


def _make_in_maps(ins):
    x = np.ascontiguousarray(np.asarray(ins["L0_attn_x"], np.float32))

    def T(a):
        return np.ascontiguousarray(np.asarray(a, np.float32).T)

    def colt(v, nch):
        return np.ascontiguousarray(np.asarray(v, np.float32).reshape(nch, 128).T)

    shup = np.asarray(ins["L1_ffn_shared_up"], np.float32)
    shdn = np.asarray(ins["L1_ffn_shared_down"], np.float32)
    roup = np.asarray(ins["L1_ffn_routed_up"], np.float32)
    rodn = np.asarray(ins["L1_ffn_routed_down"], np.float32)

    common = {
        "wq": np.ascontiguousarray(np.asarray(ins["L0_attn_wq"], np.float32)),
        "wk": np.ascontiguousarray(np.asarray(ins["L0_attn_wk"], np.float32)),
        "wv": np.ascontiguousarray(np.asarray(ins["L0_attn_wv"], np.float32)),
        "wo": np.ascontiguousarray(np.asarray(ins["L0_attn_wo"], np.float32)),
        "bq_c": colt(ins["L0_attn_bq"], NCH),
        "bk_c": colt(ins["L0_attn_bk"], NCH),
        "rows_c": _rows_array(ins),
        "w_ffn0_t": T(ins["L0_ffn_weight"]),
        "w_in_t": T(ins["L1_attn_W_in"]),
        "conv_w_c": np.ascontiguousarray(
            np.asarray(ins["L1_attn_conv_w"], np.float32)[:, 0, :]
            .reshape(DI // 128, 128, 4).transpose(1, 0, 2).reshape(128, -1)),
        "conv_b_c": colt(ins["L1_attn_conv_b"], DI // 128),
        "w_x_t": T(ins["L1_attn_W_x"]),
        "w_dt_t": T(ins["L1_attn_W_dt"]),
        "b_dt_c": colt(ins["L1_attn_b_dt"], DI // 128),
        "a_log_c": np.ascontiguousarray(
            np.asarray(ins["L1_attn_A_log"], np.float32)
            .reshape(DI // 128, 128, DSTATE).transpose(1, 0, 2).reshape(128, -1)),
        "d_par_c": colt(ins["L1_attn_D_param"], DI // 128),
        "w_out_t": T(ins["L1_attn_W_out"]),
        "ln_wb": np.ascontiguousarray(np.concatenate([
            np.tile(np.asarray(ins["L1_attn_ln_w"], np.float32)[None, :], (128, 1)),
            np.tile(np.asarray(ins["L1_attn_ln_b"], np.float32)[None, :], (128, 1))], axis=1)),
        "shup_t": np.ascontiguousarray(shup.reshape(NSH * EDIM, D).T),
        "shdn_t": np.ascontiguousarray(shdn.transpose(0, 2, 1).reshape(NSH * EDIM, D)),
        "roup_t": np.ascontiguousarray(roup.reshape(NEXP * EDIM, D).T),
        "rodn_t": np.ascontiguousarray(rodn.transpose(0, 2, 1).reshape(NEXP * EDIM, D)),
        "wrt_t": T(ins["L1_ffn_router_w"]),
        "consts": _consts_array(ins),
    }
    maps = []
    for c in range(8):
        m = dict(common)
        m["x_own"] = np.ascontiguousarray(x[c])
        m["x_all"] = np.ascontiguousarray(x[[j for j in range(8) if j != c]])
        maps.append(m)
    return maps


def kernel(**inputs):
    ins = {k: np.asarray(v) for k, v in inputs.items()}
    nc, _ = build_program()
    res = run_bass_kernel_spmd(nc, _make_in_maps(ins), core_ids=list(range(8)))
    return np.stack([np.asarray(res.results[i]["y"]) for i in range(8)], axis=0)


# version 05
# speedup vs baseline: 1.2144x; 1.0205x over previous
import contextlib

import numpy as np

import concourse.bass as bass
import concourse.mybir as mybir
from concourse.bass_utils import run_bass_kernel_spmd

F32 = mybir.dt.float32
BF16 = mybir.dt.bfloat16
F16 = mybir.dt.float16
OP = mybir.AluOpType
AF = mybir.ActivationFunctionType
AX = mybir.AxisListType

B, S, D = 8, 2048, 1024
H, DK = 16, 64
NT = S // 128
NCH = D // 128
DI = 2048
DSTATE = 16
DTR = 64
NEXP, EDIM = 16, 512
NSH = 2
EPS = 1e-5

ENGINES = ("sync", "tensor", "vector", "scalar", "gpsimd")


class Prog:
    def __init__(self, nc, sems):
        self.nc = nc
        self.sems = sems
        self.ops = {e: [] for e in ENGINES}
        self.cnt = {name: 0 for name in sems}

    def op(self, eng, fn, inc=None, n=1):
        if inc is None:
            self.ops[eng].append(fn)
        else:
            sem = self.sems[inc]

            def g(e, fn=fn, sem=sem, n=n):
                fn(e).then_inc(sem, n)

            self.ops[eng].append(g)
            self.cnt[inc] += n
        return self.cnt[inc] if inc else None

    def wait(self, eng, sem, val=None):
        v = self.cnt[sem] if val is None else val
        s = self.sems[sem]
        self.ops[eng].append(lambda e, s=s, v=v: e.wait_ge(s, v))

    def dma(self, q, out, in_, inc=None):
        inc = inc or {"sync": "WD", "gpsimd": "GD", "scalar": "SD"}[q]
        self.op(q, lambda e, out=out, in_=in_: e.dma_start(out=out, in_=in_),
                inc=inc, n=16)
        return self.cnt[inc]

    def mark(self, eng, sem):
        if eng in ("vector", "gpsimd"):
            self.op(eng, lambda e: e.engine_nop(), inc=sem)
        else:
            self.op(eng, lambda e: e.nop(), inc=sem)
        return self.cnt[sem]


class Banks:

    def __init__(self, P):
        self.P = P
        self.state = [None] * 8

    def acquire(self, i):
        st = self.state[i]
        if st is not None:
            self.P.wait("tensor", st[0], st[1])

    def release(self, i, sem):
        self.state[i] = (sem, self.P.cnt[sem])



def build_program(debug_taps=(), stage=None):
    nc = bass.Bass("TRN2", target_bir_lowering=False, detect_race_conditions=False)
    dram = {}

    def din(name, shape, dt=F32):
        dram[name] = nc.dram_tensor(name, list(shape), dt, kind="ExternalInput").ap()

    def dout(name, shape, dt=F32):
        dram[name] = nc.dram_tensor(name, list(shape), dt, kind="ExternalOutput").ap()

    din("x_own", [S, D])
    din("x_all", [B - 1, S, D])
    din("wq", [D, D]); din("wk", [D, D]); din("wv", [D, D]); din("wo", [D, D])
    din("bq_c", [128, NCH]); din("bk_c", [128, NCH])
    din("rows_c", [48, 2048])
    din("w_ffn0_t", [D, D])
    din("w_in_t", [D, 2 * DI])
    din("conv_w_c", [128, (DI // 128) * 4])
    din("conv_b_c", [128, DI // 128])
    din("w_x_t", [DI, 96])
    din("w_dt_t", [DTR, DI])
    din("b_dt_c", [128, DI // 128])
    din("a_log_c", [128, (DI // 128) * DSTATE])
    din("d_par_c", [128, DI // 128])
    din("w_out_t", [DI, D])
    din("ln_wb", [128, 2 * D])
    din("shup_t", [D, NSH * EDIM]); din("shdn_t", [NSH * EDIM, D])
    din("roup_t", [D, NEXP * EDIM]); din("rodn_t", [NEXP * EDIM, D])
    din("wrt_t", [D, NEXP])
    din("consts", [128, 1024])
    dout("y", [S, D])

    dbg_names = []
    es = contextlib.ExitStack()
    with es:
        sb = {}
        sb["x_res"] = es.enter_context(nc.sbuf_tensor("x_res", [128, NT * 1024], F32))
        sb["fmA"] = es.enter_context(nc.sbuf_tensor("fmA", [128, NT * 1024], F32))
        sb["s1"] = es.enter_context(nc.sbuf_tensor("s1", [128, 6912], F32))
        sb["s2"] = es.enter_context(nc.sbuf_tensor("s2", [128, 6144], F32))
        sb["wb"] = es.enter_context(nc.sbuf_tensor("wb", [128, 3584], F32))
        sb["cst"] = es.enter_context(nc.sbuf_tensor("cst", [128, 1024], F32))
        sb["rowa"] = es.enter_context(nc.sbuf_tensor("rowa", [128, 2048], F32))
        sb["stat"] = es.enter_context(nc.sbuf_tensor("stat", [128, 128], F32))
        psum = [es.enter_context(nc.psum_tensor(f"pb{i}", [128, 512], F32)) for i in range(8)]
        sb["psum"] = psum

        def _alias(name, shape, dtype, arena, byte_off):
            return nc.alloc_sbuf_tensor_at(
                name, shape, dtype, offset=nc.lookup_mloc(sb[arena]).addr + byte_off)
        sb["lnj16"] = _alias("lnj16", [128, 16384], F16, "fmA", 0)
        sb["att16"] = _alias("att16", [128, 16384], F16, "fmA", 32768)
        sb["kq16"] = _alias("kq16", [128, 4096], F16, "s1", 0)
        sb["v16"] = _alias("v16", [128, 2080], F16, "s1", 8192)
        sb["er16"] = _alias("er16", [128, 4096], F16, "s1", 12544)
        sb["w16"] = _alias("w16", [128, 4096], F16, "wb", 4096)
        sb["mask16"] = _alias("mask16", [128, 512], F16, "cst", 2560)
        sb["wqB"] = _alias("wqB", [128, 8192], BF16, "s2", 0)
        sb["xqB"] = _alias("xqB", [128, 4096], BF16, "s1", 0)
        sb["lnH"] = _alias("lnH", [128, 16384], BF16, "fmA", 0)
        sb["lnL"] = _alias("lnL", [128, 16384], BF16, "fmA", 32768)
        sb["upH"] = _alias("upH", [128, 4096], BF16, "s2", 0)
        sb["upL"] = _alias("upL", [128, 4096], BF16, "s2", 8192)
        sb["dnH"] = _alias("dnH", [128, 4096], BF16, "s2", 16384)
        sb["dnL"] = _alias("dnL", [128, 4096], BF16, "wb", 0)
        sb["wrtH"] = _alias("wrtH", [128, 128], BF16, "wb", 8192)
        sb["wrtL"] = _alias("wrtL", [128, 128], BF16, "wb", 8448)
        sb["act16"] = _alias("act16", [128, 4096], BF16, "s1", 8192)

        sems = {}
        for name in ("WD", "GD", "SD", "T", "V", "Sc", "G", "St"):
            sems[name] = es.enter_context(nc.semaphore(name))
        P = Prog(nc, sems)
        banks = Banks(P)

        def dbg_tap(name, ap):
            if name in debug_taps:
                dout("dbg_" + name, [ap.shape[0], ap.shape[-1]])
                dbg_names.append(name)
                for e_ in ("V", "T", "Sc"):
                    P.wait("gpsimd", e_)
                P.dma("gpsimd", dram["dbg_" + name][:, :], ap, inc="St")
                P.wait("vector", "St")
                P.wait("scalar", "St")

        emit_all(P, banks, dram, sb, dbg_tap, stage)

        block = es.enter_context(nc.Block())
        if True:
            @block.sync
            def _(eng):
                for f in P.ops["sync"]:
                    f(eng)

            @block.tensor
            def _(eng):
                for f in P.ops["tensor"]:
                    f(eng)

            @block.vector
            def _(eng):
                for f in P.ops["vector"]:
                    f(eng)

            @block.scalar
            def _(eng):
                for f in P.ops["scalar"]:
                    f(eng)

            @block.gpsimd
            def _(eng):
                for f in P.ops["gpsimd"]:
                    f(eng)

    return nc, dbg_names



def vfence(P):
    P.mark("vector", "V")
    P.wait("vector", "V")


def ln_tile(P, sb, src_ap, dst_ap, scratch_ap, gamma=None, beta=None):
    stat = sb["stat"]
    mean = stat[:, 0:1]
    vsum = stat[:, 1:2]
    rs = stat[:, 2:3]
    W = src_ap.shape[-1]
    P.op("vector", lambda e: e.tensor_reduce(mean, src_ap, axis=AX.X, op=OP.add), inc="V")
    P.wait("vector", "V")
    P.op("vector", lambda e: e.tensor_scalar(mean, mean, 1.0 / W, None, OP.mult), inc="V")
    P.wait("vector", "V")
    P.op("vector", lambda e: e.tensor_scalar(dst_ap, src_ap, mean, None, OP.subtract))
    P.op("vector", lambda e: e.scalar_tensor_tensor(
        scratch_ap, dst_ap, 1.0, dst_ap, OP.mult, OP.mult, accum_out=vsum), inc="V")
    P.wait("scalar", "V")
    P.op("scalar", lambda e: e.activation(out=rs, in_=vsum, func=AF.Sqrt,
                                          bias=stat[:, 3:4], scale=1.0 / W), inc="Sc")
    P.wait("vector", "Sc")
    P.op("vector", lambda e: e.reciprocal(rs, rs), inc="V")
    P.wait("vector", "V")
    last_ops = [lambda e: e.tensor_scalar(dst_ap, dst_ap, rs, None, OP.mult)]
    if gamma is not None:
        last_ops.append(lambda e: e.tensor_scalar(dst_ap, dst_ap, gamma, None, OP.mult))
    if beta is not None:
        last_ops.append(lambda e: e.tensor_scalar(dst_ap, dst_ap, beta, None, OP.add))
    for i, f in enumerate(last_ops):
        P.op("vector", f, inc="V" if i == len(last_ops) - 1 else None)
    return P.cnt["V"]


def ln_tile_c(P, sb, src_ap, dst_ap, scratch_ap, mean, vsum, rs):
    W = 1024
    P.op("vector", lambda e: e.tensor_reduce(mean, src_ap, axis=AX.X, op=OP.add), inc="V")
    P.wait("vector", "V")
    P.op("vector", lambda e: e.tensor_scalar(mean, mean, 1.0 / W, None, OP.mult), inc="V")
    P.wait("vector", "V")
    P.op("vector", lambda e: e.tensor_scalar(dst_ap, src_ap, mean, None, OP.subtract))
    P.op("vector", lambda e: e.scalar_tensor_tensor(
        scratch_ap, dst_ap, 1.0, dst_ap, OP.mult, OP.mult, accum_out=vsum), inc="V")
    P.wait("scalar", "V")
    P.op("scalar", lambda e: e.activation(out=rs, in_=vsum, func=AF.Sqrt,
                                          bias=sb["stat"][:, 3:4], scale=1.0 / W), inc="Sc")
    P.wait("vector", "Sc")
    P.op("vector", lambda e: e.reciprocal(rs, rs), inc="V")
    P.wait("vector", "V")
    P.op("vector", lambda e: e.tensor_scalar(dst_ap, dst_ap, rs, None, OP.mult), inc="V")
    return P.cnt["V"]


def transpose_tile(P, banks, sb, src_ap_1024, dst_fn, bank_pair=(6, 7),
                   evac_eng="scalar", out_scale=None, ident=None, dst_lo_fn=None):
    psum = sb["psum"]
    if ident is None:
        ident = sb["cst"][:, 0:128]
    is_16 = (ident.dtype in (BF16, F16))

    def pslot(bank, j):
        if is_16:
            return psum[bank][:, j * 64:(j + 1) * 64].bitcast(ident.dtype)
        return psum[bank][:, j * 128:(j + 1) * 128]

    sem = {"scalar": "Sc", "vector": "V"}[evac_eng]
    for half, bank in ((0, bank_pair[0]), (1, bank_pair[1])):
        banks.acquire(bank)
        for j in range(4):
            kc = half * 4 + j
            P.op("tensor",
                 lambda e, kc=kc, j=j, bank=bank: e.matmul(
                     pslot(bank, j),
                     src_ap_1024[:, kc * 128:(kc + 1) * 128], ident,
                     is_transpose=True, start=True, stop=True),
                 inc="T" if j == 3 else None)
        wr = P.cnt["T"]
        P.wait(evac_eng, "T", wr)
        for j in range(4):
            kc = half * 4 + j
            if evac_eng == "scalar":
                if out_scale is None:
                    P.op("scalar", lambda e, kc=kc, j=j, bank=bank: e.copy(
                        out=dst_fn(kc), in_=pslot(bank, j)),
                        inc=sem if j == 3 else None)
                else:
                    P.op("scalar", lambda e, kc=kc, j=j, bank=bank: e.activation(
                        out=dst_fn(kc), in_=pslot(bank, j),
                        func=AF.Copy, scale=out_scale), inc=sem if j == 3 else None)
            else:
                P.op("vector", lambda e, kc=kc, j=j, bank=bank: e.tensor_copy(
                    dst_fn(kc), pslot(bank, j)),
                    inc=None if dst_lo_fn is not None else (sem if j == 3 else None))
                if dst_lo_fn is not None:
                    P.op("vector", lambda e, kc=kc, j=j, bank=bank: e.tensor_tensor(
                        dst_lo_fn(kc), pslot(bank, j), dst_fn(kc), OP.subtract),
                        inc=sem if j == 3 else None)
        banks.release(bank, sem)
    return P.cnt[sem]




def scalar_finalize(P, banks, sb, col_ap, out_row_ap, op=OP.max):
    psum = sb["psum"]
    ident = sb["cst"][:, 0:128]
    banks.acquire(6)
    P.wait("tensor", "V")
    P.op("tensor", lambda e: e.matmul(psum[6][0:1, 0:128], col_ap, ident,
                                      is_transpose=True, start=True, stop=True), inc="T")
    wr = P.cnt["T"]
    P.wait("vector", "T", wr)
    row = sb["rowa"][0:1, 1536:1664]
    P.op("vector", lambda e: e.tensor_copy(row, psum[6][0:1, 0:128]), inc="V")
    banks.release(6, "V")
    P.wait("vector", "V")
    P.op("vector", lambda e: e.tensor_reduce(out_row_ap, row, axis=AX.X, op=op), inc="V")
    P.wait("vector", "V")


def bcast_p0(P, banks, sb, src_row_ap, dst_ap, n):
    psum = sb["psum"]
    ones_row = sb["rowa"][0:1, 0:128]
    banks.acquire(6)
    P.wait("tensor", "V")
    P.op("tensor", lambda e: e.matmul(psum[6][:, 0:n], ones_row, src_row_ap,
                                      start=True, stop=True), inc="T")
    wr = P.cnt["T"]
    P.wait("vector", "T", wr)
    P.op("vector", lambda e: e.tensor_copy(dst_ap, psum[6][:, 0:n]), inc="V")
    banks.release(6, "V")
    P.wait("vector", "V")


def quant_grid_tile(P, sb, y_ap, out_ap, scr, c4col, mcol):
    c, d_, f1, msk = scr
    P.op("vector", lambda e: e.tensor_scalar(c, y_ap, c4col, None, OP.mult))
    P.op("vector", lambda e: e.tensor_tensor(d_, c, y_ap, OP.subtract))
    P.op("vector", lambda e: e.tensor_tensor(c, c, d_, OP.subtract))
    P.op("vector", lambda e: e.tensor_scalar(f1, y_ap, mcol, None, OP.add))
    P.op("vector", lambda e: e.tensor_scalar(f1, f1, mcol, None, OP.subtract))
    P.op("vector", lambda e: e.tensor_tensor(msk, y_ap, y_ap, OP.mult))
    P.op("vector", lambda e: e.tensor_scalar(msk, msk, 2.0 ** -12, None, OP.is_ge))
    P.op("vector", lambda e: e.tensor_tensor(d_, c, f1, OP.subtract))
    P.op("vector", lambda e: e.tensor_tensor(d_, d_, msk, OP.mult))
    P.op("vector", lambda e: e.tensor_tensor(out_ap, d_, f1, OP.add), inc="V")



def emit_all(P, banks, dram, sb, dbg_tap, stage=None):
    psum = sb["psum"]
    x_res = sb["x_res"]
    fmA = sb["fmA"]
    s1 = sb["s1"]
    s2 = sb["s2"]
    wb = sb["wb"]
    cst = sb["cst"]
    rowa = sb["rowa"]

    def xres_t(tt):
        return x_res[:, tt * 1024:(tt + 1) * 1024]

    def fm_ap(kc, tt):
        return fmA[:, kc * 2048 + tt * 128: kc * 2048 + (tt + 1) * 128]

    def fm_span(kc, q0, w):
        return fmA[:, kc * 2048 + q0: kc * 2048 + q0 + w]

    ones_row = rowa[0:1, 0:128]
    bias_row = rowa[0:1, 128:1152]
    bv_row = rowa[0:1, 1408:1536]
    recip_row = rowa[0:1, 1536:2048]
    maskrep = cst[:, 128:640]

    xo = dram["x_own"].rearrange("(t p) d -> t p d", p=128)
    for g in range(4):
        P.dma("sync", x_res[:, g * 4096:(g + 1) * 4096].rearrange("p (t d) -> p t d", t=4),
              xo[g * 4:(g + 1) * 4, :, :].rearrange("t p d -> p t d"))
    P.dma("sync", cst[:, :], dram["consts"][:, :])
    P.dma("sync", rowa[0:1, 0:1152], dram["rows_c"][0:1, 0:1152])
    P.dma("sync", rowa[32:48, :], dram["rows_c"][32:48, :])
    P.dma("sync", sb["stat"][:, 8:8 + NCH], dram["bq_c"][:, :])
    P.dma("sync", sb["stat"][:, 16:16 + NCH], dram["bk_c"][:, :])
    ldx = P.cnt["WD"]
    if stage == "x0":
        store_y(P, dram, x_res)
        return
    P.op("vector", lambda e: e.memset(sb["stat"][:, :], 0.0))
    P.op("vector", lambda e: e.memset(sb["stat"][:, 3:4], EPS))
    P.wait("vector", "WD", ldx)

    ln_tmp = s2[:, 0:1024]
    ln_sq = s2[:, 1024:2048]
    P.wait("vector", "WD", ldx)
    P.wait("tensor", "WD", ldx)
    for tt in range(NT):
        if tt > 0:
            P.wait("vector", "Sc")
        ln_tile(P, sb, xres_t(tt), ln_tmp, ln_sq)
        if stage == "ln0_notr":
            P.op("vector", lambda e, tt=tt: e.tensor_copy(
                fmA[:, tt * 1024:(tt + 1) * 1024], ln_tmp), inc="V")
            P.mark("scalar", "Sc")
            continue
        P.wait("tensor", "V")
        transpose_tile(P, banks, sb, ln_tmp, lambda kc, tt=tt: fm_ap(kc, tt))
    fm_done = P.cnt["Sc"]
    dbg_tap("ln0_fm", fmA[:, :])
    if stage in ("ln0", "ln0_notr"):
        store_y(P, dram, x_res)
        return

    k_pair = s1[:, 0:2048]
    q_pair = s1[:, 2048:4096]
    v_pair = s1[:, 4096:4096 + 16 * 130]
    rb = s1[0:64, 6176:6688]
    e_ring = s2[:, 0:4096]
    att_fm = s2[:, 4096:6144]

    wqs = wb[:, 0:1024]
    wks = wb[:, 1024:2048]
    wvs = wb[:, 2048:3072]

    def wslice_dma(dst, wname, hg):
        src = dram[wname][:, hg * 128:(hg + 1) * 128].rearrange("(c p) o -> p c o", p=128)
        return P.dma("sync", dst.rearrange("p (c o) -> p c o", c=NCH), src)

    last_hg_T = None
    for hg in range(8):
        if last_hg_T is not None:
            P.wait("sync", "T", last_hg_T)
        wslice_dma(wqs, "wq", hg)
        wslice_dma(wks, "wk", hg)
        P.dma("sync", bv_row, dram["rows_c"][3:4, hg * 128:(hg + 1) * 128])
        w_ready_kq = P.cnt["WD"]
        wslice_dma(wvs, "wv", hg)
        w_ready_v = P.cnt["WD"]

        P.wait("tensor", "WD", w_ready_kq)
        P.wait("tensor", "Sc", fm_done)

        def proj_fm(wsl, bcol, dst, scale, bank):
            for sp in range(4):
                banks.acquire(bank)
                for kc in range(NCH):
                    P.op("tensor", lambda e, kc=kc, sp=sp, wsl=wsl, bank=bank: e.matmul(
                        psum[bank][:, :], wsl[:, kc * 128:(kc + 1) * 128],
                        fm_span(kc, sp * 512, 512), start=(kc == 0),
                        stop=(kc == NCH - 1)), inc="T" if kc == NCH - 1 else None)
                wr = P.cnt["T"]
                P.wait("vector", "T", wr)
                if scale == 1.0:
                    P.op("vector", lambda e, sp=sp, dst=dst, bank=bank, bcol=bcol: e.tensor_scalar(
                        dst[:, sp * 512:(sp + 1) * 512], psum[bank][:, :],
                        bcol, None, OP.add), inc="V")
                else:
                    P.op("vector", lambda e, sp=sp, dst=dst, scale=scale, bank=bank, bcol=bcol: e.tensor_scalar(
                        dst[:, sp * 512:(sp + 1) * 512], psum[bank][:, :],
                        bcol, scale, OP.add, OP.mult), inc="V")
                banks.release(bank, "V")

        proj_fm(wks, sb["stat"][:, 16 + hg:17 + hg], k_pair, 1.0, 0)
        proj_fm(wqs, sb["stat"][:, 8 + hg:9 + hg], q_pair, 0.125, 1)

        P.wait("tensor", "WD", w_ready_v)
        for tt in range(NT):
            banks.acquire(2)
            for kc in range(NCH):
                P.op("tensor", lambda e, kc=kc, tt=tt: e.matmul(
                    psum[2][:, 0:128], fm_ap(kc, tt), wvs[:, kc * 128:(kc + 1) * 128],
                    start=(kc == 0), stop=False))
            P.op("tensor", lambda e: e.matmul(
                psum[2][:, 0:128], ones_row, bv_row,
                start=False, stop=True), inc="T")
            wr = P.cnt["T"]
            P.wait("vector", "T", wr)
            P.op("vector", lambda e, tt=tt: e.tensor_copy(
                v_pair[:, tt * 130:(tt + 1) * 130].rearrange("p (h c) -> p h c", h=2)[:, :, 0:64],
                psum[2][:, 0:128].rearrange("p (h c) -> p h c", h=2)), inc="V")
            banks.release(2, "V")
        P.op("vector", lambda e: e.memset(
            v_pair.rearrange("p (t h c) -> p t h c", t=16, h=2)[:, :, :, 64:65], 1.0),
            inc="V")
        v_done = P.cnt["V"]

        P.wait("tensor", "V")
        slot_guard = [0] * 8
        for h2 in range(2):
            hr0 = 64 * h2
            pv_bank = 3 + h2
            sc_bank = 0 + h2
            for sp in range(4):
                banks.acquire(pv_bank)
                nkc = 4 * (sp + 1)
                for kc in range(nkc):
                    is_diag = kc >= 4 * sp
                    j = kc - 4 * sp
                    q0 = 128 * j if is_diag else 0
                    qw = 512 - q0
                    slot = (4 + j) if is_diag else (kc % 2)
                    banks.acquire(sc_bank)
                    P.op("tensor", lambda e, kc=kc, sp=sp, hr0=hr0, q0=q0, qw=qw, b=sc_bank: e.matmul(
                        psum[b][:, 0:qw],
                        k_pair[hr0:hr0 + 64, kc * 128:(kc + 1) * 128],
                        q_pair[hr0:hr0 + 64, sp * 512 + q0: sp * 512 + q0 + qw],
                        start=True, stop=True), inc="T")
                    wr = P.cnt["T"]
                    P.wait("scalar", "T", max(wr, slot_guard[slot]))
                    P.op("scalar", lambda e, slot=slot, qw=qw, b=sc_bank: e.activation(
                        out=e_ring[:, slot * 512: slot * 512 + qw],
                        in_=psum[b][:, 0:qw], func=AF.Exp), inc="Sc")
                    banks.release(sc_bank, "Sc")
                    if not is_diag:
                        P.wait("tensor", "Sc")
                        if kc == 0:
                            P.wait("tensor", "V", v_done)
                        P.op("tensor", lambda e, kc=kc, h2=h2, pb=pv_bank, slot=slot: e.matmul(
                            psum[pb][0:65, 0:512],
                            v_pair[:, kc * 130 + 65 * h2: kc * 130 + 65 * h2 + 65],
                            e_ring[:, slot * 512: slot * 512 + 512],
                            start=(kc == 0), stop=False), inc="T")
                        slot_guard[slot] = P.cnt["T"]
                P.wait("vector", "Sc")
                P.op("vector", lambda e: e.tensor_tensor(
                    e_ring[:, 4 * 512:8 * 512].rearrange("p (s c) -> p s c", s=4)[:, :, 0:128],
                    e_ring[:, 4 * 512:8 * 512].rearrange("p (s c) -> p s c", s=4)[:, :, 0:128],
                    maskrep.rearrange("p (s c) -> p s c", s=4), OP.mult), inc="V")
                mk = P.cnt["V"]
                P.wait("tensor", "V", mk)
                if sp == 0:
                    P.wait("tensor", "V", v_done)
                for j in range(4):
                    kc = 4 * sp + j
                    q0 = 128 * j
                    P.op("tensor", lambda e, j=j, kc=kc, q0=q0, h2=h2, pb=pv_bank, sp=sp: e.matmul(
                        psum[pb][0:65, q0:512],
                        v_pair[:, kc * 130 + 65 * h2: kc * 130 + 65 * h2 + 65],
                        e_ring[:, (4 + j) * 512: (4 + j) * 512 + (512 - q0)],
                        start=(sp == 0 and j == 0), stop=(j == 3)), inc="T")
                    slot_guard[4 + j] = P.cnt["T"]
                wr = P.cnt["T"]
                P.wait("vector", "T", wr)
                P.op("vector", lambda e, pb=pv_bank: e.reciprocal(
                    recip_row, psum[pb][64:65, 0:512]), inc="V")
                rp = P.cnt["V"]
                banks.acquire(5)
                P.wait("tensor", "V", rp)
                P.op("tensor", lambda e: e.matmul(
                    psum[5][0:64, 0:512], ones_row[:, 0:64], recip_row,
                    start=True, stop=True), inc="T")
                wr2 = P.cnt["T"]
                P.wait("vector", "T", wr2)
                P.op("vector", lambda e: e.tensor_copy(rb, psum[5][0:64, 0:512]), inc="V")
                banks.release(5, "V")
                P.op("vector", lambda e, hr0=hr0, sp=sp, pb=pv_bank: e.tensor_tensor(
                    att_fm[hr0:hr0 + 64, sp * 512:(sp + 1) * 512],
                    psum[pb][0:64, 0:512], rb, OP.mult), inc="V")
                banks.release(pv_bank, "V")
        att_done = P.cnt["V"]

        P.wait("sync", "V", att_done)
        wos = wb[:, 0:1024]
        wo_rdy = P.dma("sync", wos, dram["wo"][hg * 128:(hg + 1) * 128, :])
        P.wait("tensor", "WD", wo_rdy)
        for tt in range(NT):
            for osp in range(2):
                banks.acquire(5)
                P.op("tensor", lambda e, tt=tt, osp=osp, hg=hg: e.matmul(
                    psum[5][:, :], att_fm[:, tt * 128:(tt + 1) * 128],
                    wos[:, osp * 512:(osp + 1) * 512],
                    start=True, stop=(hg != 0)), inc=None if hg == 0 else "T")
                if hg == 0:
                    P.op("tensor", lambda e, osp=osp: e.matmul(
                        psum[5][:, :], ones_row,
                        bias_row[:, osp * 512:(osp + 1) * 512],
                        start=False, stop=True), inc="T")
                wr = P.cnt["T"]
                P.wait("vector", "T", wr)
                P.op("vector", lambda e, tt=tt, osp=osp: e.tensor_tensor(
                    xres_t(tt)[:, osp * 512:(osp + 1) * 512],
                    xres_t(tt)[:, osp * 512:(osp + 1) * 512],
                    psum[5][:, :], OP.add), inc="V")
                banks.release(5, "V")
        last_hg_T = P.cnt["T"]



    stat = sb["stat"]
    amax_run = stat[:, 32:33]
    tmax = stat[:, 33:34]
    mean_c = stat[:, 35:36]
    vsum_c = stat[:, 36:37]
    rs_c = stat[:, 37:38]
    P.op("vector", lambda e: e.memset(amax_run, 0.0))
    lnj_bf = sb["lnj16"][:, :]
    att_bf = sb["att16"][:, :]
    kq_bf = sb["kq16"][:, :]
    v_bf = sb["v16"][:, :]
    er_bf = sb["er16"][:, :]
    w_bf = sb["w16"][:, :]
    xt32 = s2[:, 0:1024]
    lntm32 = s2[:, 2048:3072]
    x1t = s2[:, 1024:2048]
    mask_bf = sb["mask16"][:, :]
    rb_c = s1[0:64, 6176:6688]
    P.op("vector", lambda e: e.tensor_copy(mask_bf, cst[:, 128:640]), inc="V")

    xall = dram["x_all"].rearrange("b (t p) d -> b t p d", p=128)
    xb = (xt32, s2[:, 3072:4096])
    xrel = [P.cnt["V"], P.cnt["V"]]
    for j in range(B - 1):
        for tt in range(NT):
            b_ = tt % 2
            P.wait("gpsimd", "V", xrel[b_])
            xrd = P.dma("gpsimd", xb[b_], xall[j, tt, :, :])
            P.wait("vector", "GD", xrd)
            xrel[b_] = ln_tile_c(P, sb, xb[b_], lntm32, s2[:, 1024:2048], mean_c, vsum_c, rs_c)
            P.wait("tensor", "V")
            transpose_tile(P, banks, sb, lntm32,
                           lambda kc, tt=tt: lnj_bf[:, kc * 2048 + tt * 128: kc * 2048 + (tt + 1) * 128],
                           bank_pair=(6, 7), evac_eng="vector")
        lnj_done = P.cnt["V"]

        last_T = None
        for hg in range(8):
            if last_T is not None:
                P.wait("gpsimd", "T", last_T)
            for wi, wname in enumerate(("wq", "wk", "wv")):
                P.dma("gpsimd", w_bf[:, wi * 1024:(wi + 1) * 1024].rearrange("p (c o) -> p c o", c=NCH),
                      dram[wname][:, hg * 128:(hg + 1) * 128].rearrange("(c p) o -> p c o", p=128))
            P.dma("gpsimd", bv_row, dram["rows_c"][3:4, hg * 128:(hg + 1) * 128])
            w_rdy = P.cnt["GD"]
            P.wait("tensor", "GD", w_rdy)
            P.wait("tensor", "V", lnj_done)
            for which, wofs, bcol, dst_ofs, sc_ in (
                    ("k", 1024, sb["stat"][:, 16 + hg:17 + hg], 0, 1.0),
                    ("q", 0, sb["stat"][:, 8 + hg:9 + hg], 2048, 0.125)):
                for sp in range(4):
                    banks.acquire(0)
                    for kc in range(NCH):
                        P.op("tensor", lambda e, kc=kc, sp=sp, wofs=wofs: e.matmul(
                            psum[0][:, :], w_bf[:, wofs + kc * 128:(wofs + kc * 128) + 128],
                            lnj_bf[:, kc * 2048 + sp * 512: kc * 2048 + (sp + 1) * 512],
                            start=(kc == 0), stop=(kc == NCH - 1)),
                            inc="T" if kc == NCH - 1 else None)
                    wr = P.cnt["T"]
                    P.wait("vector", "T", wr)
                    P.op("vector", lambda e, sp=sp, dst_ofs=dst_ofs, bcol=bcol, sc_=sc_: e.tensor_scalar(
                        kq_bf[:, dst_ofs + sp * 512: dst_ofs + (sp + 1) * 512],
                        psum[0][:, :], bcol, sc_, OP.add, OP.mult), inc="V")
                    banks.release(0, "V")
            for tt in range(NT):
                banks.acquire(2)
                for kc in range(NCH):
                    P.op("tensor", lambda e, kc=kc, tt=tt: e.matmul(
                        psum[2][:, 0:128],
                        lnj_bf[:, kc * 2048 + tt * 128: kc * 2048 + (tt + 1) * 128],
                        w_bf[:, 2048 + kc * 128: 2048 + (kc + 1) * 128],
                        start=(kc == 0), stop=False))
                P.op("tensor", lambda e: e.matmul(
                    psum[2][:, 0:128], ones_row, bv_row, start=False, stop=True), inc="T")
                wr = P.cnt["T"]
                P.wait("vector", "T", wr)
                P.op("vector", lambda e, tt=tt: e.tensor_copy(
                    v_bf[:, tt * 130:(tt + 1) * 130].rearrange("p (h c) -> p h c", h=2)[:, :, 0:64],
                    psum[2][:, 0:128].rearrange("p (h c) -> p h c", h=2)), inc="V")
                banks.release(2, "V")
            P.op("vector", lambda e: e.memset(
                v_bf.rearrange("p (t h c) -> p t h c", t=16, h=2)[:, :, :, 64:65], 1.0), inc="V")
            v_done = P.cnt["V"]
            P.wait("tensor", "V", v_done)
            slot_guard = [0] * 8
            for h2 in range(2):
                hr0 = 64 * h2
                pv_bank = 3 + h2
                sc_bank = 0 + h2
                for sp in range(4):
                    banks.acquire(pv_bank)
                    nkc = 4 * (sp + 1)
                    for kc in range(nkc):
                        is_diag = kc >= 4 * sp
                        jj = kc - 4 * sp
                        q0 = 128 * jj if is_diag else 0
                        qw = 512 - q0
                        slot = (4 + jj) if is_diag else (kc % 2)
                        banks.acquire(sc_bank)
                        P.op("tensor", lambda e, kc=kc, sp=sp, hr0=hr0, q0=q0, qw=qw, b=sc_bank: e.matmul(
                            psum[b][:, 0:qw],
                            kq_bf[hr0:hr0 + 64, kc * 128:(kc + 1) * 128],
                            kq_bf[hr0:hr0 + 64, 2048 + sp * 512 + q0: 2048 + sp * 512 + q0 + qw],
                            start=True, stop=True), inc="T")
                        wr = P.cnt["T"]
                        P.wait("scalar", "T", max(wr, slot_guard[slot]))
                        P.op("scalar", lambda e, slot=slot, qw=qw, b=sc_bank: e.activation(
                            out=er_bf[:, slot * 512: slot * 512 + qw],
                            in_=psum[b][:, 0:qw], func=AF.Exp), inc="Sc")
                        banks.release(sc_bank, "Sc")
                        if not is_diag:
                            P.wait("tensor", "Sc")
                            P.op("tensor", lambda e, kc=kc, h2=h2, pb=pv_bank, slot=slot: e.matmul(
                                psum[pb][0:65, 0:512],
                                v_bf[:, kc * 130 + 65 * h2: kc * 130 + 65 * h2 + 65],
                                er_bf[:, slot * 512: slot * 512 + 512],
                                start=(kc == 0), stop=False), inc="T")
                            slot_guard[slot] = P.cnt["T"]
                    P.wait("vector", "Sc")
                    P.op("vector", lambda e: e.tensor_tensor(
                        er_bf[:, 4 * 512:8 * 512].rearrange("p (s c) -> p s c", s=4)[:, :, 0:128],
                        er_bf[:, 4 * 512:8 * 512].rearrange("p (s c) -> p s c", s=4)[:, :, 0:128],
                        mask_bf.rearrange("p (s c) -> p s c", s=4), OP.mult), inc="V")
                    mk = P.cnt["V"]
                    P.wait("tensor", "V", mk)
                    for jj in range(4):
                        kc = 4 * sp + jj
                        q0 = 128 * jj
                        P.op("tensor", lambda e, jj=jj, kc=kc, q0=q0, h2=h2, pb=pv_bank, sp=sp: e.matmul(
                            psum[pb][0:65, q0:512],
                            v_bf[:, kc * 130 + 65 * h2: kc * 130 + 65 * h2 + 65],
                            er_bf[:, (4 + jj) * 512: (4 + jj) * 512 + (512 - q0)],
                            start=(sp == 0 and jj == 0), stop=(jj == 3)), inc="T")
                        slot_guard[4 + jj] = P.cnt["T"]
                    wr = P.cnt["T"]
                    P.wait("vector", "T", wr)
                    P.op("vector", lambda e, pb=pv_bank: e.reciprocal(
                        recip_row, psum[pb][64:65, 0:512]), inc="V")
                    rp = P.cnt["V"]
                    banks.acquire(5)
                    P.wait("tensor", "V", rp)
                    P.op("tensor", lambda e: e.matmul(
                        psum[5][0:64, 0:512], ones_row[:, 0:64], recip_row,
                        start=True, stop=True), inc="T")
                    wr2 = P.cnt["T"]
                    P.wait("vector", "T", wr2)
                    P.op("vector", lambda e: e.tensor_copy(rb_c, psum[5][0:64, 0:512]), inc="V")
                    banks.release(5, "V")
                    P.op("vector", lambda e, hg=hg, hr0=hr0, sp=sp, pb=pv_bank: e.tensor_tensor(
                        att_bf[hr0:hr0 + 64, hg * 2048 + sp * 512: hg * 2048 + (sp + 1) * 512],
                        psum[pb][0:64, 0:512], rb_c, OP.mult), inc="V")
                    banks.release(pv_bank, "V")
            last_T = P.cnt["T"]
        att_done_c = P.cnt["V"]

        wo_bf0 = sb["w16"][:, :]
        wo_bf1 = sb["er16"][:, :]
        P.wait("gpsimd", "T", last_T)
        P.wait("gpsimd", "Sc")
        P.dma("gpsimd", wo_bf0.rearrange("p (c o) -> p c o", c=NCH),
              dram["wo"][:, 0:512].rearrange("(c p) o -> p c o", p=128))
        P.dma("gpsimd", wo_bf1.rearrange("p (c o) -> p c o", c=NCH),
              dram["wo"][:, 512:1024].rearrange("(c p) o -> p c o", p=128))
        wo_rdy = P.cnt["GD"]
        P.wait("tensor", "GD", wo_rdy)
        P.wait("tensor", "V", att_done_c)
        for tt in range(NT):
            P.wait("gpsimd", "V")
            xrd = P.dma("gpsimd", xt32, xall[j, tt, :, :])
            for osp in range(2):
                banks.acquire(5)
                wo_h = wo_bf0 if osp == 0 else wo_bf1
                for kc in range(NCH):
                    P.op("tensor", lambda e, kc=kc, tt=tt, wo_h=wo_h: e.matmul(
                        psum[5][:, :], att_bf[:, kc * 2048 + tt * 128:(kc * 2048 + tt * 128) + 128],
                        wo_h[:, kc * 512:(kc + 1) * 512],
                        start=(kc == 0), stop=False))
                P.op("tensor", lambda e, osp=osp: e.matmul(
                    psum[5][:, :], ones_row, bias_row[:, osp * 512:(osp + 1) * 512],
                    start=False, stop=True), inc="T")
                wr = P.cnt["T"]
                if osp == 0:
                    P.wait("vector", "GD", xrd)
                P.wait("vector", "T", wr)
                P.op("vector", lambda e, osp=osp: e.tensor_tensor(
                    x1t[:, osp * 512:(osp + 1) * 512], psum[5][:, :],
                    xt32[:, osp * 512:(osp + 1) * 512], OP.add), inc="V")
                banks.release(5, "V")
            P.op("vector", lambda e: e.tensor_reduce(mean_c, x1t, axis=AX.X, op=OP.add), inc="V")
            P.wait("vector", "V")
            P.op("vector", lambda e: e.tensor_scalar(mean_c, mean_c, 1.0 / D, None, OP.mult), inc="V")
            P.wait("vector", "V")
            P.op("vector", lambda e: e.tensor_scalar(
                s2[:, 0:1024], x1t, mean_c, None, OP.subtract))
            P.op("vector", lambda e: e.scalar_tensor_tensor(
                s2[:, 4096:5120], s2[:, 0:1024], 1.0, s2[:, 0:1024], OP.mult, OP.mult,
                accum_out=vsum_c), inc="V")
            P.wait("scalar", "V")
            P.op("scalar", lambda e: e.activation(out=rs_c, in_=vsum_c, func=AF.Sqrt,
                                                  bias=stat[:, 3:4], scale=1.0 / D), inc="Sc")
            P.wait("vector", "Sc")
            P.op("vector", lambda e: e.reciprocal(rs_c, rs_c), inc="V")
            P.wait("vector", "V")
            P.op("vector", lambda e: e.tensor_reduce(
                tmax, s2[:, 0:1024], axis=AX.X, op=OP.max, apply_absolute_value=True), inc="V")
            P.wait("vector", "V")
            P.op("vector", lambda e: e.tensor_tensor(tmax, tmax, rs_c, OP.mult), inc="V")
            P.wait("vector", "V")
            P.op("vector", lambda e: e.tensor_tensor(amax_run, amax_run, tmax, OP.max), inc="V")
        xrel[0] = xrel[1] = P.cnt["V"]

    if stage == "amax":
        store_y(P, dram, x_res)
        return

    stat = sb["stat"]
    ind = stat[:, 24:25]
    c4col = stat[:, 25:26]
    mcol = stat[:, 26:27]
    sc_x = stat[:, 27:28]
    inv_x = stat[:, 28:29]
    sc_w = stat[:, 29:30]
    inv_w = stat[:, 30:31]
    inv2 = stat[:, 31:32]
    amax_run = stat[:, 32:33]
    tmax = stat[:, 33:34]

    def vf():
        P.mark("vector", "V")
        P.wait("vector", "V")

    P.op("vector", lambda e: e.memset(ind, 2.0 ** -30))
    vf()
    P.op("vector", lambda e: e.tensor_scalar(ind, ind, 1.0, None, OP.add))
    vf()
    P.op("vector", lambda e: e.tensor_scalar(ind, ind, 1.0, None, OP.subtract))
    vf()
    P.op("vector", lambda e: e.tensor_scalar(ind, ind, 2.0 ** 30, None, OP.mult))
    vf()
    P.op("vector", lambda e: e.tensor_scalar(c4col, ind, 2.0 ** 49, None, OP.mult))
    P.op("vector", lambda e: e.tensor_scalar(c4col, c4col, float(2 ** 20 + 1), None, OP.add))
    vf()
    P.op("vector", lambda e: e.tensor_scalar(tmax, ind, 2.0 ** 20, None, OP.mult))
    vf()
    P.op("vector", lambda e: e.tensor_tensor(c4col, c4col, tmax, OP.subtract))
    vf()
    P.op("vector", lambda e: e.tensor_scalar(mcol, ind, 1.5 * 2.0 ** 43, None, OP.mult))
    P.op("vector", lambda e: e.tensor_scalar(mcol, mcol, 1.5 * 2.0 ** 14, None, OP.add))
    vf()
    P.op("vector", lambda e: e.tensor_scalar(tmax, ind, 1.5 * 2.0 ** 14, None, OP.mult))
    vf()
    P.op("vector", lambda e: e.tensor_tensor(mcol, mcol, tmax, OP.subtract))
    vf()

    for tt in range(NT):
        if tt > 0:
            P.wait("vector", "Sc")
        ln_tile(P, sb, xres_t(tt), ln_tmp, ln_sq)
        P.op("vector", lambda e: e.tensor_reduce(
            tmax, ln_tmp, axis=AX.X, op=OP.max, apply_absolute_value=True), inc="V")
        P.wait("vector", "V")
        P.op("vector", lambda e: e.tensor_tensor(amax_run, amax_run, tmax, OP.max))
        P.wait("tensor", "V")
        transpose_tile(P, banks, sb, ln_tmp, lambda kc, tt=tt: fm_ap(kc, tt))
    ln1_done = P.cnt["Sc"]
    dbg_tap("ln1_fm", fmA[:, :])

    vf()
    arow = rowa[0:1, 1664:1665]
    scrow = rowa[0:1, 1665:1667]
    scalar_finalize(P, banks, sb, amax_run, arow, op=OP.max)
    P.op("vector", lambda e: e.tensor_scalar(arow, arow, 1e-12, None, OP.max))
    vf()
    P.op("vector", lambda e: e.reciprocal(scrow[:, 0:1], arow))
    vf()
    P.op("vector", lambda e: e.tensor_scalar(scrow[:, 0:1], scrow[:, 0:1], 448.0, None, OP.mult))
    P.op("vector", lambda e: e.tensor_scalar(scrow[:, 1:2], arow, 1.0 / 448.0, None, OP.mult))
    vf()
    bcast_p0(P, banks, sb, scrow, stat[:, 27:29], 2)

    wstage = s1[:, 2048:6144]
    P.op("vector", lambda e: e.memset(amax_run, 0.0))
    wf_t = dram["w_ffn0_t"].rearrange("(c p) o -> p c o", p=128)
    P.wait("sync", "GD")
    P.wait("sync", "V")
    P.wait("sync", "T")
    for half in range(2):
        if half > 0:
            P.wait("sync", "V")
        wrd = P.dma("sync", wstage.rearrange("p (c o) -> p c o", c=4),
                    wf_t[:, half * 4:(half + 1) * 4, :])
        P.wait("vector", "WD", wrd)
        P.op("vector", lambda e: e.tensor_reduce(
            tmax, wstage, axis=AX.X, op=OP.max, apply_absolute_value=True), inc="V")
        P.wait("vector", "V")
        P.op("vector", lambda e: e.tensor_tensor(amax_run, amax_run, tmax, OP.max), inc="V")
    vf()
    scalar_finalize(P, banks, sb, amax_run, arow, op=OP.max)
    P.op("vector", lambda e: e.tensor_scalar(arow, arow, 1e-12, None, OP.max))
    vf()
    P.op("vector", lambda e: e.reciprocal(scrow[:, 0:1], arow))
    vf()
    P.op("vector", lambda e: e.tensor_scalar(scrow[:, 0:1], scrow[:, 0:1], 448.0, None, OP.mult))
    P.op("vector", lambda e: e.tensor_scalar(scrow[:, 1:2], arow, 1.0 / 448.0, None, OP.mult))
    vf()
    bcast_p0(P, banks, sb, scrow, stat[:, 29:31], 2)
    P.op("vector", lambda e: e.tensor_tensor(inv2, inv_x, inv_w, OP.mult), inc="V")
    P.wait("vector", "V")

    wq_bf = sb["wqB"][:, :]
    qscr = [s2[:, 4096 + i * 512:4096 + (i + 1) * 512] for i in range(4)]
    yq = s1[:, 0:2048]
    for half in range(2):
        P.wait("sync", "V")
        wrd = P.dma("sync", wstage.rearrange("p (c o) -> p c o", c=4),
                    wf_t[:, half * 4:(half + 1) * 4, :])
        P.wait("vector", "WD", wrd)
        for sub in range(8):
            kc = half * 4 + sub // 2
            off = (sub % 2) * 512
            ysl_w = yq[:, 0:512]
            P.op("vector", lambda e, sub=sub, ysl_w=ysl_w: e.tensor_scalar(
                ysl_w, wstage[:, sub * 512:(sub + 1) * 512], sc_w, None, OP.mult))
            quant_grid_tile(P, sb, ysl_w, wq_bf[:, kc * 1024 + off: kc * 1024 + off + 512],
                            qscr, c4col, mcol)
    wq_done = P.cnt["V"]
    dbg_tap("wq_s2", s2[:, 0:4096])

    brd = P.dma("sync", bias_row, dram["rows_c"][4:5, 0:1024])
    P.wait("vector", "WD", brd)
    P.op("vector", lambda e: e.tensor_tensor(tmax, sc_x, sc_w, OP.mult), inc="V")
    P.wait("vector", "V")
    P.op("vector", lambda e: e.tensor_scalar(bias_row, bias_row, tmax[0:1, 0:1], None, OP.mult), inc="V")
    bias_done = P.cnt["V"]

    xq_bf = sb["xqB"][:, :]
    for sp in range(4):
        for kc in range(NCH):
            ysl_x = s1[:, 6400:6912]
            P.op("vector", lambda e, kc=kc, sp=sp, ysl_x=ysl_x: e.tensor_scalar(
                ysl_x, fm_span(kc, sp * 512, 512), sc_x, None, OP.mult))
            quant_grid_tile(P, sb, ysl_x, xq_bf[:, kc * 512:(kc + 1) * 512],
                            qscr, c4col, mcol)
        xq_done = P.cnt["V"]
        P.wait("tensor", "V", xq_done)
        P.wait("tensor", "Sc", ln1_done)
        for t4 in range(4):
            tt = sp * 4 + t4
            for osp in range(2):
                banks.acquire(5)
                for kc in range(NCH):
                    P.op("tensor", lambda e, kc=kc, t4=t4, osp=osp: e.matmul(
                        psum[5][:, :],
                        xq_bf[:, kc * 512 + t4 * 128: kc * 512 + (t4 + 1) * 128],
                        wq_bf[:, kc * 1024 + osp * 512: kc * 1024 + (osp + 1) * 512],
                        start=(kc == 0), stop=False))
                P.op("tensor", lambda e, osp=osp: e.matmul(
                    psum[5][:, :], ones_row, bias_row[:, osp * 512:(osp + 1) * 512],
                    start=False, stop=True), inc="T")
                wr = P.cnt["T"]
                P.wait("vector", "T", wr)
                P.op("vector", lambda e, tt=tt, osp=osp: e.scalar_tensor_tensor(
                    xres_t(tt)[:, osp * 512:(osp + 1) * 512],
                    psum[5][:, :], inv2,
                    xres_t(tt)[:, osp * 512:(osp + 1) * 512],
                    OP.mult, OP.add), inc="V")
                banks.release(5, "V")

    dbg_tap("statd", sb["stat"][:, :])
    if stage == "x2":
        store_y(P, dram, x_res)
        return

    TC = 256
    NCHK = S // TC
    NPT = DI // 128
    xs_a = lambda pt: fmA[:, pt * TC:(pt + 1) * TC]
    u_a = lambda pt: fmA[:, 4096 + pt * TC: 4096 + (pt + 1) * TC]
    dl_a = lambda pt: fmA[:, 8192 + pt * TC: 8192 + (pt + 1) * TC]
    y_a = lambda pt: fmA[:, 12288 + pt * TC: 12288 + (pt + 1) * TC]
    wstg1 = fmA[:, 4096:12288]
    ln2fm = lambda kc: s1[:, kc * TC:(kc + 1) * TC]
    halo = s1[:, 4096:4144]
    negA = s1[:, 5424:5680]
    convw = s1[:, 5680:5744]
    convb = s1[:, 5744:5760]
    bdt_c = s1[:, 5760:5776]
    dpar_c = s1[:, 5776:5792]
    carry = s1[:, 6144:6400]
    a_t = s2[:, 0:256]
    b_t = s2[:, 256:512]
    h_t = s2[:, 512:768]
    xdbl = s2[:, 768:1024]
    e_scr = s2[:, 1024:2048]
    ln2tm = s2[:, 2048:4096]
    t_tm = s2[:, 4096:5120]
    ln_scr = s2[:, 5120:6144]
    bc_b = s1[32:48, 6400:6656]
    bc_c = s1[32:48, 6656:6912]
    onehot16 = rowa[32:48, :]

    P.wait("sync", "V")
    P.wait("sync", "T")
    P.dma("sync", s1[:, 5424:5680], dram["a_log_c"][:, :])
    P.dma("sync", convw, dram["conv_w_c"][:, :])
    P.dma("sync", convb, dram["conv_b_c"][:, :])
    P.dma("sync", bdt_c, dram["b_dt_c"][:, :])
    P.dma("sync", dpar_c, dram["d_par_c"][:, :])
    P.dma("sync", wb[:, 0:1536].rearrange("p (c r) -> p c r", c=16),
          dram["w_x_t"].rearrange("(c p) r -> p c r", p=128))
    P.dma("sync", wb[0:DTR, 1536:3584], dram["w_dt_t"][:, :])
    ld_e = P.cnt["WD"]
    P.wait("scalar", "WD", ld_e)
    P.op("scalar", lambda e: e.activation(out=negA, in_=negA, func=AF.Exp, scale=1.0), inc="Sc")
    P.wait("vector", "Sc")
    P.op("vector", lambda e: e.tensor_scalar(negA, negA, -1.0, None, OP.mult), inc="V")
    P.op("vector", lambda e: e.memset(halo, 0.0))
    P.op("vector", lambda e: e.memset(carry, 0.0), inc="V")

    win_t = dram["w_in_t"].rearrange("(c p) e -> p c e", p=128)
    wout_t = dram["w_out_t"].rearrange("(c p) o -> p c o", p=128)

    for ch in range(NCHK):
        P.wait("vector", "T")
        for t2 in range(2):
            tile = 2 * ch + t2
            ln_tile(P, sb, xres_t(tile), ln2tm[:, t2 * 1024:(t2 + 1) * 1024], ln_scr)
            P.wait("tensor", "V")
            transpose_tile(P, banks, sb, ln2tm[:, t2 * 1024:(t2 + 1) * 1024],
                           lambda kc, t2=t2: ln2fm(kc)[:, t2 * 128:(t2 + 1) * 128],
                           bank_pair=(6, 7), evac_eng="vector")
        ln2_done = P.cnt["V"]

        for half in range(2):
            P.wait("sync", "V")
            wrd = P.dma("sync", wstg1, win_t[:, :, half * 1024:(half + 1) * 1024])
            P.wait("tensor", "WD", wrd)
            P.wait("tensor", "V", ln2_done)
            for es8 in range(8):
                es = half * 8 + es8
                banks.acquire(4)
                for kc in range(NCH):
                    P.op("tensor", lambda e, kc=kc, es8=es8: e.matmul(
                        psum[4][:, 0:TC], wstg1[:, kc * 1024 + es8 * 128: kc * 1024 + (es8 + 1) * 128],
                        ln2fm(kc), start=(kc == 0), stop=(kc == NCH - 1)),
                        inc="T" if kc == NCH - 1 else None)
                wr = P.cnt["T"]
                P.wait("vector", "T", wr)
                P.op("vector", lambda e, es=es: e.tensor_copy(xs_a(es), psum[4][:, 0:TC]), inc="V")
                banks.release(4, "V")

        for pt in range(NPT):
            if pt > 0:
                P.wait("vector", "Sc")
            acc = s2[:, 1024:1280]
            acc2 = s2[:, 1280:1536]
            for k in range(4):
                def shifted(e_unused, pt=pt, k=k):
                    pass
                if k < 3:
                    nh = 3 - k
                    srcs = [(halo[:, pt * 3 + (3 - nh): pt * 3 + 3], 0, nh),
                            (xs_a(pt)[:, 0:TC - nh], nh, TC - nh)]
                else:
                    srcs = [(xs_a(pt), 0, TC)]
                wcol = convw[:, pt * 4 + k: pt * 4 + k + 1]
                for (src_ap, off, w_) in srcs:
                    if k == 0 and off == 0:
                        P.op("vector", lambda e, src_ap=src_ap, off=off, w_=w_, wcol=wcol, acc=acc: e.tensor_scalar(
                            acc[:, off:off + w_], src_ap, wcol, None, OP.mult))
                    elif k == 0:
                        P.op("vector", lambda e, src_ap=src_ap, off=off, w_=w_, wcol=wcol, acc=acc: e.tensor_scalar(
                            acc[:, off:off + w_], src_ap, wcol, None, OP.mult))
                    else:
                        P.op("vector", lambda e, src_ap=src_ap, off=off, w_=w_, wcol=wcol, acc=acc, acc2=acc2: e.scalar_tensor_tensor(
                            acc2[:, off:off + w_], src_ap, wcol, acc[:, off:off + w_],
                            OP.mult, OP.add))
                if k > 0:
                    acc, acc2 = acc2, acc
            P.op("vector", lambda e, acc=acc: e.engine_nop(), inc="V")
            P.wait("scalar", "V")
            P.op("scalar", lambda e, pt=pt, acc=acc: e.activation(
                out=u_a(pt), in_=acc, func=AF.Silu,
                bias=convb[:, pt:pt + 1], scale=1.0), inc="Sc")
        P.op("vector", lambda e: e.tensor_copy(
            halo.rearrange("p (t c) -> p t c", t=16),
            fmA[:, 0:4096].rearrange("p (t c) -> p t c", t=16)[:, :, TC - 3:TC]), inc="V")
        u_done = P.cnt["Sc"]

        P.wait("tensor", "Sc", u_done)
        P.wait("tensor", "WD", ld_e)
        banks.acquire(5)
        for ec in range(NPT):
            P.op("tensor", lambda e, ec=ec: e.matmul(
                psum[5][0:96, 0:TC], wb[:, ec * 96:(ec + 1) * 96], u_a(ec),
                start=(ec == 0), stop=(ec == NPT - 1)), inc="T" if ec == NPT - 1 else None)
        wr = P.cnt["T"]
        P.wait("vector", "T", wr)
        P.op("vector", lambda e: e.tensor_copy(xdbl[0:96, :], psum[5][0:96, 0:TC]), inc="V")
        banks.release(5, "V")
        P.wait("gpsimd", "V")
        P.dma("gpsimd", bc_b, xdbl[64:80, :], inc="GD")
        P.dma("gpsimd", bc_c, xdbl[80:96, :], inc="GD")
        bc_done = P.cnt["GD"]

        for pt in range(NPT):
            banks.acquire(5)
            wsl = wb[0:DTR, 1536 + pt * 128: 1536 + (pt + 1) * 128]
            P.op("tensor", lambda e, wsl=wsl: e.matmul(
                psum[5][:, 0:TC], wsl, xdbl[0:64, :], start=True, stop=True), inc="T")
            wr = P.cnt["T"]
            P.wait("scalar", "T", wr)
            P.op("scalar", lambda e, pt=pt: e.activation(
                out=e_scr[:, 0:TC], in_=psum[5][:, 0:TC], func=AF.Exp,
                bias=bdt_c[:, pt:pt + 1], scale=1.0), inc="Sc")
            banks.release(5, "Sc")
            P.op("scalar", lambda e, pt=pt: e.activation(
                out=dl_a(pt), in_=e_scr[:, 0:TC], func=AF.Ln, bias=1.0, scale=1.0), inc="Sc")
        dl_done = P.cnt["Sc"]

        P.wait("vector", "Sc", dl_done)
        for pt in range(NPT):
            P.op("vector", lambda e, pt=pt: e.tensor_tensor(xs_a(pt), dl_a(pt), u_a(pt), OP.mult))
        P.op("vector", lambda e: e.engine_nop(), inc="V")
        du_done = P.cnt["V"]

        prev_scan = [P.cnt["V"]]
        P.wait("tensor", "GD", bc_done)
        for n in range(DSTATE):
            banks.acquire(2)
            P.op("tensor", lambda e, n=n: e.matmul(
                psum[2][:, 0:TC], onehot16[:, n * 128:(n + 1) * 128], bc_b,
                start=True, stop=True), inc="T")
            banks.acquire(3)
            P.op("tensor", lambda e, n=n: e.matmul(
                psum[3][:, 0:TC], onehot16[:, n * 128:(n + 1) * 128], bc_c,
                start=True, stop=True), inc="T")
            wr = P.cnt["T"]
            P.wait("vector", "T", wr)
            P.wait("scalar", "T", wr)
            if n == 0:
                P.wait("vector", "V", du_done)
                P.wait("scalar", "V", du_done)
            for pt in range(NPT):
                nacol = negA[:, pt * 16 + n: pt * 16 + n + 1]
                ccol = carry[:, pt * 16 + n: pt * 16 + n + 1]
                P.wait("scalar", "V", prev_scan[0])
                P.op("scalar", lambda e, pt=pt, nacol=nacol: e.activation(
                    out=a_t, in_=dl_a(pt), func=AF.Exp, scale=nacol), inc="Sc")
                sc_a = P.cnt["Sc"]
                P.op("vector", lambda e, pt=pt: e.tensor_tensor(
                    b_t, xs_a(pt), psum[2][:, 0:TC], OP.mult), inc="V")
                P.wait("vector", "Sc", sc_a)
                P.op("vector", lambda e, ccol=ccol: e.tensor_tensor_scan(
                    h_t, a_t, b_t, ccol, OP.mult, OP.add), inc="V")
                prev_scan[0] = P.cnt["V"]
                P.wait("vector", "V")
                P.op("vector", lambda e, ccol=ccol: e.tensor_copy(ccol, h_t[:, TC - 1:TC]))
                P.op("vector", lambda e: e.tensor_tensor(b_t, h_t, psum[3][:, 0:TC], OP.mult))
                if n == 0:
                    P.op("vector", lambda e, pt=pt: e.tensor_copy(y_a(pt), b_t), inc="V")
                else:
                    P.op("vector", lambda e, pt=pt: e.tensor_tensor(
                        y_a(pt), y_a(pt), b_t, OP.add), inc="V")
            banks.release(2, "V")
            banks.release(3, "V")
        scan_done = P.cnt["V"]

        for pt in range(NPT):
            P.op("vector", lambda e, pt=pt: e.scalar_tensor_tensor(
                y_a(pt), u_a(pt), dpar_c[:, pt:pt + 1], y_a(pt), OP.mult, OP.add), inc="V")
        for q in range(4):
            stg = fmA[:, 0:4096] if q % 2 == 0 else fmA[:, 8192:12288]
            P.wait("sync", "V")
            P.wait("sync", "T")
            wrd = P.dma("sync", stg.rearrange("p (c e) -> p c e", c=8),
                        win_t[:, :, 2048 + q * 512: 2048 + (q + 1) * 512])
            P.wait("tensor", "WD", wrd)
            for es4 in range(4):
                pt = q * 4 + es4
                banks.acquire(4)
                for kc in range(NCH):
                    P.op("tensor", lambda e, kc=kc, es4=es4, stg=stg: e.matmul(
                        psum[4][:, 0:TC], stg[:, kc * 512 + es4 * 128: kc * 512 + (es4 + 1) * 128],
                        ln2fm(kc), start=(kc == 0), stop=(kc == NCH - 1)),
                        inc="T" if kc == NCH - 1 else None)
                wr = P.cnt["T"]
                P.wait("scalar", "T", wr)
                P.op("scalar", lambda e: e.activation(
                    out=e_scr[:, 0:TC], in_=psum[4][:, 0:TC], func=AF.Silu), inc="Sc")
                banks.release(4, "Sc")
                P.wait("vector", "Sc")
                P.op("vector", lambda e, pt=pt: e.tensor_tensor(
                    y_a(pt), y_a(pt), e_scr[:, 0:TC], OP.mult), inc="V")
        y_done = P.cnt["V"]

        lnwb = s1[:, 0:2048]
        P.wait("sync", "V", y_done)
        P.wait("sync", "T")
        for wh in range(2):
            if wh == 1:
                P.wait("sync", "T")
            wrd = P.dma("sync", fmA[:, 0:8192].rearrange("p (c o) -> p c o", c=8),
                        wout_t[:, wh * 8:(wh + 1) * 8, :])
            P.wait("tensor", "WD", wrd)
            for t2 in range(2):
                for osp in range(2):
                    bank = 2 + t2 * 2 + osp
                    if wh == 0:
                        banks.acquire(bank)
                    for e8 in range(8):
                        ec = wh * 8 + e8
                        P.op("tensor", lambda e, ec=ec, e8=e8, t2=t2, osp=osp, bank=bank: e.matmul(
                            psum[bank][:, :], y_a(ec)[:, t2 * 128:(t2 + 1) * 128],
                            fmA[:, e8 * 1024 + osp * 512: e8 * 1024 + osp * 512 + 512],
                            start=(ec == 0), stop=(ec == NPT - 1)),
                            inc="T" if (e8 == 7) else None)
        wo_mm = P.cnt["T"]
        brd = P.dma("sync", lnwb, dram["ln_wb"][:, :])
        for t2 in range(2):
            P.wait("vector", "T", wo_mm)
            for osp in range(2):
                bank = 2 + t2 * 2 + osp
                P.op("vector", lambda e, t2=t2, osp=osp, bank=bank: e.tensor_tensor(
                    t_tm[:, osp * 512:(osp + 1) * 512], psum[bank][:, :],
                    ln2tm[:, t2 * 1024 + osp * 512: t2 * 1024 + (osp + 1) * 512],
                    OP.add), inc="V")
                banks.release(bank, "V")
            tile = 2 * ch + t2
            ln_tile(P, sb, t_tm, ln_scr[:, 0:1024], s2[:, 1024:2048])
            if t2 == 0:
                P.wait("vector", "WD", brd)
            P.op("vector", lambda e: e.tensor_tensor(
                ln_scr[:, 0:1024], ln_scr[:, 0:1024], lnwb[:, 0:1024], OP.mult))
            P.op("vector", lambda e: e.tensor_tensor(
                ln_scr[:, 0:1024], ln_scr[:, 0:1024], lnwb[:, 1024:2048], OP.add))
            P.op("vector", lambda e, tile=tile: e.tensor_tensor(
                xres_t(tile), xres_t(tile), ln_scr[:, 0:1024], OP.add), inc="V")

    if stage == "x3":
        store_y(P, dram, x_res)
        return

    lnH = sb["lnH"][:, :]
    lnL = sb["lnL"][:, :]
    upH = sb["upH"][:, :]
    upL = sb["upL"][:, :]
    dnH = sb["dnH"][:, :]
    dnL = sb["dnL"][:, :]
    wrtH = sb["wrtH"][:, :]
    wrtL = sb["wrtL"][:, :]
    stg = wb[:, 2176:3200]
    a16 = sb["act16"][:, :]
    wrt_sb = s1[:, 6400:6528]
    probs = s1[:, 6528:6544]
    rank_a = s1[:, 6544:6560]
    rank_b = s1[:, 6560:6576]
    gate_tm = s1[:, 6576:6592]
    gpark = s1[0:16, 6144:6272]
    gfm_all = s1[32:48, 0:2048]
    act32 = s1[:, 4096:6144]
    mask_buf = cst[:, 896:912]
    ln3tm = s2[:, 2048:4096]
    den = sb["stat"][:, 34:35]
    NSLAB = 2

    P.wait("sync", "V")
    P.wait("sync", "T")
    wrd = P.dma("sync", wrt_sb.rearrange("p (c r) -> p c r", c=NCH),
                dram["wrt_t"].rearrange("(c p) r -> p c r", p=128))
    P.wait("vector", "WD", wrd)
    P.wait("vector", "T")
    P.op("vector", lambda e: e.tensor_copy(wrtH, wrt_sb))
    P.op("vector", lambda e: e.tensor_tensor(wrtL, wrt_sb, wrtH, OP.subtract), inc="V")

    for tile in range(NT):
        P.wait("vector", "T")
        ln_tile(P, sb, xres_t(tile), ln3tm[:, 0:1024], s2[:, 1024:2048])
        P.wait("tensor", "V")
        transpose_tile(P, banks, sb, ln3tm[:, 0:1024],
                       lambda kc, tile=tile: lnH[:, kc * 2048 + tile * 128: kc * 2048 + (tile + 1) * 128],
                       bank_pair=(6, 7), evac_eng="vector",
                       dst_lo_fn=lambda kc, tile=tile: lnL[:, kc * 2048 + tile * 128: kc * 2048 + (tile + 1) * 128])
        ln_done_t = P.cnt["V"]
        banks.acquire(7)
        P.wait("tensor", "V", ln_done_t)
        nmm = 0
        for (aa, bb) in ((lnH, wrtH), (lnH, wrtL), (lnL, wrtH)):
            for kc in range(NCH):
                nmm += 1
                P.op("tensor", lambda e, aa=aa, bb=bb, kc=kc, tile=tile,
                     st=(nmm == 1), sp_=(nmm == 24): e.matmul(
                    psum[7][:, 0:NEXP],
                    aa[:, kc * 2048 + tile * 128: kc * 2048 + (tile + 1) * 128],
                    bb[:, kc * 16:(kc + 1) * 16],
                    start=st, stop=sp_), inc="T" if nmm == 24 else None)
        wr = P.cnt["T"]
        P.wait("scalar", "T", wr)
        P.op("scalar", lambda e: e.activation(
            out=probs, in_=psum[7][:, 0:NEXP], func=AF.Exp, accum_out=den), inc="Sc")
        banks.release(7, "Sc")
        P.wait("vector", "Sc")
        P.op("vector", lambda e: e.reciprocal(den, den), inc="V")
        P.wait("vector", "V")
        P.op("vector", lambda e: e.tensor_scalar(probs, probs, den, None, OP.mult), inc="V")
        P.wait("vector", "V")
        P.op("vector", lambda e: e.tensor_scalar(
            rank_a, probs, probs[:, 0:1], None, OP.is_lt))
        cur, nxt = rank_a, rank_b
        for j in range(1, NEXP):
            P.op("vector", lambda e, j=j, cur=cur, nxt=nxt: e.scalar_tensor_tensor(
                nxt, probs, probs[:, j:j + 1], cur, OP.is_lt, OP.add))
            cur, nxt = nxt, cur
        P.op("vector", lambda e: e.engine_nop(), inc="V")
        P.wait("vector", "V")
        P.op("vector", lambda e, cur=cur: e.tensor_scalar(
            mask_buf, cur, 3.5, None, OP.is_lt), inc="V")
        P.wait("vector", "V")
        P.op("vector", lambda e: e.tensor_tensor(
            gate_tm, probs, mask_buf, OP.mult), inc="V")
        gt = P.cnt["V"]
        banks.acquire(7)
        P.wait("tensor", "V", gt)
        P.op("tensor", lambda e: e.matmul(
            psum[7][0:16, 0:128], gate_tm, cst[:, 0:128],
            is_transpose=True, start=True, stop=True), inc="T")
        wr = P.cnt["T"]
        P.wait("vector", "T", wr)
        P.wait("vector", "GD")
        P.op("vector", lambda e: e.tensor_copy(gpark, psum[7][0:16, 0:128]), inc="V")
        banks.release(7, "V")
        gv = P.cnt["V"]
        P.wait("gpsimd", "V", gv)
        P.dma("gpsimd", gfm_all[:, tile * 128:(tile + 1) * 128], gpark, inc="GD")
    gate_rdy = P.cnt["GD"]
    ln_rdy = P.cnt["V"]
    dbg_tap("lnH0", lnH[:, 0:2048])
    dbg_tap("lnL0", lnL[:, 0:2048])
    P.wait("gpsimd", "GD")
    dbg_tap("gfm", gfm_all)

    roup = dram["roup_t"].rearrange("(c p) e -> p c e", p=128)
    shup = dram["shup_t"].rearrange("(c p) e -> p c e", p=128)
    rodn = dram["rodn_t"].rearrange("(r p) o -> r p o", p=128)
    shdn = dram["shdn_t"].rearrange("(r p) o -> r p o", p=128)
    NEX = NEXP + NSH

    def slab(g):
        s = (g % NSLAB) * 512
        return stg[:, s:s + 512]

    def slab_src(ex, k):
        if k < 8:
            if ex < NEXP:
                return roup[:, k:k + 1, ex * EDIM:(ex + 1) * EDIM].rearrange("p c e -> p (c e)")
            return shup[:, k:k + 1, (ex - NEXP) * EDIM:(ex - NEXP + 1) * EDIM].rearrange("p c e -> p (c e)")
        r, osp = (k - 8) // 2, (k - 8) % 2
        rr = (ex * 4 + r) if ex < NEXP else ((ex - NEXP) * 4 + r)
        srcd = rodn if ex < NEXP else shdn
        return srcd[rr:rr + 1, :, osp * 512:(osp + 1) * 512].rearrange("r p o -> p (r o)")

    def slab_dst(k):
        if k < 8:
            return (upH[:, k * 512:(k + 1) * 512], upL[:, k * 512:(k + 1) * 512])
        r, osp = (k - 8) // 2, (k - 8) % 2
        o0 = r * 1024 + osp * 512
        return (dnH[:, o0:o0 + 512], dnL[:, o0:o0 + 512])

    convV = {}

    def emit_stage(ex, ks, tgate):
        for k in ks:
            g = ex * 16 + k
            if g - NSLAB in convV:
                P.wait("sync", "V", convV[g - NSLAB])
            wd = P.dma("sync", slab(g), slab_src(ex, k))
            P.wait("vector", "WD", wd)
            if tgate:
                P.wait("vector", "T", tgate)
            hi, lo = slab_dst(k)
            P.op("vector", lambda e, hi=hi, g=g: e.tensor_copy(hi, slab(g)))
            P.op("vector", lambda e, hi=hi, lo=lo, g=g: e.tensor_tensor(
                lo, slab(g), hi, OP.subtract), inc="V")
            convV[g] = P.cnt["V"]

    emit_stage(0, range(16), P.cnt["T"])
    dbg_tap("upH0", upH)
    dbg_tap("dnH0", dnH)
    dbg_tap("stg0", stg)
    up_conv_last = {0: convV[7]}
    dn_conv_last = {0: convV[15]}
    act_guard = [0] * 4
    act_vcnt = [0] * 4
    P.wait("tensor", "GD", gate_rdy)
    for ex in range(NEX):
        routed = ex < NEXP
        up_T = 0
        for sp in range(4):
            if routed:
                banks.acquire(6)
                P.op("tensor", lambda e, ex=ex, sp=sp: e.matmul(
                    psum[6][:, 0:512], rowa[32:48, ex * 128:(ex + 1) * 128],
                    gfm_all[:, sp * 512:(sp + 1) * 512],
                    start=True, stop=True), inc="T")
                gb = P.cnt["T"]
            for et in range(4):
                ub = 4 + (et % 2)
                banks.acquire(ub)
                if sp == 0 and et == 0:
                    P.wait("tensor", "V", max(ln_rdy, up_conv_last[ex]))
                nmm = 0
                for (aa, bb) in ((upH, lnH), (upH, lnL), (upL, lnH)):
                    for kc in range(NCH):
                        nmm += 1
                        P.op("tensor", lambda e, aa=aa, bb=bb, kc=kc, et=et, sp=sp, ub=ub,
                             st=(nmm == 1), sp_=(nmm == 24): e.matmul(
                            psum[ub][:, 0:512],
                            aa[:, kc * 512 + et * 128: kc * 512 + et * 128 + 128],
                            bb[:, kc * 2048 + sp * 512: kc * 2048 + (sp + 1) * 512],
                            start=st, stop=sp_), inc="T" if nmm == 24 else None)
                if sp == 3 and et == 3:
                    up_T = P.cnt["T"]
                wr = P.cnt["T"]
                P.wait("scalar", "T", wr)
                P.wait("scalar", "V", act_vcnt[et])
                P.op("scalar", lambda e, et=et, ub=ub: e.activation(
                    out=act32[:, et * 512:(et + 1) * 512], in_=psum[ub][:, 0:512],
                    func=AF.Gelu), inc="Sc")
                banks.release(ub, "Sc")
                sc = P.cnt["Sc"]
                P.wait("vector", "Sc", sc)
                P.wait("vector", "T", act_guard[et])
                a32 = act32[:, et * 512:(et + 1) * 512]
                ahi = a16[:, et * 512:(et + 1) * 512]
                alo = a16[:, 2048 + et * 512: 2048 + (et + 1) * 512]
                if routed:
                    P.wait("vector", "T", gb)
                    P.op("vector", lambda e, a32=a32: e.tensor_tensor(
                        a32, a32, psum[6][:, 0:512], OP.mult))
                P.op("vector", lambda e, a32=a32, ahi=ahi: e.tensor_copy(ahi, a32))
                P.op("vector", lambda e, a32=a32, ahi=ahi, alo=alo: e.tensor_tensor(
                    alo, a32, ahi, OP.subtract), inc="V")
                act_vcnt[et] = P.cnt["V"]
            if routed:
                banks.release(6, "V")
            if ex == 0 and sp == 0:
                dbg_tap("act0", a16[:, 0:2048])
            if sp == 3 and ex + 1 < NEX:
                emit_stage(ex + 1, range(8), up_T)
                up_conv_last[ex + 1] = convV[(ex + 1) * 16 + 7]
            P.wait("tensor", "V", act_vcnt[3])
            if sp == 0:
                P.wait("tensor", "V", dn_conv_last[ex])
            for mt in range(4):
                tile = sp * 4 + mt
                for osp in range(2):
                    db = (mt * 2 + osp) % 4
                    banks.acquire(db)
                    nmm = 0
                    for (aa, bb) in (("h", "h"), ("h", "l"), ("l", "h")):
                        for et in range(4):
                            nmm += 1
                            ao = (0 if aa == "h" else 2048) + et * 512 + mt * 128
                            asrc = a16[:, ao:ao + 128]
                            bsrc = (dnH if bb == "h" else dnL)[
                                :, et * 1024 + osp * 512: et * 1024 + osp * 512 + 512]
                            P.op("tensor", lambda e, asrc=asrc, bsrc=bsrc, db=db,
                                 st=(nmm == 1), sp_=(nmm == 12): e.matmul(
                                psum[db][:, :], asrc, bsrc, start=st, stop=sp_),
                                inc="T" if nmm == 12 else None)
                    wr = P.cnt["T"]
                    P.wait("vector", "T", wr)
                    P.op("vector", lambda e, tile=tile, osp=osp, db=db: e.tensor_tensor(
                        xres_t(tile)[:, osp * 512:(osp + 1) * 512],
                        xres_t(tile)[:, osp * 512:(osp + 1) * 512],
                        psum[db][:, :], OP.add), inc="V")
                    banks.release(db, "V")
            for et in range(4):
                act_guard[et] = P.cnt["T"]
        if ex + 1 < NEX:
            emit_stage(ex + 1, range(8, 16), P.cnt["T"])
            dn_conv_last[ex + 1] = convV[(ex + 1) * 16 + 15]

    store_y(P, dram, x_res)


def store_y(P, dram, x_res):
    for e_ in ("V", "T", "Sc", "WD"):
        P.wait("gpsimd", e_)
    yo = dram["y"].rearrange("(t p) d -> t p d", p=128)
    for g in range(4):
        P.dma("gpsimd", yo[g * 4:(g + 1) * 4, :, :].rearrange("t p d -> p t d"),
              x_res[:, g * 4096:(g + 1) * 4096].rearrange("p (t d) -> p t d", t=4), inc="St")
    P.wait("gpsimd", "St")



def _consts_array(ins):
    c = np.zeros((128, 1024), np.float32)
    c[:, 0:128] = np.eye(128, dtype=np.float32)
    tri = (np.arange(128)[None, :] >= np.arange(128)[:, None]).astype(np.float32)
    c[:, 128:640] = np.tile(tri, (1, 4))
    return c


def _rows_array(ins):
    r = np.zeros((48, 2048), np.float32)
    for k in range(16):
        r[32 + k, k * 128:(k + 1) * 128] = 1.0
    r[0, 0:128] = 1.0
    r[0, 128:1152] = np.asarray(ins["L0_attn_bo"], np.float32)
    r[1, 0:1024] = np.asarray(ins["L0_attn_bq"], np.float32)
    r[2, 0:1024] = np.asarray(ins["L0_attn_bk"], np.float32)
    r[3, 0:1024] = np.asarray(ins["L0_attn_bv"], np.float32)
    r[4, 0:1024] = np.asarray(ins["L0_ffn_bias"], np.float32)
    return r


def _make_in_maps(ins):
    x = np.ascontiguousarray(np.asarray(ins["L0_attn_x"], np.float32))

    def T(a):
        return np.ascontiguousarray(np.asarray(a, np.float32).T)

    def colt(v, nch):
        return np.ascontiguousarray(np.asarray(v, np.float32).reshape(nch, 128).T)

    shup = np.asarray(ins["L1_ffn_shared_up"], np.float32)
    shdn = np.asarray(ins["L1_ffn_shared_down"], np.float32)
    roup = np.asarray(ins["L1_ffn_routed_up"], np.float32)
    rodn = np.asarray(ins["L1_ffn_routed_down"], np.float32)

    common = {
        "wq": np.ascontiguousarray(np.asarray(ins["L0_attn_wq"], np.float32)),
        "wk": np.ascontiguousarray(np.asarray(ins["L0_attn_wk"], np.float32)),
        "wv": np.ascontiguousarray(np.asarray(ins["L0_attn_wv"], np.float32)),
        "wo": np.ascontiguousarray(np.asarray(ins["L0_attn_wo"], np.float32)),
        "bq_c": colt(ins["L0_attn_bq"], NCH),
        "bk_c": colt(ins["L0_attn_bk"], NCH),
        "rows_c": _rows_array(ins),
        "w_ffn0_t": T(ins["L0_ffn_weight"]),
        "w_in_t": T(ins["L1_attn_W_in"]),
        "conv_w_c": np.ascontiguousarray(
            np.asarray(ins["L1_attn_conv_w"], np.float32)[:, 0, :]
            .reshape(DI // 128, 128, 4).transpose(1, 0, 2).reshape(128, -1)),
        "conv_b_c": colt(ins["L1_attn_conv_b"], DI // 128),
        "w_x_t": T(ins["L1_attn_W_x"]),
        "w_dt_t": T(ins["L1_attn_W_dt"]),
        "b_dt_c": colt(ins["L1_attn_b_dt"], DI // 128),
        "a_log_c": np.ascontiguousarray(
            np.asarray(ins["L1_attn_A_log"], np.float32)
            .reshape(DI // 128, 128, DSTATE).transpose(1, 0, 2).reshape(128, -1)),
        "d_par_c": colt(ins["L1_attn_D_param"], DI // 128),
        "w_out_t": T(ins["L1_attn_W_out"]),
        "ln_wb": np.ascontiguousarray(np.concatenate([
            np.tile(np.asarray(ins["L1_attn_ln_w"], np.float32)[None, :], (128, 1)),
            np.tile(np.asarray(ins["L1_attn_ln_b"], np.float32)[None, :], (128, 1))], axis=1)),
        "shup_t": np.ascontiguousarray(shup.reshape(NSH * EDIM, D).T),
        "shdn_t": np.ascontiguousarray(shdn.transpose(0, 2, 1).reshape(NSH * EDIM, D)),
        "roup_t": np.ascontiguousarray(roup.reshape(NEXP * EDIM, D).T),
        "rodn_t": np.ascontiguousarray(rodn.transpose(0, 2, 1).reshape(NEXP * EDIM, D)),
        "wrt_t": T(ins["L1_ffn_router_w"]),
        "consts": _consts_array(ins),
    }
    maps = []
    for c in range(8):
        m = dict(common)
        m["x_own"] = np.ascontiguousarray(x[c])
        m["x_all"] = np.ascontiguousarray(x[[j for j in range(8) if j != c]])
        maps.append(m)
    return maps


def kernel(**inputs):
    ins = {k: np.asarray(v) for k, v in inputs.items()}
    nc, _ = build_program()
    res = run_bass_kernel_spmd(nc, _make_in_maps(ins), core_ids=list(range(8)))
    return np.stack([np.asarray(res.results[i]["y"]) for i in range(8)], axis=0)
